# version 13
# baseline (speedup 1.0000x reference)
import numpy as np
import ml_dtypes
from contextlib import ExitStack

import concourse.bass as bass
import concourse.mybir as mybir
from concourse.bass_utils import run_bass_kernel_spmd
from concourse.aot_env import allow_ici_all_to_all

BFNP = ml_dtypes.bfloat16
F32 = mybir.dt.float32
BF16 = mybir.dt.bfloat16
I32 = mybir.dt.int32
AF = mybir.ActivationFunctionType
ALU = mybir.AluOpType

B, S, D = 2, 8192, 4096
NH, NKV, HD, LAT, KVD = 16, 4, 64, 1024, 256
TOK = 2048
HALO = 8
CL = TOK + HALO
C1L = TOK + 2

OFF_Q = 0
OFF_K = 256 * TOK
OFF_V = OFF_K + 64 * TOK
SH_W = OFF_V + TOK * 65
SH2_W = 256 * TOK

DEBUG_TAPS = ()
PHASES = 4

TC5 = [(0, 512), (512, 1024), (1024, 1536), (1536, 2048), (2048, CL)]
TC5C = [(0, 512), (512, 1024), (1024, 1536), (1536, 2048), (2048, C1L)]
TC4 = [(0, 512), (512, 1024), (1024, 1536), (1536, 2048)]


class Plan:
    ENG = ("sync", "gpsimd", "tensor", "vector", "scalar")

    def __init__(self):
        self.ops = {e: [] for e in self.ENG}
        self.cnt = {}

    def n(self, sem):
        return self.cnt.get(sem, 0)

    def emit(self, eng, fn, waits=(), inc=None, by=1):
        if inc is not None:
            self.cnt[inc] = self.cnt.get(inc, 0) + by
        self.ops[eng].append((fn, tuple(waits), inc, by))
        return self.cnt.get(inc) if inc is not None else None

    def dma(self, queue, out, in_, waits=(), sem=None, **kw):
        if sem is None:
            sem = {"sync": "LD", "gpsimd": "GD", "scalar": "XD"}[queue]
        return self.emit(
            queue,
            (lambda e, out=out, in_=in_, kw=kw: e.dma_start(out=out, in_=in_, **kw)),
            waits, inc=sem, by=16)

    def barrier(self, engines=None):
        engines = engines or self.ENG
        vals = {}
        for e in engines:
            vals[e] = self.emit(e, (lambda eng: eng.nop()), (), inc="B" + e, by=1)
        for e in engines:
            waits = tuple(("B" + o, vals[o]) for o in engines if o != e)
            self.emit(e, None, waits)


def _replay(plan, block, sems):
    def mk(ename):
        def body(eng):
            hw = {}
            for fn, waits, inc, by in plan.ops[ename]:
                for s, v in waits:
                    if v > hw.get(s, 0):
                        eng.wait_ge(sems[s], v)
                        hw[s] = v
                if fn is not None:
                    ins = fn(eng)
                    if inc is not None:
                        ins.then_inc(sems[inc], by)
        return body
    block.sync(mk("sync"))
    block.gpsimd(mk("gpsimd"))
    block.tensor(mk("tensor"))
    block.vector(mk("vector"))
    block.scalar(mk("scalar"))


class _StopBuild(Exception):
    pass


_CTX = {}


def build_program():
    try:
        return _build_inner()
    except _StopBuild:
        nc, P, sems, es, taps = (_CTX["nc"], _CTX["P"], _CTX["sems"],
                                 _CTX["es"], _CTX["taps"])
        P.emit("gpsimd", None, waits=[("ST", P.n("ST")), ("SA", P.n("SA")),
                                      ("SB", P.n("SB"))])
        P.barrier()
        with nc.Block() as block:
            with allow_ici_all_to_all():
                _replay(P, block, sems)
        es.close()
        return nc, list(taps.keys())


def _build_inner():
    nc = bass.Bass("TRN2", target_bir_lowering=False, debug=False)
    P = Plan()

    def I(name, shape, dt):
        return nc.dram_tensor(name, shape, dt, kind="ExternalInput")

    x_loc = I("x_loc", [CL, D], F32)
    w_qkv = I("w_qkv", [1536, 512], F32)
    w_o = I("w_o", [D, 128], F32)
    qc1_w = I("qc1_w", [LAT, HD, 3], F32)
    qc2_w = I("qc2_w", [LAT, 128, 3], F32)
    kc1_w = I("kc1_w", [KVD, HD, 3], F32)
    kc2_w = I("kc2_w", [KVD, KVD, 3], F32)
    qc1_bT = I("qc1_bT", [64, 16], F32)
    qc2_bT = I("qc2_bT", [128, 8], F32)
    kc1_bT = I("kc1_bT", [64, 4], F32)
    kc2_bT = I("kc2_bT", [128, 2], F32)
    qweight = I("qweight", [D, 4, 128], I32)
    scalesT = I("scalesT", [128, 32, 4], F32)
    zerosT = I("zerosT", [128, 32, 4], F32)
    ffn_biasT = I("ffn_biasT", [128, 32], F32)
    key_temp_b = I("key_temp_b", [128, 1], F32)
    halo_mask = I("halo_mask", [128, 1], F32)
    ident_bf = I("ident_bf", [128, 128], BF16)
    ident_f32 = I("ident_f32", [128, 128], F32)
    rot_bf = I("rot_bf", [128, 128], BF16)
    meank_l = I("meank_l", [128, 128], BF16)
    meanq_l = I("meanq_l", [128, 128], BF16)
    bcast2 = I("bcast2", [2, 128], F32)
    cosT = I("cosT", [128, TOK], BF16)
    sinT = I("sinT", [128, TOK], BF16)
    mask01 = I("mask01", [128, 2048], BF16)
    bsel = I("bsel", [128, 2], F32)

    out_loc = nc.dram_tensor("out_loc", [TOK, D], F32, kind="ExternalOutput")
    taps = {}
    def mktap(name, shape, dt):
        if name in DEBUG_TAPS:
            taps[name] = nc.dram_tensor("tap_" + name, shape, dt, kind="ExternalOutput")

    kc2T_d = nc.dram_tensor("kc2T_d", [3, KVD, KVD], BF16)
    OQ, OW, OC2, OF = 0, 786432, 1310720, 1703936
    PSL = 3801088
    pp_in = nc.dram_tensor("pp_in", [1, PSL], BF16)
    pp_out = nc.dram_tensor("pp_out", [8, PSL], BF16)
    xnT_d = nc.dram_tensor("xnT_d", [D, CL], BF16)
    a1_in = nc.dram_tensor("a1_in", [8, SH_W], BF16)
    a1_out = nc.dram_tensor("a1_out", [8, SH_W], BF16)
    a2_in = nc.dram_tensor("a2_in", [8, SH2_W], BF16)
    a2_out = nc.dram_tensor("a2_out", [8, SH2_W], BF16)
    dn_d = nc.dram_tensor("dn_d", [64 * 512], BF16)
    groups = [[0, 1, 2, 3, 4, 5, 6, 7]]

    es = ExitStack()
    sems = {}
    for s in ["LD", "GD", "ST", "XD", "CC", "TE", "VE", "SE",
              "D0", "D1", "D2", "D3", "SA", "SB", "W0", "W1",
              "Bsync", "Bgpsimd", "Btensor", "Bvector", "Bscalar"]:
        sems[s] = es.enter_context(nc.semaphore(s))

    sb = lambda *a: es.enter_context(nc.sbuf_tensor(*a))
    cid = sb("c_id", [128, 128], BF16)
    cidf = sb("c_idf", [128, 128], F32)
    crot = sb("c_rot", [128, 128], BF16)
    cmk = sb("c_mk", [128, 128], BF16)
    cmq = sb("c_mq", [128, 128], BF16)
    cb2 = sb("c_b2", [2, 128], F32)
    ccos = sb("c_cos", [128, TOK], BF16)
    csin = sb("c_sin", [128, TOK], BF16)
    cq1b = sb("c_q1b", [64, 16], F32)
    cq2b = sb("c_q2b", [128, 8], F32)
    ck1b = sb("c_k1b", [64, 4], F32)
    ck2b = sb("c_k2b", [128, 2], F32)
    cfb = sb("c_fb", [128, 32], F32)
    ctmp = sb("c_tmp", [128, 1], F32)
    ctmp8 = sb("c_tmp8", [128, 1], F32)
    cnbias = sb("c_nbias", [128, 1], F32)
    chmask = sb("c_hmask", [128, 1], F32)
    cbl1 = sb("c_bl1", [128, 2], F32)
    cones1 = sb("c_ones1", [1, 64], F32)
    cmask01 = sb("c_mask01", [128, 2048], BF16)
    cbsel = sb("c_bsel", [128, 2], F32)
    ceps = sb("c_eps", [128, 1], F32)
    qc1T = sb("qc1T", [128, 8 * 3 * 64], BF16)
    kc1T = sb("kc1T", [128, 2 * 3 * 64], BF16)

    for t, src in [(cid, ident_bf), (cidf, ident_f32), (crot, rot_bf),
                   (cmk, meank_l), (cmq, meanq_l), (cb2, bcast2),
                   (ccos, cosT), (csin, sinT), (cq1b, qc1_bT), (cq2b, qc2_bT),
                   (ck1b, kc1_bT), (ck2b, kc2_bT), (cfb, ffn_biasT),
                   (chmask, halo_mask), (cmask01, mask01), (ctmp, key_temp_b),
                   (cbsel, bsel)]:
        P.dma("sync", t[:, :], src[:, :], sem="D0")
    LDC = P.n("D0")
    P.emit("scalar", lambda e: e.activation(ctmp8[:, :], ctmp[:, :], AF.Copy, bias=0.0, scale=0.125),
           waits=[("D0", LDC)], inc="SE")
    sA = P.emit("scalar", lambda e: e.activation(cnbias[:, :], ctmp[:, :], AF.Abs, bias=0.0, scale=0.125),
                waits=[("D0", LDC)], inc="SE")
    P.emit("vector", lambda e: e.tensor_scalar_mul(cnbias[:, :], cnbias[:, :], -1.0),
           waits=[("SE", sA)], inc="VE")
    P.emit("vector", lambda e: e.memset(cbl1[:, :], 0.0), inc="VE")
    P.emit("vector", lambda e: e.memset(cbl1[0:64, 0:1], 1.0), inc="VE")
    P.emit("vector", lambda e: e.memset(cbl1[64:128, 1:2], 1.0), inc="VE")
    P.emit("vector", lambda e: e.memset(cones1[:, :], 1.0), inc="VE")
    VEPS = P.emit("vector", lambda e: e.memset(ceps[:, :], 1e-5), inc="VE")

    with ExitStack() as p0:
        wbf = p0.enter_context(nc.sbuf_tensor("p0_wbf", [128, 17408], BF16))
        big = p0.enter_context(nc.sbuf_tensor("p0_big", [128, 16384], BF16))
        stg = [p0.enter_context(nc.sbuf_tensor(f"p0_stg{i}", [128, 4096], BF16))
               for i in range(2)]
        stg_ld = [0, 0]
        qwi = p0.enter_context(nc.sbuf_tensor("p0_qwi", [128, 2 * 512], I32))
        scz = p0.enter_context(nc.sbuf_tensor("p0_scz", [128, 256], F32))
        wde = p0.enter_context(nc.sbuf_tensor("p0_wde", [128, 2 * 512], BF16))
        pst = [p0.enter_context(nc.psum_tensor(f"p0_ps{i}", [128, 512], BF16))
               for i in range(2)]
        pst_ve = [0, 0]
        psn = [0]

        def tgroup(src_aps, ready_waits):
            si = psn[0] % 2
            psn[0] += 1
            ps = pst[si]
            tv = None
            for i, ap in enumerate(src_aps):
                w = list(ready_waits) + [("VE", pst_ve[si]), ("D0", LDC)]
                tv = P.emit("tensor",
                            (lambda e, ap=ap, ps=ps, i=i:
                             e.transpose(ps[:, i * 128:(i + 1) * 128], ap, cid[:, :])),
                            waits=w, inc="TE")
            return ps, si, tv

        def evac(dst_ap, width, ps, si, tv, extra=()):
            sv = P.emit("vector",
                        (lambda e, ps=ps, dst_ap=dst_ap, width=width:
                         e.tensor_copy(dst_ap, ps[:, 0:width])),
                        waits=[("TE", tv)] + list(extra), inc="VE")
            pst_ve[si] = sv
            return sv

        def ppdst(off, n):
            return pp_in.ap()[0, off:off + n]

        W_QKV, W_O, W_Q2, W_K2, W_Q1, W_K1 = 0, 6144, 10240, 13312, 14848, 16384
        P.dma("gpsimd", wbf[:, W_QKV:W_QKV + 6144].rearrange("p (t c) -> p t c", c=512),
              w_qkv.ap().rearrange("(t p) c -> p t c", p=128), sem="XD")
        P.dma("gpsimd", wbf[:, W_O:W_O + 4096].rearrange("p (t c) -> p t c", c=128),
              w_o.ap().rearrange("(t p) c -> p t c", p=128), sem="XD")
        P.dma("gpsimd", wbf[:, W_Q2:W_Q2 + 3072].rearrange("p (t c) -> p t c", c=384),
              qc2_w.ap().rearrange("(t p) i k -> p t (i k)", p=128), sem="XD")
        P.dma("gpsimd", wbf[:, W_K2:W_K2 + 1536].rearrange("p (t c) -> p t c", c=768),
              kc2_w.ap().rearrange("(t p) i k -> p t (i k)", p=128), sem="XD")
        P.dma("gpsimd", wbf[:, W_Q1:W_Q1 + 1536].rearrange("p (t c) -> p t c", c=192),
              qc1_w.ap().rearrange("(t p) i k -> p t (i k)", p=128), sem="XD")
        P.dma("gpsimd", wbf[:, W_K1:W_K1 + 384].rearrange("p (t c) -> p t c", c=192),
              kc1_w.ap().rearrange("(t p) i k -> p t (i k)", p=128), sem="XD")
        GD_L = P.n("XD")
        P.dma("sync", scz[:, 0:128], scalesT[:, :, :])
        l2 = P.dma("sync", scz[:, 128:256], zerosT[:, :, :])

        for j in range(4):
            sgi = j % 2
            for g0 in range(0, 12, 4):
                aps = [wbf[:, W_QKV + (g0 + t) * 512 + j * 128:
                           W_QKV + (g0 + t) * 512 + (j + 1) * 128] for t in range(4)]
                ps, si, tv = tgroup(aps, [("XD", GD_L)])
                evac(stg[sgi][:, g0 * 128:(g0 + 4) * 128], 512, ps, si, tv,
                     extra=[("SA" if sgi == 0 else "SB", stg_ld[sgi])])
            sv = P.n("VE")
            ld = P.dma("sync",
                       ppdst(OQ + j * 128 * 1536, 128 * 1536)
                       .rearrange("(p c) -> p c", c=1536),
                       stg[sgi][:, 0:1536],
                       waits=[("VE", sv)], sem=("SA" if sgi == 0 else "SB"))
            stg_ld[sgi] = ld

        for g0 in range(0, 32, 4):
            sgi = 0
            aps = [wbf[:, W_O + (g0 + t) * 128: W_O + (g0 + t + 1) * 128]
                   for t in range(4)]
            ps, si, tv = tgroup(aps, [("XD", GD_L)])
            evac(stg[0][:, g0 * 128:(g0 + 4) * 128], 512, ps, si, tv,
                 extra=[("SA", stg_ld[0])])
        sv = P.n("VE")
        ld = P.dma("sync", ppdst(OW, 524288).rearrange("(p c) -> p c", c=4096),
                   stg[0][:, 0:4096], waits=[("VE", sv)], sem="SA")
        stg_ld[0] = ld

        for tau in range(3):
            rps = None
            for ot in range(8):
                rps = P.emit(
                    "vector",
                    (lambda e, tau=tau, ot=ot:
                     e.tensor_copy(big[:, ot * 128:(ot + 1) * 128],
                                   wbf[:, W_Q2 + ot * 384 + tau:
                                       W_Q2 + (ot + 1) * 384:3])),
                    waits=[("XD", GD_L), ("TE", P.n("TE"))], inc="VE")
            sgi = 1
            for g0 in range(0, 8, 4):
                aps = [big[:, (g0 + t) * 128:(g0 + t + 1) * 128] for t in range(4)]
                ps, si, tv = tgroup(aps, [("VE", rps)])
                evac(stg[1][:, g0 * 128:(g0 + 4) * 128], 512, ps, si, tv,
                     extra=[("SB", stg_ld[1])])
            sv = P.n("VE")
            ld = P.dma("sync",
                       ppdst(OC2 + tau * 131072, 131072)
                       .rearrange("(p c) -> p c", c=1024),
                       stg[1][:, 0:1024], waits=[("VE", sv)], sem="SB")
            stg_ld[1] = ld

        for r in range(2):
            rpv = {}
            for tau in range(3):
                rpv[tau] = P.emit(
                    "vector",
                    (lambda e, r=r, tau=tau:
                     e.tensor_copy(big[:, 1024 + tau * 256: 1024 + (tau + 1) * 256],
                                   wbf[:, W_K2 + r * 768 + tau:
                                       W_K2 + (r + 1) * 768:3])),
                    waits=[("XD", GD_L), ("TE", P.n("TE"))], inc="VE")
            for tau in range(3):
                base = 1024 + tau * 256
                aps = [big[:, base + i * 128: base + (i + 1) * 128] for i in range(2)]
                ps, si, tv = tgroup(aps, [("VE", rpv[tau])])
                sgi = tau % 2
                sv = evac(stg[sgi][:, 0:256], 256, ps, si, tv,
                          extra=[("SA" if sgi == 0 else "SB", stg_ld[sgi])])
                dst = kc2T_d.ap()[tau, :, r * 128:(r + 1) * 128]
                dst = dst.rearrange("(b p) c -> p b c", p=128)
                ld = P.dma("sync", dst,
                           stg[sgi][:, 0:256].rearrange("p (b c) -> p b c", c=128),
                           waits=[("VE", sv)], sem=("SA" if sgi == 0 else "SB"))
                stg_ld[sgi] = ld

        def conv1_prep(nr, srcbase, dstT):
            for r in range(nr):
                vv = None
                for tau in range(3):
                    vv = P.emit(
                        "vector",
                        (lambda e, r=r, tau=tau, srcbase=srcbase:
                         e.tensor_copy(big[:, 2048 + tau * 64: 2048 + (tau + 1) * 64],
                                       wbf[:, srcbase + r * 192 + tau:
                                           srcbase + (r + 1) * 192:3])),
                        waits=[("XD", GD_L), ("TE", P.n("TE")), ("ST", P.n("ST"))],
                        inc="VE")
                for tau in range(3):
                    si = psn[0] % 2
                    psn[0] += 1
                    ps = pst[si]
                    tv = P.emit("tensor",
                                (lambda e, tau=tau, ps=ps:
                                 e.transpose(ps[0:64, 0:128],
                                             big[:, 2048 + tau * 64: 2048 + (tau + 1) * 64],
                                             cid[:, :])),
                                waits=[("VE", vv), ("VE", pst_ve[si]), ("D0", LDC)],
                                inc="TE")
                    sv = P.emit("vector",
                                (lambda e, ps=ps, r=r, tau=tau, dstT=dstT:
                                 e.tensor_copy(dstT[0:64, (r * 3 + tau) * 64:(r * 3 + tau + 1) * 64],
                                               ps[0:64, 0:64])),
                                waits=[("TE", tv)], inc="VE")
                    sv2 = P.emit("vector",
                                 (lambda e, ps=ps, tau=tau:
                                  e.tensor_copy(big[0:64, 2240 + tau * 64: 2240 + (tau + 1) * 64],
                                                ps[0:64, 64:128])),
                                 waits=[("TE", tv)], inc="VE")
                    pst_ve[si] = sv2
                    P.dma("gpsimd",
                          dstT[64:128, (r * 3 + tau) * 64:(r * 3 + tau + 1) * 64],
                          big[0:64, 2240 + tau * 64: 2240 + (tau + 1) * 64],
                          waits=[("VE", sv2)], sem="ST")
        conv1_prep(8, W_Q1, qc1T)
        conv1_prep(2, W_K1, kc1T)

        qw_ld = [0, 0]
        dq_rd = [0, 0]
        for ot in range(32):
            qsl = ot % 2
            gq = P.dma("gpsimd", qwi[:, qsl * 512:(qsl + 1) * 512],
                       qweight[ot * 128:(ot + 1) * 128, :, :],
                       waits=[("VE", qw_ld[qsl])],
                       sem=("D2" if qsl == 0 else "D3"))
            te_wde = P.n("TE")
            dq = None
            for g in range(4):
                dq = P.emit("vector",
                            (lambda e, g=g, qsl=qsl, ot=ot:
                             e.tensor_scalar(wde[:, qsl * 512 + g * 128:
                                                 qsl * 512 + (g + 1) * 128],
                                             qwi[:, qsl * 512 + g * 128:
                                                 qsl * 512 + (g + 1) * 128],
                                             scz[:, ot * 4 + g: ot * 4 + g + 1],
                                             scz[:, 128 + ot * 4 + g: 129 + ot * 4 + g],
                                             ALU.mult, ALU.add)),
                            waits=[("D2" if qsl == 0 else "D3", gq), ("LD", l2),
                                   ("TE", dq_rd[qsl])],
                            inc="VE")
            qw_ld[qsl] = dq
            aps = [wde[:, qsl * 512 + i * 128: qsl * 512 + (i + 1) * 128]
                   for i in range(4)]
            ps, si, tv = tgroup(aps, [("VE", dq)])
            dq_rd[qsl] = P.n("TE")
            P.emit("vector",
                   (lambda e, ps=ps, ot=ot:
                    e.tensor_copy(big.ap().rearrange("p (b c) -> p b c", c=4096)
                                  [:, 0:4, ot * 128:(ot + 1) * 128],
                                  ps[:, 0:512].rearrange("p (b c) -> p b c", c=128))),
                   waits=[("TE", tv), ("SA", stg_ld[0]), ("SB", stg_ld[1])], inc="VE")
            pst_ve[si] = P.n("VE")
        WTF_VE = P.n("VE")
        if "bigw" in DEBUG_TAPS:
            mktap("bigw", [128, 16384], BF16)
            P.dma("gpsimd", taps["bigw"][:, :], big[:, :],
                  waits=[("VE", WTF_VE)], sem="ST")
        if "wdew" in DEBUG_TAPS:
            mktap("wdew", [128, 1024], BF16)
            P.dma("gpsimd", taps["wdew"][:, :], wde[:, :],
                  waits=[("VE", WTF_VE)], sem="ST")
        for j in range(4):
            P.dma("sync", ppdst(OF + j * 524288, 524288)
                  .rearrange("(p c) -> p c", c=4096),
                  big[:, j * 4096:(j + 1) * 4096],
                  waits=[("VE", WTF_VE)], sem=("SA" if j % 2 == 0 else "SB"))

    P.barrier()
    ccp = P.emit("gpsimd",
                 (lambda e: e.collective_compute(
                     "AllGather", ALU.bypass, replica_groups=groups,
                     ins=[pp_in.ap().opt()], outs=[pp_out.ap().opt()])),
                 waits=[("SA", P.n("SA")), ("SB", P.n("SB")), ("GD", P.n("GD"))],
                 inc="CC")
    CC_P0 = P.n("CC")
    if "ppw" in DEBUG_TAPS:
        mktap("ppw", [8, PSL], BF16)
        P.dma("gpsimd", taps["ppw"][:, :], pp_out[:, :],
              waits=[("CC", CC_P0)], sem="ST")
    if "ppi" in DEBUG_TAPS:
        mktap("ppi", [1, PSL], BF16)
        P.dma("gpsimd", taps["ppi"][:, :], pp_in[:, :],
              waits=[("CC", CC_P0)], sem="ST")
    LD_P0 = P.n("LD")
    GD_P0 = P.n("GD")


    _CTX.update(nc=nc, P=P, sems=sems, es=es, taps=taps)
    with ExitStack() as pa:
        qkvT = pa.enter_context(nc.sbuf_tensor("qkvT", [128, 12 * CL], BF16))

        HL = 1032
        TC3 = [(0, 512), (512, 1024), (1024, HL)]
        with ExitStack() as pa1:
            xnT = pa1.enter_context(nc.sbuf_tensor("xnT", [128, 32 * HL], BF16))
            xbuf = pa1.enter_context(nc.sbuf_tensor("a_x", [128, 2 * D], F32))
            xnt = pa1.enter_context(nc.sbuf_tensor("a_xn", [128, 2 * D], BF16))
            lnst = pa1.enter_context(nc.sbuf_tensor("a_ln", [128, 2 * 64], F32))
            wq = pa1.enter_context(nc.sbuf_tensor("a_wq", [128, 2 * 4096], BF16))
            pstr = [pa1.enter_context(nc.psum_tensor(f"a_tr{i}", [128, 512], BF16))
                    for i in range(2)]
            pstr_ve = [0, 0]
            psq = [pa1.enter_context(nc.psum_tensor(f"a_q{i}", [128, 512], F32))
                   for i in range(6)]

            evq = {}
            wqld = [0, 0]
            seapp = {}
            trdone = {}
            spill_ld = 0
            ve_tile = {}
            for h in range(2):
                for kk9 in range(9):
                    k = h * 9 + kk9
                    rows = 128 if kk9 < 8 else 8
                    r0 = h * 1024 + kk9 * 128
                    slot = k % 2
                    w = [("SE", seapp.get(k - 2, 0))]
                    l1 = P.dma("sync", xbuf[0:rows, slot * D:slot * D + D],
                               x_loc[r0:r0 + rows, :], waits=w,
                               sem=("D0" if slot == 0 else "D1"))
                    vs = None
                    for i in range(8):
                        vs = P.emit("vector",
                                    (lambda e, rows=rows, slot=slot, i=i:
                                     e.bn_stats(lnst[0:rows, slot * 64 + i * 6: slot * 64 + (i + 1) * 6],
                                                xbuf[0:rows, slot * D + i * 512: slot * D + (i + 1) * 512])),
                                    waits=[("D0" if slot == 0 else "D1", l1)], inc="VE")
                    va = P.emit("vector",
                                (lambda e, rows=rows, slot=slot:
                                 e.bn_aggr(lnst[0:rows, slot * 64 + 48: slot * 64 + 50],
                                           lnst[0:rows, slot * 64: slot * 64 + 48])),
                                waits=[("VE", vs)], inc="VE")
                    sq = P.emit("scalar",
                                (lambda e, rows=rows, slot=slot:
                                 e.activation(lnst[0:rows, slot * 64 + 50: slot * 64 + 51],
                                              lnst[0:rows, slot * 64 + 49: slot * 64 + 50],
                                              AF.Sqrt, bias=ceps[0:rows, 0:1], scale=1.0)),
                                waits=[("VE", va), ("VE", VEPS)], inc="SE")
                    vr = P.emit("vector",
                                (lambda e, rows=rows, slot=slot:
                                 e.reciprocal(lnst[0:rows, slot * 64 + 51: slot * 64 + 52],
                                              lnst[0:rows, slot * 64 + 50: slot * 64 + 51])),
                                waits=[("SE", sq)], inc="VE")
                    vn = P.emit("vector",
                                (lambda e, rows=rows, slot=slot:
                                 e.scalar_tensor_tensor(lnst[0:rows, slot * 64 + 52: slot * 64 + 53],
                                                        lnst[0:rows, slot * 64 + 48: slot * 64 + 49],
                                                        -1.0,
                                                        lnst[0:rows, slot * 64 + 51: slot * 64 + 52],
                                                        ALU.mult, ALU.mult)),
                                waits=[("VE", vr)], inc="VE")
                    sa = P.emit("scalar",
                                (lambda e, rows=rows, slot=slot:
                                 e.activation(xnt[0:rows, slot * D: slot * D + D],
                                              xbuf[0:rows, slot * D: slot * D + D],
                                              AF.Identity,
                                              bias=lnst[0:rows, slot * 64 + 52: slot * 64 + 53],
                                              scale=lnst[0:rows, slot * 64 + 51: slot * 64 + 52])),
                                waits=[("VE", vn), ("TE", trdone.get(k - 2, 0))], inc="SE")
                    seapp[k] = sa
                    for g0 in range(0, 32, 4):
                        si = g0 // 4 % 2
                        ps = pstr[si]
                        tv = None
                        for i in range(4):
                            j = g0 + i
                            if rows == 128:
                                tv = P.emit("tensor",
                                            (lambda e, j=j, i=i, ps=ps, slot=slot:
                                             e.transpose(ps[:, i * 128:(i + 1) * 128],
                                                         xnt[:, slot * D + j * 128: slot * D + (j + 1) * 128],
                                                         cid[:, :])),
                                            waits=[("SE", sa), ("VE", pstr_ve[si])], inc="TE")
                            else:
                                tv = P.emit("tensor",
                                            (lambda e, j=j, i=i, ps=ps, slot=slot:
                                             e.transpose(ps[:, i * 8:(i + 1) * 8],
                                                         xnt[0:8, slot * D + j * 128: slot * D + (j + 1) * 128],
                                                         cid[0:8, 0:8])),
                                            waits=[("SE", sa), ("VE", pstr_ve[si])], inc="TE")
                        wdt = 128 if rows == 128 else 8
                        sv = P.emit("vector",
                                    (lambda e, ps=ps, g0=g0, wdt=wdt, kk9=kk9:
                                     e.tensor_copy(
                                         xnT.ap().rearrange("p (j t) -> p j t", t=HL)
                                         [:, g0:g0 + 4, kk9 * 128: kk9 * 128 + wdt],
                                         ps[:, 0:4 * wdt].rearrange("p (b c) -> p b c", c=wdt))),
                                    waits=[("TE", tv), ("LD", spill_ld)], inc="VE")
                        pstr_ve[si] = sv
                    trdone[k] = P.n("TE")
                    ve_tile[kk9] = P.n("VE")
                VE_LN = P.n("VE")

                spill_ld = P.dma(
                    "sync",
                    xnT_d.ap().rearrange("(j p) t -> p j t", p=128)[:, :, h * 1024: h * 1024 + HL],
                    xnT[:, :].rearrange("p (j t) -> p j t", t=HL),
                    waits=[("VE", VE_LN)])

                for r in range(12):
                    slot = r % 2
                    lw = None
                    for rr8 in range(8):
                        lw = P.dma("sync",
                                   wq[:, slot * 4096 + rr8 * 512:
                                      slot * 4096 + (rr8 + 1) * 512]
                                   .rearrange("p (b c) -> p b c", c=128),
                                   pp_out.ap()[rr8, OQ:OQ + 786432]
                                   .rearrange("(b p c) -> p b c", p=128, c=1536)
                                   [:, :, r * 128:(r + 1) * 128],
                                   waits=([("CC", CC_P0), ("TE", wqld[slot])]
                                          if rr8 == 0 else []),
                                   sem=("D2" if slot == 0 else "D3"))
                    for i in range(32):
                        for tci, (c0, c1) in enumerate(TC3):
                            P.emit(
                                "tensor",
                                (lambda e, i=i, tci=tci, c0=c0, c1=c1, slot=slot:
                                 e.matmul(psq[tci][:, 0:c1 - c0],
                                          wq[:, slot * 4096 + i * 128: slot * 4096 + (i + 1) * 128],
                                          xnT[:, i * HL + c0: i * HL + c1],
                                          start=(i == 0), stop=(i == 31))),
                                waits=([("D2" if slot == 0 else "D3", lw),
                                        ("VE", ve_tile[(3, 7, 8)[tci]]),
                                        ("SE", evq.get(tci, 0))]
                                       if i == 0 else []),
                                inc="TE")
                    wqld[slot] = P.n("TE")
                    for tci, (c0, c1) in enumerate(TC3):
                        evq[tci] = P.emit(
                            "scalar",
                            (lambda e, r=r, h=h, tci=tci, c0=c0, c1=c1:
                             e.activation(qkvT[:, r * CL + h * 1024 + c0: r * CL + h * 1024 + c1],
                                          psq[tci][:, 0:c1 - c0], AF.Copy,
                                          bias=0.0, scale=1.0)),
                            waits=[("TE", P.n("TE"))], inc="SE")
        SE_QKV = P.n("SE")
        if "qkvT" in DEBUG_TAPS:
            mktap("qkvT", [1536, CL], BF16)
            P.dma("gpsimd", taps["qkvT"].ap().rearrange("(j p) t -> p j t", p=128),
                  qkvT[:, :].rearrange("p (j t) -> p j t", t=CL),
                  waits=[("SE", SE_QKV)], sem="ST")
        if "xnT" in DEBUG_TAPS:
            mktap("xnT", [D, CL], BF16)
            P.dma("gpsimd", taps["xnT"].ap().rearrange("(j p) t -> p j t", p=128),
                  xnT_d.ap().rearrange("(j p) t -> p j t", p=128),
                  waits=[("SE", SE_QKV)], sem="ST")

        if PHASES < 2:
            raise _StopBuild()
        P.barrier()
        with ExitStack() as pa2:
            qmix = pa2.enter_context(nc.sbuf_tensor("qmix", [128, 8 * TOK], BF16))
            kmix = pa2.enter_context(nc.sbuf_tensor("kmix", [128, 2 * TOK], BF16))
            w2s = pa2.enter_context(nc.sbuf_tensor("w2s", [128, 2 * 24 * 128], BF16))
            oddst = pa2.enter_context(nc.sbuf_tensor("oddst", [64, 512], BF16))
            pa2a = pa2.enter_context(ExitStack())
            c1q = pa2a.enter_context(nc.sbuf_tensor("c1q", [128, 8 * C1L], BF16))
            c1k = pa2a.enter_context(nc.sbuf_tensor("c1k", [128, 2 * C1L], BF16))
            ps1 = [pa2.enter_context(nc.psum_tensor(f"a2_p{i}", [128, 512], F32))
                   for i in range(6)]
            ps1_ev = [("SE", 0)] * 6

            def conv1(nh2, srcT, src_t0, dstT, wT, biasT):
                for r in range(nh2):
                    for tci, (c0, c1) in enumerate(TC5C):
                        wdt = c1 - c0
                        for hh in range(2):
                            pi = (r * 2 + hh) % 4
                            ps = ps1[pi]
                            base = 64 * hh
                            tv = None
                            for tau in range(3):
                                tv = P.emit(
                                    "tensor",
                                    (lambda e, r=r, hh=hh, tau=tau, c0=c0, wdt=wdt,
                                     ps=ps, base=base, srcT=srcT, src_t0=src_t0, wT=wT:
                                     e.matmul(ps[0:64, 0:wdt],
                                              wT[base:base + 64,
                                                 (r * 3 + tau) * 64:(r * 3 + tau + 1) * 64],
                                              srcT[base:base + 64,
                                                   src_t0 + r * CL + 4 + c0 + tau:
                                                   src_t0 + r * CL + 4 + c0 + tau + wdt],
                                              start=(tau == 0), stop=(tau == 2))),
                                    waits=([("SE", SE_QKV), ps1_ev[pi]]
                                           if tau == 0 else []),
                                    inc="TE")
                            h = 2 * r + hh
                            if hh == 0:
                                ev = P.emit(
                                    "scalar",
                                    (lambda e, r=r, c0=c0, wdt=wdt, ps=ps, h=h,
                                     dstT=dstT, biasT=biasT:
                                     e.activation(dstT[0:64, r * C1L + c0: r * C1L + c0 + wdt],
                                                  ps[0:64, 0:wdt], AF.Identity,
                                                  bias=biasT[0:64, h:h + 1], scale=1.0)),
                                    waits=[("TE", tv)], inc="SE")
                                ps1_ev[pi] = ("SE", ev)
                            else:
                                ev = P.emit(
                                    "scalar",
                                    (lambda e, c0=c0, wdt=wdt, ps=ps, h=h, biasT=biasT:
                                     e.activation(oddst[0:64, 0:wdt],
                                                  ps[0:64, 0:wdt], AF.Identity,
                                                  bias=biasT[0:64, h:h + 1], scale=1.0)),
                                    waits=[("TE", tv), ("GD", P.n("GD"))], inc="SE")
                                ps1_ev[pi] = ("SE", ev)
                                P.dma("gpsimd",
                                      dstT[64:128, r * C1L + c0: r * C1L + c0 + wdt],
                                      oddst[0:64, 0:wdt], waits=[("SE", ev)])
            conv1(8, qkvT, 0, c1q, qc1T, cq1b)
            conv1(2, qkvT, 8 * CL, c1k, kc1T, ck1b)
            GD_C1 = P.n("GD")
            hm = []
            for t, nt in [(c1q, 8), (c1k, 2)]:
                for r in range(nt):
                    hm.append(P.emit(
                        "vector",
                        (lambda e, t=t, r=r:
                         e.tensor_scalar_mul(t[:, r * C1L: r * C1L + 2],
                                             t[:, r * C1L: r * C1L + 2],
                                             chmask[:, 0:1])),
                        waits=[("SE", P.n("SE")), ("GD", GD_C1)], inc="VE"))
            VE_HM = P.n("VE")

            w2ld = [0, 0]

            def conv2(n_out, n_in, srcC, dstT, w2d, biasT, mean_l, mean_src_t0, mean_n):
                for o in range(n_out):
                    slot = o % 2
                    if w2d is None:
                        lw = None
                        for tau3 in range(3):
                            lw = P.dma(
                                "sync",
                                w2s[:, slot * 3072 + tau3 * n_in * 128:
                                    slot * 3072 + (tau3 + 1) * n_in * 128]
                                .rearrange("p (j c) -> p j c", c=128),
                                pp_out.ap()[:, OC2 + tau3 * 131072:
                                            OC2 + (tau3 + 1) * 131072]
                                .rearrange("r (p c) -> p r c", p=128, c=1024)
                                [:, :, o * 128:(o + 1) * 128],
                                waits=([("CC", CC_P0), ("TE", w2ld[slot])]
                                       if tau3 == 0 else []),
                                sem=("D0" if slot == 0 else "D1"))
                    else:
                        lw = P.dma("sync",
                                   w2s[:, slot * 3072: slot * 3072 + 3 * n_in * 128]
                                   .rearrange("p (b c) -> p b c", c=128),
                                   w2d.ap()[:, :, o * 128:(o + 1) * 128]
                                   .rearrange("t (j p) c -> p (t j) c", p=128),
                                   waits=[("CC", CC_P0), ("TE", w2ld[slot])],
                                   sem=("D0" if slot == 0 else "D1"))
                    for tci, (c0, c1) in enumerate(TC4):
                        wdt = c1 - c0
                        pi = 4 + (o * 4 + tci) % 2
                        ps = ps1[pi]
                        tv = None
                        first = True
                        for tau in range(3):
                            for j in range(n_in):
                                tv = P.emit(
                                    "tensor",
                                    (lambda e, tau=tau, j=j, o=o, c0=c0, wdt=wdt, ps=ps,
                                     slot=slot, srcC=srcC, n_in=n_in:
                                     e.matmul(ps[:, 0:wdt],
                                              w2s[:, slot * 3072 + (tau * n_in + j) * 128:
                                                  slot * 3072 + (tau * n_in + j + 1) * 128],
                                              srcC[:, j * C1L + c0 + tau: j * C1L + c0 + tau + wdt],
                                              start=(tau == 0 and j == 0), stop=False)),
                                    waits=([("D0" if slot == 0 else "D1", lw),
                                            ("VE", VE_HM), ps1_ev[pi]]
                                           if first else []),
                                    inc="TE")
                                first = False
                        for mi in range(mean_n):
                            tv = P.emit(
                                "tensor",
                                (lambda e, mi=mi, c0=c0, wdt=wdt, ps=ps,
                                 mean_l=mean_l, mean_src_t0=mean_src_t0:
                                 e.matmul(ps[:, 0:wdt],
                                          mean_l[:, :],
                                          qkvT[:, mean_src_t0 + mi * CL + 8 + c0:
                                               mean_src_t0 + mi * CL + 8 + c0 + wdt],
                                          start=False, stop=(mi == mean_n - 1))),
                                inc="TE")
                        ev = P.emit(
                            "scalar",
                            (lambda e, o=o, c0=c0, wdt=wdt, ps=ps, dstT=dstT, biasT=biasT:
                             e.activation(dstT[:, o * TOK + c0: o * TOK + c0 + wdt],
                                          ps[:, 0:wdt], AF.Identity,
                                          bias=biasT[:, o:o + 1], scale=1.0)),
                            waits=[("TE", tv)], inc="SE")
                        ps1_ev[pi] = ("SE", ev)
                    w2ld[slot] = P.n("TE")
            conv2(8, 8, c1q, qmix, None, cq2b, cmk, 8 * CL, 2)
            conv2(2, 2, c1k, kmix, kc2T_d, ck2b, cmq, 0, 8)
            SE_C2 = P.n("SE")
            pa2a.close()
            P.barrier()
            qhatT = pa2.enter_context(nc.sbuf_tensor("qhatT", [128, 8 * TOK], BF16))
            khatT = pa2.enter_context(nc.sbuf_tensor("khatT", [128, 2 * TOK], BF16))
            vstage = pa2.enter_context(nc.sbuf_tensor("vstage", [128, 4 * 16 * 65], BF16))
            sqs = pa2.enter_context(nc.sbuf_tensor("sqs", [128, 1024], F32))
            rsd = pa2.enter_context(nc.sbuf_tensor("rsd", [2, 2 * TOK], F32))
            rbb = pa2.enter_context(nc.sbuf_tensor("rbb", [128, 2 * 512], BF16))
            rtb = pa2.enter_context(nc.sbuf_tensor("rtb", [128, 2 * 512], BF16))
            tm1 = pa2.enter_context(nc.sbuf_tensor("tm1", [128, 2 * 512], BF16))
            if "qmix" in DEBUG_TAPS:
                mktap("qmix", [LAT, TOK], BF16)
                P.dma("gpsimd", taps["qmix"].ap().rearrange("(j p) t -> p j t", p=128),
                      qmix[:, :].rearrange("p (j t) -> p j t", t=TOK),
                      waits=[("SE", SE_C2)], sem="ST")

            def heads(n_t, srcM, dstT, is_k):
                for r in range(n_t):
                    rr = r % 2
                    for tci, (c0, c1) in enumerate(TC4):
                        sl2 = tci % 2
                        sqv = P.emit("scalar",
                                     (lambda e, r=r, c0=c0, c1=c1, sl2=sl2, srcM=srcM:
                                      e.activation(sqs[:, sl2 * 512:(sl2 + 1) * 512],
                                                   srcM[:, r * TOK + c0: r * TOK + c1],
                                                   AF.Square, bias=0.0, scale=1.0)),
                                     waits=[("SE", SE_C2), ("TE", P.n("TE"))], inc="SE")
                        pi = tci % 2
                        ps = ps1[pi]
                        tv = P.emit("tensor",
                                    (lambda e, c0=c0, c1=c1, ps=ps, sl2=sl2:
                                     e.matmul(ps[0:2, 0:c1 - c0], cbl1[:, :],
                                              sqs[:, sl2 * 512:(sl2 + 1) * 512],
                                              start=True, stop=True)),
                                    waits=[("SE", sqv), ps1_ev[pi]], inc="TE")
                        sn = P.emit("scalar",
                                    (lambda e, c0=c0, c1=c1, ps=ps, rr=rr:
                                     e.activation(rsd[0:2, rr * TOK + c0: rr * TOK + c1],
                                                  ps[0:2, 0:c1 - c0],
                                                  AF.Sqrt, bias=0.0, scale=1.0)),
                                    waits=[("TE", tv)], inc="SE")
                        ps1_ev[pi] = ("SE", sn)
                    snv = P.n("SE")
                    vm = P.emit("vector",
                                (lambda e, rr=rr:
                                 e.tensor_scalar_max(rsd[0:2, rr * TOK:(rr + 1) * TOK],
                                                     rsd[0:2, rr * TOK:(rr + 1) * TOK], 1e-12)),
                                waits=[("SE", snv)], inc="VE")
                    vrc = P.emit("vector",
                                 (lambda e, rr=rr:
                                  e.reciprocal(rsd[0:2, rr * TOK:(rr + 1) * TOK],
                                               rsd[0:2, rr * TOK:(rr + 1) * TOK])),
                                 waits=[("VE", vm)], inc="VE")
                    if is_k:
                        vrc = P.emit("vector",
                                     (lambda e, rr=rr:
                                      e.tensor_scalar_mul(rsd[0:2, rr * TOK:(rr + 1) * TOK],
                                                          rsd[0:2, rr * TOK:(rr + 1) * TOK],
                                                          ctmp8[0:2, 0:1])),
                                     waits=[("VE", vrc)], inc="VE")
                    for tci, (c0, c1) in enumerate(TC4):
                        wdt = c1 - c0
                        pib = 2 + tci % 2
                        psb = ps1[pib]
                        tb = P.emit("tensor",
                                    (lambda e, rr=rr, c0=c0, c1=c1, psb=psb:
                                     e.matmul(psb[:, 0:c1 - c0], cb2[:, :],
                                              rsd[0:2, rr * TOK + c0: rr * TOK + c1],
                                              start=True, stop=True)),
                                    waits=[("VE", vrc), ps1_ev[pib]], inc="TE")
                        sl = tci % 2
                        vb = P.emit("vector",
                                    (lambda e, psb=psb, sl=sl, wdt=wdt:
                                     e.tensor_copy(rbb[:, sl * 512: sl * 512 + wdt],
                                                   psb[:, 0:wdt])),
                                    waits=[("TE", tb)], inc="VE")
                        ps1_ev[pib] = ("VE", vb)
                        vq = P.emit("vector",
                                    (lambda e, r=r, c0=c0, wdt=wdt, sl=sl, srcM=srcM:
                                     e.tensor_mul(tm1[:, sl * 512: sl * 512 + wdt],
                                                  srcM[:, r * TOK + c0: r * TOK + c0 + wdt],
                                                  rbb[:, sl * 512: sl * 512 + wdt])),
                                    waits=[("VE", vb)], inc="VE")
                        pir = 4 + tci % 2
                        psr = ps1[pir]
                        tr = P.emit("tensor",
                                    (lambda e, sl=sl, wdt=wdt, psr=psr:
                                     e.matmul(psr[:, 0:wdt], crot[:, :],
                                              tm1[:, sl * 512: sl * 512 + wdt],
                                              start=True, stop=True)),
                                    waits=[("VE", vq), ps1_ev[pir]], inc="TE")
                        vrt = P.emit("vector",
                                     (lambda e, sl=sl, wdt=wdt, psr=psr:
                                      e.tensor_copy(rtb[:, sl * 512: sl * 512 + wdt],
                                                    psr[:, 0:wdt])),
                                     waits=[("TE", tr)], inc="VE")
                        ps1_ev[pir] = ("VE", vrt)
                        v1 = P.emit("vector",
                                    (lambda e, sl=sl, wdt=wdt, c0=c0:
                                     e.tensor_mul(tm1[:, sl * 512: sl * 512 + wdt],
                                                  tm1[:, sl * 512: sl * 512 + wdt],
                                                  ccos[:, c0:c0 + wdt])),
                                    waits=[("VE", vrt)], inc="VE")
                        v2 = P.emit("vector",
                                    (lambda e, sl=sl, wdt=wdt, c0=c0:
                                     e.tensor_mul(rtb[:, sl * 512: sl * 512 + wdt],
                                                  rtb[:, sl * 512: sl * 512 + wdt],
                                                  csin[:, c0:c0 + wdt])),
                                    inc="VE")
                        v3 = P.emit("vector",
                                    (lambda e, r=r, sl=sl, wdt=wdt, c0=c0, dstT=dstT:
                                     e.tensor_add(dstT[:, r * TOK + c0: r * TOK + c0 + wdt],
                                                  tm1[:, sl * 512: sl * 512 + wdt],
                                                  rtb[:, sl * 512: sl * 512 + wdt])),
                                    inc="VE")
            heads(8, qmix, qhatT, False)
            heads(2, kmix, khatT, True)
            VE_HEAD = P.n("VE")

            psv = [pa2.enter_context(nc.psum_tensor(f"a2_pv{i}", [128, 128], BF16))
                   for i in range(2)]
            psv_ve = [0, 0]
            for tt in range(16):
                for vt in range(2):
                    si = (tt * 2 + vt) % 2
                    ps = psv[si]
                    shift = 0 if vt == 0 else -1
                    tv = P.emit("tensor",
                                (lambda e, tt=tt, vt=vt, ps=ps, shift=shift:
                                 e.transpose(ps[:, :],
                                             qkvT[:, (10 + vt) * CL + 8 + tt * 128 + shift:
                                                  (10 + vt) * CL + 8 + (tt + 1) * 128 + shift],
                                             cid[:, :])),
                                waits=[("SE", SE_QKV), ("VE", psv_ve[si])], inc="TE")
                    for kvh in range(2):
                        k4 = vt * 2 + kvh
                        vv = P.emit("vector",
                                    (lambda e, tt=tt, ps=ps, kvh=kvh, k4=k4:
                                     e.tensor_copy(
                                         vstage[:, k4 * 1040 + tt * 65: k4 * 1040 + tt * 65 + 64],
                                         ps[:, kvh * 64:(kvh + 1) * 64])),
                                    waits=[("TE", tv)], inc="VE")
                    psv_ve[si] = P.n("VE")
            vo = None
            for k4 in range(4):
                for tt in range(16):
                    vo = P.emit("vector",
                                (lambda e, k4=k4, tt=tt:
                                 e.memset(vstage[:, k4 * 1040 + tt * 65 + 64:
                                                 k4 * 1040 + tt * 65 + 65], 1.0)),
                                inc="VE")
            VE_V = P.n("VE")

            for j8 in range(8):
                k4 = j8 % 4
                for t2 in range(2):
                    P.dma("gpsimd",
                          a1_in.ap()[j8, OFF_Q + t2 * 128 * TOK: OFF_Q + (t2 + 1) * 128 * TOK]
                          .rearrange("(p t) -> p t", p=128),
                          qhatT[:, (2 * k4 + t2) * TOK:(2 * k4 + t2 + 1) * TOK],
                          waits=[("VE", VE_HEAD)], sem="GD")
                P.dma("gpsimd",
                      a1_in.ap()[j8, OFF_K: OFF_K + 64 * TOK].rearrange("(p t) -> p t", p=64),
                      khatT[64 * (k4 % 2): 64 * (k4 % 2) + 64,
                            (k4 // 2) * TOK:(k4 // 2 + 1) * TOK],
                      waits=[("VE", VE_HEAD)], sem="GD")
                P.dma("gpsimd",
                      a1_in.ap()[j8, OFF_V: OFF_V + TOK * 65]
                      .rearrange("(tt p c) -> p tt c", p=128, c=65),
                      vstage[:, k4 * 1040:(k4 + 1) * 1040]
                      .rearrange("p (tt c) -> p tt c", c=65),
                      waits=[("VE", VE_V)], sem="GD")
            GD_A1 = P.n("GD")

    if PHASES < 3:
        raise _StopBuild()
    cc1 = P.emit("gpsimd",
                 (lambda e: e.collective_compute(
                     "AllToAll", ALU.bypass, replica_groups=groups,
                     ins=[a1_in.ap().opt()], outs=[a1_out.ap().opt()])),
                 waits=[("GD", P.n("GD"))], inc="CC")
    P.barrier()

    with ExitStack() as pb:
        qB = pb.enter_context(nc.sbuf_tensor("qB", [128, 2 * S], BF16))
        Ksb = pb.enter_context(nc.sbuf_tensor("Ksb", [128, S], BF16))
        Vsb = pb.enter_context(nc.sbuf_tensor("Vsb", [128, 64 * 65], BF16))
        probs = pb.enter_context(nc.sbuf_tensor("probs", [128, 4 * 1024], BF16))
        attnL = pb.enter_context(nc.sbuf_tensor("attnL", [128, 2 * S], BF16))
        rcs = pb.enter_context(nc.sbuf_tensor("rcs", [1, 512], F32))
        bcs = pb.enter_context(nc.sbuf_tensor("bcs", [64, 512], F32))
        cones1b = pb.enter_context(nc.sbuf_tensor("cones1b", [128, 64], BF16))
        pss = [pb.enter_context(nc.psum_tensor(f"b_s{i}", [128, 1024], F32))
               for i in range(2)]
        psa = [[pb.enter_context(nc.psum_tensor(f"b_a{i}{h}", [128, 512], F32))
                for h in range(2)] for i in range(2)]

        pbsh = ExitStack()
        qB2 = pbsh.enter_context(nc.sbuf_tensor("qB2", [128, 2 * S], BF16))
        Ksb2 = pbsh.enter_context(nc.sbuf_tensor("Ksb2", [128, S], BF16))
        Vsb2 = pbsh.enter_context(nc.sbuf_tensor("Vsb2", [128, 64 * 65], BF16))
        for dst, dst2, src in [(qB, qB2, 0), (Ksb, Ksb2, 1), (Vsb, Vsb2, 2)]:
            for half, d in [(0, dst), (1, dst2)]:
                for i in range(4):
                    blk = half * 4 + i
                    if src == 0:
                        for t2 in range(2):
                            for hh in range(2):
                                P.dma("sync",
                                      d[64 * hh:64 * hh + 64,
                                        t2 * S + i * TOK: t2 * S + (i + 1) * TOK],
                                      a1_out.ap()[blk, OFF_Q + (t2 * 2 + hh) * 64 * TOK:
                                                  OFF_Q + (t2 * 2 + hh + 1) * 64 * TOK]
                                      .rearrange("(p t) -> p t", p=64),
                                      waits=[("CC", cc1)])
                    elif src == 1:
                        for hh in range(2):
                            P.dma("sync",
                                  d[64 * hh:64 * hh + 64, i * TOK:(i + 1) * TOK],
                                  a1_out.ap()[blk, OFF_K: OFF_K + 64 * TOK]
                                  .rearrange("(p t) -> p t", p=64),
                                  waits=[("CC", cc1)])
                    else:
                        P.dma("sync",
                              d[:, i * 16 * 65:(i + 1) * 16 * 65]
                              .rearrange("p (tt c) -> p tt c", c=65),
                              a1_out.ap()[blk, OFF_V: OFF_V + TOK * 65]
                              .rearrange("(tt p c) -> p tt c", p=128, c=65),
                              waits=[("CC", cc1)])
        LD_B0 = P.n("LD")
        selv = 0
        for d, d2, w in [(qB, qB2, 2 * S), (Ksb, Ksb2, S), (Vsb, Vsb2, 64 * 65)]:
            P.emit("vector",
                   (lambda e, d=d, w=w:
                    e.tensor_scalar_mul(d[:, 0:w], d[:, 0:w], cbsel[:, 0:1])),
                   waits=[("LD", LD_B0)], inc="VE")
            selv = P.emit("vector",
                          (lambda e, d=d, d2=d2, w=w:
                           e.scalar_tensor_tensor(d[:, 0:w], d2[:, 0:w],
                                                  cbsel[:, 1:2], d[:, 0:w],
                                                  ALU.mult, ALU.add)),
                          inc="VE")
        LD_B = LD_B0

        pbsh.close()
        avstage = pb.enter_context(nc.sbuf_tensor("avstage", [128, 64 * 512], BF16))
        dtile = pb.enter_context(nc.sbuf_tensor("dtile", [128, 256], F32))
        P.emit("vector", lambda e: e.memset(cones1b[64:65, :], 1.0), inc="VE")
        CONB = P.n("VE")

        exp_done = [0, 0]
        probs_rd = [[0, 0], [0, 0]]
        av_done = [0, 0]
        for qc in range(16):
            n_kt = 4 * qc + 4
            mv = [0, 0]
            for kt in range(n_kt):
                diag = kt >= n_kt - 4
                buf = kt % 2
                for pair in range(2):
                    ps = pss[pair]
                    pidx = pair * 2 + buf
                    tv = None
                    for hh in range(2):
                        tv = P.emit(
                            "tensor",
                            (lambda e, pair=pair, qc=qc, kt=kt, hh=hh, ps=ps:
                             e.matmul(ps[:, hh * 512:(hh + 1) * 512],
                                      Ksb[64 * hh:64 * hh + 64, kt * 128:(kt + 1) * 128],
                                      qB[64 * hh:64 * hh + 64,
                                         pair * S + qc * 512:pair * S + (qc + 1) * 512],
                                      start=True, stop=True)),
                            waits=([("LD", LD_B), ("VE", selv),
                                    ("SE", exp_done[pair])]
                                   if hh == 0 else []),
                            inc="TE")
                    ex = P.emit(
                        "scalar",
                        (lambda e, ps=ps, pidx=pidx:
                         e.activation(probs[:, pidx * 1024:(pidx + 1) * 1024],
                                      ps[:, :], AF.Exp,
                                      bias=cnbias[:, 0:1], scale=1.0)),
                        waits=[("TE", tv), ("TE", probs_rd[pair][buf])], inc="SE")
                    exp_done[pair] = ex
                    if diag:
                        v = kt - (n_kt - 4)
                        for hh in range(2):
                            mv[pair] = P.emit(
                                "vector",
                                (lambda e, pidx=pidx, hh=hh, v=v:
                                 e.tensor_mul(probs[:, pidx * 1024 + hh * 512:
                                                    pidx * 1024 + (hh + 1) * 512],
                                              probs[:, pidx * 1024 + hh * 512:
                                                    pidx * 1024 + (hh + 1) * 512],
                                              cmask01[:, v * 512:(v + 1) * 512])),
                                waits=[("SE", ex)], inc="VE")
                for pair in range(2):
                    pidx = pair * 2 + buf
                    pready = ("VE", mv[pair]) if kt >= n_kt - 4 else ("SE", exp_done[pair])
                    for hh in range(2):
                        P.emit(
                            "tensor",
                            (lambda e, kt=kt, hh=hh, pair=pair, pidx=pidx, n_kt=n_kt:
                             e.matmul(psa[pair][hh][0:65, :],
                                      Vsb[:, kt * 65:(kt + 1) * 65],
                                      probs[:, pidx * 1024 + hh * 512:
                                            pidx * 1024 + (hh + 1) * 512],
                                      start=(kt == 0), stop=(kt == n_kt - 1))),
                            waits=([pready, ("VE", av_done[pair])]
                                   if kt == 0 else [pready]),
                            inc="TE")
                    probs_rd[pair][buf] = P.n("TE")
            TE_AV = P.n("TE")
            for pair in range(2):
                for hh in range(2):
                    t4 = qc * 4 + pair * 2 + hh
                    av_done[pair] = P.emit(
                        "vector",
                        (lambda e, pair=pair, hh=hh, t4=t4:
                         e.tensor_copy(avstage[0:65, t4 * 512:(t4 + 1) * 512],
                                       psa[pair][hh][0:65, :])),
                        waits=[("TE", TE_AV)], inc="VE")
        VE_EV = P.n("VE")
        P.barrier()

        dge = P.emit(
            "gpsimd",
            (lambda e: e.dma_start(out=dn_d[:], in_=avstage[64:65, :])),
            waits=[("VE", VE_EV)], inc="GD", by=16)
        dge2 = P.emit(
            "gpsimd",
            (lambda e: e.dma_start(out=dtile[:, :],
                                   in_=dn_d.ap().rearrange("(p c) -> p c", c=256))),
            waits=[("GD", P.n("GD"))], inc="GD", by=16)
        vrc = P.emit("vector",
                     (lambda e: e.reciprocal(dtile[:, :], dtile[:, :])),
                     waits=[("GD", P.n("GD"))], inc="VE")
        dge3 = P.emit(
            "gpsimd",
            (lambda e: e.dma_start(out=dn_d.ap().rearrange("(p c) -> p c", c=256),
                                   in_=dtile[:, :])),
            waits=[("VE", vrc)], inc="GD", by=16)
        dge4 = P.emit(
            "gpsimd",
            (lambda e: e.dma_start(out=avstage[64:65, :], in_=dn_d[:])),
            waits=[("GD", P.n("GD"))], inc="GD", by=16)
        GD_R = P.n("GD")
        bc_ve = [0, 0]
        for t4 in range(64):
            qc, rem = divmod(t4, 4)
            pair, hh = divmod(rem, 2)
            tb = P.emit("tensor",
                        (lambda e, pair=pair, t4=t4:
                         e.matmul(pss[pair][0:64, 0:512], cones1b[64:65, :],
                                  avstage[64:65, t4 * 512:(t4 + 1) * 512],
                                  start=True, stop=True)),
                        waits=[("GD", GD_R), ("SE", exp_done[pair]),
                               ("VE", max(bc_ve[pair], CONB))], inc="TE")
            vb = P.emit("vector",
                        (lambda e, pair=pair:
                         e.tensor_copy(bcs[:, :], pss[pair][0:64, 0:512])),
                        waits=[("TE", tb)], inc="VE")
            P.emit("vector",
                   (lambda e, pair=pair, qc=qc, hh=hh, t4=t4:
                    e.tensor_mul(attnL[64 * hh:64 * hh + 64,
                                       pair * S + qc * 512:pair * S + (qc + 1) * 512],
                                 avstage[0:64, t4 * 512:(t4 + 1) * 512],
                                 bcs[:, :])),
                   inc="VE")
            bc_ve[pair] = P.n("VE")
        P.barrier()
        VE_B = P.n("VE")
        if "attnL" in DEBUG_TAPS:
            mktap("attnL", [256, S], BF16)
            P.dma("gpsimd", taps["attnL"].ap().rearrange("(j p) t -> p j t", p=128),
                  attnL[:, :].rearrange("p (j t) -> p j t", t=S),
                  waits=[("VE", VE_B)], sem="ST")

        for j8 in range(8):
            k4 = j8 % 4
            for t2 in range(2):
                for hh in range(2):
                    P.dma("gpsimd",
                          a2_in.ap()[j8, (t2 * 2 + hh) * 64 * TOK:
                                     (t2 * 2 + hh + 1) * 64 * TOK]
                          .rearrange("(p t) -> p t", p=64),
                          attnL[64 * hh:64 * hh + 64,
                                t2 * S + k4 * TOK: t2 * S + (k4 + 1) * TOK],
                          waits=[("VE", VE_B)], sem="GD")

    GD_A2C = P.n("GD")
    P.barrier()
    cc2 = P.emit("gpsimd",
                 (lambda e: e.collective_compute(
                     "AllToAll", ALU.bypass, replica_groups=groups,
                     ins=[a2_in.ap().opt()], outs=[a2_out.ap().opt()])),
                 waits=[("GD", GD_A2C)], inc="CC")
    if "xnTd" in DEBUG_TAPS:
        mktap("xnTd", [D, CL], BF16)
        P.dma("gpsimd", taps["xnTd"][:, :], xnT_d[:, :],
              waits=[("CC", P.n("CC"))], sem="ST")
    if "a2o" in DEBUG_TAPS:
        mktap("a2o", [8, SH2_W], BF16)
        P.dma("gpsimd", taps["a2o"][:, :], a2_out[:, :],
              waits=[("CC", P.n("CC"))], sem="ST")

    if PHASES < 4:
        raise _StopBuild()
    with ExitStack() as pc:
        xnC = pc.enter_context(nc.sbuf_tensor("xnC", [128, 32 * 1024], BF16))
        afH = pc.enter_context(nc.sbuf_tensor("afH", [128, 8 * 1024], BF16))
        afH2 = pc.enter_context(nc.sbuf_tensor("afH2", [128, 8 * 1024], BF16))
        wts = pc.enter_context(nc.sbuf_tensor("wts", [128, 2 * 32 * 128], BF16))
        wos = pc.enter_context(nc.sbuf_tensor("wos", [128, 2 * 8 * 128], BF16))
        gsb = pc.enter_context(nc.sbuf_tensor("gsb", [128, 1024], F32))
        gstage = pc.enter_context(nc.sbuf_tensor("gstage", [128, 8 * 1024], BF16))
        sgs = pc.enter_context(nc.sbuf_tensor("sgs", [128, 4 * 1024], F32))
        xcs = pc.enter_context(nc.sbuf_tensor("xcs", [128, 2 * 512], F32))
        osb = pc.enter_context(nc.sbuf_tensor("osb", [128, 2 * 512], F32))
        psg = [pc.enter_context(nc.psum_tensor(f"c_g{i}", [128, 512], F32)) for i in range(2)]
        pso = [pc.enter_context(nc.psum_tensor(f"c_s{i}", [128, 512], F32)) for i in range(2)]
        pstp = [pc.enter_context(nc.psum_tensor(f"c_t{i}", [128, 512], F32)) for i in range(2)]

        psg_ev = [0, 0]
        pso_ev = [0, 0]
        gsb_ev = [0, 0]
        pst_ev = [0, 0]
        xcs_ev = [0, 0]
        ost_st = [0, 0]
        wldt = [0, 0]
        wldo = [0, 0]
        GPRE = 8
        te_prevH = 0
        for H in range(2):
            lh = P.dma("sync", xnC[:, :].rearrange("p (j t) -> p j t", t=1024),
                       xnT_d.ap().rearrange("(j p) t -> p j t", p=128)[:, :, 8 + H * 1024:
                                                                      8 + (H + 1) * 1024],
                       waits=[("TE", te_prevH)])
            for ot in range(GPRE):
                slot = ot % 2
                dsl = "D0" if slot == 0 else "D1"
                lw = None
                for rr8 in range(8):
                    lw = P.dma("sync",
                               wts[:, slot * 4096 + rr8 * 512:
                                   slot * 4096 + (rr8 + 1) * 512]
                               .rearrange("p (b c) -> p b c", c=128),
                               pp_out.ap()[rr8, OF:OF + 2097152]
                               .rearrange("(b p c) -> p b c", p=128, c=4096)
                               [:, :, ot * 128:(ot + 1) * 128],
                               waits=([("TE", wldt[slot])] if rr8 == 0 else []),
                               sem=dsl)
                for tc in range(2):
                    ps = psg[tc]
                    tv = None
                    for j in range(32):
                        tv = P.emit(
                            "tensor",
                            (lambda e, j=j, tc=tc, ps=ps, slot=slot:
                             e.matmul(ps[:, :],
                                      wts[:, slot * 4096 + j * 128: slot * 4096 + (j + 1) * 128],
                                      xnC[:, j * 1024 + tc * 512: j * 1024 + (tc + 1) * 512],
                                      start=(j == 0), stop=(j == 31))),
                            waits=([(dsl, lw), ("LD", lh), ("SE", psg_ev[tc])]
                                   if j == 0 else []),
                            inc="TE")
                    sg_ = P.emit(
                        "scalar",
                        (lambda e, ot=ot, tc=tc, ps=ps:
                         e.activation(gsb[:, tc * 512:(tc + 1) * 512], ps[:, :],
                                      AF.Silu, bias=cfb[:, ot:ot + 1], scale=1.0)),
                        waits=[("TE", tv), ("VE", gsb_ev[tc])], inc="SE")
                    psg_ev[tc] = sg_
                    gsb_ev[tc] = P.emit(
                        "vector",
                        (lambda e, ot=ot, tc=tc:
                         e.tensor_copy(gstage[:, ot * 1024 + tc * 512:
                                              ot * 1024 + (tc + 1) * 512],
                                       gsb[:, tc * 512:(tc + 1) * 512])),
                        waits=[("SE", sg_)], inc="VE")
                wldt[slot] = P.n("TE")
            for half, d in [(0, afH), (1, afH2)]:
                for i in range(4):
                    for ci in range(2):
                        P.dma("gpsimd",
                              d[:, (2 * i + ci) * 1024:(2 * i + ci + 1) * 1024],
                              a2_out.ap().rearrange("r (c p t) -> r c p t", p=128, t=TOK)
                              [half * 4 + i, ci, :, H * 1024:(H + 1) * 1024],
                              waits=[("CC", cc2), ("TE", te_prevH)])
            LD_C0 = P.n("GD")
            P.emit("vector",
                   (lambda e: e.tensor_scalar_mul(afH[:, :], afH[:, :], cbsel[:, 0:1])),
                   waits=[("GD", LD_C0), ("TE", te_prevH)], inc="VE")
            afsel = P.emit("vector",
                           (lambda e: e.scalar_tensor_tensor(afH[:, :], afH2[:, :],
                                                             cbsel[:, 1:2], afH[:, :],
                                                             ALU.mult, ALU.add)),
                           inc="VE")
            afl = LD_C0
            for ot in range(32):
                slot = ot % 2
                dsl = "D0" if slot == 0 else "D1"
                wsl = "W0" if slot == 0 else "W1"
                if ot >= GPRE:
                    lw = None
                    for rr8 in range(8):
                        lw = P.dma("sync",
                                   wts[:, slot * 4096 + rr8 * 512:
                                       slot * 4096 + (rr8 + 1) * 512]
                                   .rearrange("p (b c) -> p b c", c=128),
                                   pp_out.ap()[rr8, OF:OF + 2097152]
                                   .rearrange("(b p c) -> p b c", p=128, c=4096)
                                   [:, :, ot * 128:(ot + 1) * 128],
                                   waits=([("TE", wldt[slot])] if rr8 == 0 else []),
                                   sem=dsl)
                lwo = P.dma("sync",
                            wos[:, slot * 1024:(slot + 1) * 1024]
                            .rearrange("p (j c) -> p j c", c=128),
                            pp_out.ap()[:, OW:OW + 524288]
                            .rearrange("r (p c) -> p r c", p=128, c=4096)
                            [:, :, ot * 128:(ot + 1) * 128],
                            waits=[("TE", wldo[slot])], sem=wsl)
                for tc in range(2):
                    if ot >= GPRE:
                        ps = psg[tc]
                        tv = None
                        for j in range(32):
                            tv = P.emit(
                                "tensor",
                                (lambda e, j=j, tc=tc, ps=ps, slot=slot:
                                 e.matmul(ps[:, :],
                                          wts[:, slot * 4096 + j * 128: slot * 4096 + (j + 1) * 128],
                                          xnC[:, j * 1024 + tc * 512: j * 1024 + (tc + 1) * 512],
                                          start=(j == 0), stop=(j == 31))),
                                waits=([(dsl, lw), ("LD", lh), ("SE", psg_ev[tc])]
                                       if j == 0 else []),
                                inc="TE")
                        sg_ = P.emit(
                            "scalar",
                            (lambda e, ot=ot, tc=tc, ps=ps:
                             e.activation(gsb[:, tc * 512:(tc + 1) * 512], ps[:, :],
                                          AF.Silu, bias=cfb[:, ot:ot + 1], scale=1.0)),
                            waits=[("TE", tv), ("VE", gsb_ev[tc])], inc="SE")
                        psg_ev[tc] = sg_
                    pso_ = pso[tc]
                    tvo = None
                    for j in range(8):
                        tvo = P.emit(
                            "tensor",
                            (lambda e, j=j, tc=tc, pso_=pso_, slot=slot:
                             e.matmul(pso_[:, :],
                                      wos[:, slot * 1024 + j * 128: slot * 1024 + (j + 1) * 128],
                                      afH[:, j * 1024 + tc * 512: j * 1024 + (tc + 1) * 512],
                                      start=(j == 0), stop=(j == 7))),
                            waits=([(wsl, lwo), ("GD", afl), ("VE", afsel), ("VE", pso_ev[tc])]
                                   if j == 0 else []),
                            inc="TE")
                    if ot >= GPRE:
                        vsg = P.emit(
                            "vector",
                            (lambda e, ot=ot, tc=tc, pso_=pso_:
                             e.tensor_mul(sgs[:, (ot % 4) * 1024 + tc * 512:
                                              (ot % 4) * 1024 + (tc + 1) * 512],
                                          pso_[:, :], gsb[:, tc * 512:(tc + 1) * 512])),
                            waits=[("TE", tvo), ("SE", sg_)], inc="VE")
                        gsb_ev[tc] = vsg
                    else:
                        vsg = P.emit(
                            "vector",
                            (lambda e, ot=ot, tc=tc, pso_=pso_:
                             e.tensor_mul(sgs[:, (ot % 4) * 1024 + tc * 512:
                                              (ot % 4) * 1024 + (tc + 1) * 512],
                                          pso_[:, :],
                                          gstage[:, ot * 1024 + tc * 512:
                                                 ot * 1024 + (tc + 1) * 512])),
                            waits=[("TE", tvo)], inc="VE")
                    pso_ev[tc] = vsg
                if ot >= GPRE:
                    wldt[slot] = P.n("TE")
                wldo[slot] = P.n("TE")
                if ot % 4 == 3:
                    oc = ot // 4
                    VSG = P.n("VE")
                    for tc in range(2):
                        for tb in range(4):
                            sl = (tc * 4 + tb) % 2
                            pt = pstp[sl]
                            xsem = "D2" if sl == 0 else "D3"
                            ssem = "SA" if sl == 0 else "SB"
                            lx = P.dma("scalar", xcs[:, sl * 512:(sl + 1) * 512],
                                       x_loc[HALO + H * 1024 + tc * 512 + tb * 128:
                                             HALO + H * 1024 + tc * 512 + (tb + 1) * 128,
                                             oc * 512:(oc + 1) * 512],
                                       waits=[("VE", xcs_ev[sl])], sem=xsem)
                            tt = None
                            for i4 in range(4):
                                tt = P.emit(
                                    "tensor",
                                    (lambda e, i4=i4, tc=tc, tb=tb, pt=pt:
                                     e.transpose(pt[:, i4 * 128:(i4 + 1) * 128],
                                                 sgs[:, i4 * 1024 + tc * 512 + tb * 128:
                                                     i4 * 1024 + tc * 512 + (tb + 1) * 128],
                                                 cidf[:, :])),
                                    waits=[("VE", VSG), ("VE", pst_ev[sl])], inc="TE")
                            vv = P.emit(
                                "vector",
                                (lambda e, sl=sl, pt=pt:
                                 e.tensor_add(osb[:, sl * 512:(sl + 1) * 512],
                                              pt[:, :], xcs[:, sl * 512:(sl + 1) * 512])),
                                waits=[("TE", tt), (xsem, lx), (ssem, ost_st[sl])], inc="VE")
                            st = P.dma("gpsimd",
                                       out_loc[H * 1024 + tc * 512 + tb * 128:
                                               H * 1024 + tc * 512 + (tb + 1) * 128,
                                               oc * 512:(oc + 1) * 512],
                                       osb[:, sl * 512:(sl + 1) * 512],
                                       waits=[("VE", vv)], sem=ssem)
                            pst_ev[sl] = vv
                            xcs_ev[sl] = vv
                            ost_st[sl] = st
            te_prevH = P.n("TE")
    P.emit("gpsimd", None, waits=[("ST", P.n("ST")), ("SA", P.n("SA")),
                                  ("SB", P.n("SB"))])
    P.barrier()

    with nc.Block() as block:
        with allow_ici_all_to_all():
            _replay(P, block, sems)
    es.close()
    return nc, list(taps.keys())


_BUILT = {}


def _consts():
    ident = np.eye(128)
    rot = np.zeros((128, 128), np.float32)
    for i in range(64):
        rot[2 * i + 1, 2 * i] = -1.0
        rot[2 * i, 2 * i + 1] = 1.0
    meank = np.zeros((128, 128), np.float32)
    meanq = np.zeros((128, 128), np.float32)
    for kk in range(128):
        for m64 in range(2):
            meank[kk, m64 * 64 + kk % 64] = 0.125
            meanq[kk, m64 * 64 + kk % 64] = 0.03125
    b2 = np.zeros((2, 128), np.float32)
    b2[0, 0:64] = 1.0
    b2[1, 64:128] = 1.0
    m01 = np.zeros((128, 2048), np.float32)
    kk = np.arange(128)[:, None]
    qq = np.arange(512)[None, :]
    for v in range(4):
        m01[:, v * 512:(v + 1) * 512] = (qq >= kk + 128 * v).astype(np.float32)
    return ident, rot, meank, meanq, b2, m01


def kernel(**inputs):
    x = np.ascontiguousarray(np.asarray(inputs["L0_attn_x"], np.float32))
    ident, rot, meank, meanq, b2, m01 = _consts()
    qc1_b = np.asarray(inputs["L0_attn_qc1_b"], np.float32)
    qc2_b = np.asarray(inputs["L0_attn_qc2_b"], np.float32)
    kc1_b = np.asarray(inputs["L0_attn_kc1_b"], np.float32)
    kc2_b = np.asarray(inputs["L0_attn_kc2_b"], np.float32)
    scales = np.asarray(inputs["L0_ffn_scales"], np.float32)[:, :, 0]
    zeros = np.asarray(inputs["L0_ffn_scaled_zeros"], np.float32)[:, :, 0]
    fbias = np.asarray(inputs["L0_ffn_bias"], np.float32)
    ktemp = float(np.asarray(inputs["L0_attn_key_temp"], np.float32).reshape(-1)[0])

    inv = 1.0 / 10000.0 ** (np.arange(0, HD, 2, dtype=np.float32) / HD)

    wqkv_full = np.asarray(inputs["L0_attn_w_qkv"], np.float32)
    wo_full = np.asarray(inputs["L0_attn_w_o"], np.float32)
    qc2_full = np.asarray(inputs["L0_attn_qc2_w"], np.float32)
    qw_full = np.asarray(inputs["L0_ffn_qweight"], np.int32)
    scT_full = scales.reshape(32, 128, 32).transpose(1, 0, 2)
    zrT_full = zeros.reshape(32, 128, 32).transpose(1, 0, 2)
    shared = dict(
        qc1_w=np.ascontiguousarray(np.asarray(inputs["L0_attn_qc1_w"], np.float32)),
        kc1_w=np.ascontiguousarray(np.asarray(inputs["L0_attn_kc1_w"], np.float32)),
        kc2_w=np.ascontiguousarray(np.asarray(inputs["L0_attn_kc2_w"], np.float32)),
        qc1_bT=np.ascontiguousarray(qc1_b.reshape(16, 64).T),
        qc2_bT=np.ascontiguousarray(qc2_b.reshape(8, 128).T),
        kc1_bT=np.ascontiguousarray(kc1_b.reshape(4, 64).T),
        kc2_bT=np.ascontiguousarray(kc2_b.reshape(2, 128).T),

        ffn_biasT=np.ascontiguousarray(fbias.reshape(32, 128).T),
        key_temp_b=np.full((128, 1), ktemp, np.float32),
        ident_bf=ident.astype(BFNP),
        ident_f32=ident.astype(np.float32),
        rot_bf=rot.astype(BFNP),
        meank_l=meank.astype(BFNP),
        meanq_l=meanq.astype(BFNP),
        bcast2=b2,
        mask01=m01.astype(BFNP),
    )

    in_maps = []
    for c in range(8):
        b = c // 4
        p0 = (c % 4) * TOK
        xl = np.zeros((CL, D), np.float32)
        lo = max(0, p0 - HALO)
        xl[HALO - (p0 - lo):] = x[b, lo:p0 + TOK]
        pos = (p0 + np.arange(TOK, dtype=np.float32))[None, :]
        ang = pos * inv[(np.arange(128) % 64) // 2][:, None]
        m = dict(shared)
        m["w_qkv"] = np.ascontiguousarray(wqkv_full[:, 512 * c:512 * (c + 1)])
        m["w_o"] = np.ascontiguousarray(wo_full[:, 128 * c:128 * (c + 1)])
        m["qc2_w"] = np.ascontiguousarray(qc2_full[:, 128 * c:128 * (c + 1), :])
        m["qweight"] = np.ascontiguousarray(qw_full[:, 4 * c:4 * (c + 1), :])
        m["scalesT"] = np.ascontiguousarray(scT_full[:, :, 4 * c:4 * (c + 1)])
        m["zerosT"] = np.ascontiguousarray(zrT_full[:, :, 4 * c:4 * (c + 1)])
        m["x_loc"] = xl
        m["halo_mask"] = np.full((128, 1), 0.0 if p0 == 0 else 1.0, np.float32)
        bs = np.zeros((128, 2), np.float32)
        bs[:, 0 if b == 0 else 1] = 1.0
        m["bsel"] = bs
        m["cosT"] = np.cos(ang).astype(BFNP)
        m["sinT"] = np.sin(ang).astype(BFNP)
        in_maps.append(m)

    key = "prog"
    if key not in _BUILT:
        _BUILT[key] = build_program()
    nc, tapnames = _BUILT[key]
    res = run_bass_kernel_spmd(nc, in_maps, core_ids=list(range(8)))
    out = np.empty((B, S, D), np.float32)
    for c in range(8):
        out[c // 4, (c % 4) * TOK:(c % 4 + 1) * TOK] = res.results[c]["out_loc"]
    if DEBUG_TAPS:
        kernel.tap_results = res.results
    return out


if __name__ == "__main__":
    pass


# version 14
# speedup vs baseline: 1.2330x; 1.2330x over previous
import numpy as np
import ml_dtypes
from contextlib import ExitStack

import concourse.bass as bass
import concourse.mybir as mybir
from concourse.bass_utils import run_bass_kernel_spmd
from concourse.aot_env import allow_ici_all_to_all

BFNP = ml_dtypes.bfloat16
F32 = mybir.dt.float32
BF16 = mybir.dt.bfloat16
I32 = mybir.dt.int32
AF = mybir.ActivationFunctionType
ALU = mybir.AluOpType

B, S, D = 2, 8192, 4096
NH, NKV, HD, LAT, KVD = 16, 4, 64, 1024, 256
TOK = 2048
HALO = 8
CL = TOK + HALO
C1L = TOK + 2

OFF_Q = 0
OFF_K = 256 * TOK
OFF_V = OFF_K + 64 * TOK
SH_W = OFF_V + TOK * 65
SH2_W = 256 * TOK

DEBUG_TAPS = ()
PHASES = 4

TC5 = [(0, 512), (512, 1024), (1024, 1536), (1536, 2048), (2048, CL)]
TC5C = [(0, 512), (512, 1024), (1024, 1536), (1536, 2048), (2048, C1L)]
TC4 = [(0, 512), (512, 1024), (1024, 1536), (1536, 2048)]


class Plan:
    ENG = ("sync", "gpsimd", "tensor", "vector", "scalar")

    def __init__(self):
        self.ops = {e: [] for e in self.ENG}
        self.cnt = {}

    def n(self, sem):
        return self.cnt.get(sem, 0)

    def emit(self, eng, fn, waits=(), inc=None, by=1):
        if inc is not None:
            self.cnt[inc] = self.cnt.get(inc, 0) + by
        self.ops[eng].append((fn, tuple(waits), inc, by))
        return self.cnt.get(inc) if inc is not None else None

    def dma(self, queue, out, in_, waits=(), sem=None, **kw):
        if sem is None:
            sem = {"sync": "LD", "gpsimd": "GD", "scalar": "XD"}[queue]
        return self.emit(
            queue,
            (lambda e, out=out, in_=in_, kw=kw: e.dma_start(out=out, in_=in_, **kw)),
            waits, inc=sem, by=16)

    def barrier(self, engines=None):
        engines = engines or self.ENG
        vals = {}
        for e in engines:
            vals[e] = self.emit(e, (lambda eng: eng.nop()), (), inc="B" + e, by=1)
        for e in engines:
            waits = tuple(("B" + o, vals[o]) for o in engines if o != e)
            self.emit(e, None, waits)


def _replay(plan, block, sems):
    def mk(ename):
        def body(eng):
            hw = {}
            for fn, waits, inc, by in plan.ops[ename]:
                for s, v in waits:
                    if v > hw.get(s, 0):
                        eng.wait_ge(sems[s], v)
                        hw[s] = v
                if fn is not None:
                    ins = fn(eng)
                    if inc is not None:
                        ins.then_inc(sems[inc], by)
        return body
    block.sync(mk("sync"))
    block.gpsimd(mk("gpsimd"))
    block.tensor(mk("tensor"))
    block.vector(mk("vector"))
    block.scalar(mk("scalar"))


class _StopBuild(Exception):
    pass


_CTX = {}


def build_program():
    try:
        return _build_inner()
    except _StopBuild:
        nc, P, sems, es, taps = (_CTX["nc"], _CTX["P"], _CTX["sems"],
                                 _CTX["es"], _CTX["taps"])
        P.emit("gpsimd", None, waits=[("ST", P.n("ST")), ("SA", P.n("SA")),
                                      ("SB", P.n("SB"))])
        P.barrier()
        with nc.Block() as block:
            with allow_ici_all_to_all():
                _replay(P, block, sems)
        es.close()
        return nc, list(taps.keys())


def _build_inner():
    nc = bass.Bass("TRN2", target_bir_lowering=False, debug=False)
    P = Plan()

    def I(name, shape, dt):
        return nc.dram_tensor(name, shape, dt, kind="ExternalInput")

    x_loc = I("x_loc", [CL, D], F32)
    w_qkv = I("w_qkv", [1536, 512], F32)
    w_o = I("w_o", [D, 128], F32)
    qc1_w = I("qc1_w", [LAT, HD, 3], F32)
    qc2_w = I("qc2_w", [LAT, 128, 3], F32)
    kc1_w = I("kc1_w", [KVD, HD, 3], F32)
    kc2_w = I("kc2_w", [KVD, KVD, 3], F32)
    qc1_bT = I("qc1_bT", [64, 16], F32)
    qc2_bT = I("qc2_bT", [128, 8], F32)
    kc1_bT = I("kc1_bT", [64, 4], F32)
    kc2_bT = I("kc2_bT", [128, 2], F32)
    qweight = I("qweight", [D, 4, 128], I32)
    scalesT = I("scalesT", [128, 32, 4], F32)
    zerosT = I("zerosT", [128, 32, 4], F32)
    ffn_biasT = I("ffn_biasT", [128, 32], F32)
    key_temp_b = I("key_temp_b", [128, 1], F32)
    halo_mask = I("halo_mask", [128, 1], F32)
    ident_bf = I("ident_bf", [128, 128], BF16)
    ident_f32 = I("ident_f32", [128, 128], F32)
    rot_bf = I("rot_bf", [128, 128], BF16)
    meank_l = I("meank_l", [128, 128], BF16)
    meanq_l = I("meanq_l", [128, 128], BF16)
    bcast2 = I("bcast2", [2, 128], F32)
    cosT = I("cosT", [128, TOK], BF16)
    sinT = I("sinT", [128, TOK], BF16)
    mask01 = I("mask01", [128, 2048], BF16)
    bsel = I("bsel", [128, 2], F32)

    out_loc = nc.dram_tensor("out_loc", [TOK, D], F32, kind="ExternalOutput")
    taps = {}
    def mktap(name, shape, dt):
        if name in DEBUG_TAPS:
            taps[name] = nc.dram_tensor("tap_" + name, shape, dt, kind="ExternalOutput")

    kc2T_d = nc.dram_tensor("kc2T_d", [3, KVD, KVD], BF16)
    OQ, OW, OC2, OF = 0, 786432, 1310720, 1703936
    PSL = 3801088
    pp_in = nc.dram_tensor("pp_in", [1, PSL], BF16)
    pp_out = nc.dram_tensor("pp_out", [8, PSL], BF16)
    xnT_d = nc.dram_tensor("xnT_d", [D, CL], BF16)
    a1_in = nc.dram_tensor("a1_in", [8, SH_W], BF16)
    a1_out = nc.dram_tensor("a1_out", [8, SH_W], BF16)
    a2_in = nc.dram_tensor("a2_in", [8, SH2_W], BF16)
    a2_out = nc.dram_tensor("a2_out", [8, SH2_W], BF16)
    dn_d = nc.dram_tensor("dn_d", [64 * 512], BF16)
    groups = [[0, 1, 2, 3, 4, 5, 6, 7]]

    es = ExitStack()
    sems = {}
    for s in ["LD", "GD", "ST", "XD", "CC", "TE", "VE", "SE",
              "D0", "D1", "D2", "D3", "SA", "SB", "W0", "W1",
              "Bsync", "Bgpsimd", "Btensor", "Bvector", "Bscalar"]:
        sems[s] = es.enter_context(nc.semaphore(s))

    sb = lambda *a: es.enter_context(nc.sbuf_tensor(*a))
    cid = sb("c_id", [128, 128], BF16)
    cidf = sb("c_idf", [128, 128], F32)
    crot = sb("c_rot", [128, 128], BF16)
    cmk = sb("c_mk", [128, 128], BF16)
    cmq = sb("c_mq", [128, 128], BF16)
    cb2 = sb("c_b2", [2, 128], F32)
    ccos = sb("c_cos", [128, TOK], BF16)
    csin = sb("c_sin", [128, TOK], BF16)
    cq1b = sb("c_q1b", [64, 16], F32)
    cq2b = sb("c_q2b", [128, 8], F32)
    ck1b = sb("c_k1b", [64, 4], F32)
    ck2b = sb("c_k2b", [128, 2], F32)
    cfb = sb("c_fb", [128, 32], F32)
    ctmp = sb("c_tmp", [128, 1], F32)
    ctmp8 = sb("c_tmp8", [128, 1], F32)
    cnbias = sb("c_nbias", [128, 1], F32)
    chmask = sb("c_hmask", [128, 1], F32)
    cbl1 = sb("c_bl1", [128, 2], F32)
    cones1 = sb("c_ones1", [1, 64], F32)
    cmask01 = sb("c_mask01", [128, 2048], BF16)
    cbsel = sb("c_bsel", [128, 2], F32)
    ceps = sb("c_eps", [128, 1], F32)
    qc1T = sb("qc1T", [128, 8 * 3 * 64], BF16)
    kc1T = sb("kc1T", [128, 2 * 3 * 64], BF16)

    for t, src in [(cid, ident_bf), (cidf, ident_f32), (crot, rot_bf),
                   (cmk, meank_l), (cmq, meanq_l), (cb2, bcast2),
                   (ccos, cosT), (csin, sinT), (cq1b, qc1_bT), (cq2b, qc2_bT),
                   (ck1b, kc1_bT), (ck2b, kc2_bT), (cfb, ffn_biasT),
                   (chmask, halo_mask), (cmask01, mask01), (ctmp, key_temp_b),
                   (cbsel, bsel)]:
        P.dma("sync", t[:, :], src[:, :], sem="D0")
    LDC = P.n("D0")
    P.emit("scalar", lambda e: e.activation(ctmp8[:, :], ctmp[:, :], AF.Copy, bias=0.0, scale=0.125),
           waits=[("D0", LDC)], inc="SE")
    sA = P.emit("scalar", lambda e: e.activation(cnbias[:, :], ctmp[:, :], AF.Abs, bias=0.0, scale=0.125),
                waits=[("D0", LDC)], inc="SE")
    P.emit("vector", lambda e: e.tensor_scalar_mul(cnbias[:, :], cnbias[:, :], -1.0),
           waits=[("SE", sA)], inc="VE")
    P.emit("vector", lambda e: e.memset(cbl1[:, :], 0.0), inc="VE")
    P.emit("vector", lambda e: e.memset(cbl1[0:64, 0:1], 1.0), inc="VE")
    P.emit("vector", lambda e: e.memset(cbl1[64:128, 1:2], 1.0), inc="VE")
    P.emit("vector", lambda e: e.memset(cones1[:, :], 1.0), inc="VE")
    VEPS = P.emit("vector", lambda e: e.memset(ceps[:, :], 1e-5), inc="VE")

    with ExitStack() as p0:
        wbf = p0.enter_context(nc.sbuf_tensor("p0_wbf", [128, 17408], BF16))
        big = p0.enter_context(nc.sbuf_tensor("p0_big", [128, 16384], BF16))
        stg = [p0.enter_context(nc.sbuf_tensor(f"p0_stg{i}", [128, 4096], BF16))
               for i in range(2)]
        stg_ld = [0, 0]
        qwi = p0.enter_context(nc.sbuf_tensor("p0_qwi", [128, 2 * 512], I32))
        scz = p0.enter_context(nc.sbuf_tensor("p0_scz", [128, 256], F32))
        wde = p0.enter_context(nc.sbuf_tensor("p0_wde", [128, 2 * 512], BF16))
        pst = [p0.enter_context(nc.psum_tensor(f"p0_ps{i}", [128, 512], BF16))
               for i in range(2)]
        pst_ve = [0, 0]
        psn = [0]

        def tgroup(src_aps, ready_waits):
            si = psn[0] % 2
            psn[0] += 1
            ps = pst[si]
            tv = None
            for i, ap in enumerate(src_aps):
                w = list(ready_waits) + [("VE", pst_ve[si]), ("D0", LDC)]
                tv = P.emit("tensor",
                            (lambda e, ap=ap, ps=ps, i=i:
                             e.transpose(ps[:, i * 128:(i + 1) * 128], ap, cid[:, :])),
                            waits=w, inc="TE")
            return ps, si, tv

        def evac(dst_ap, width, ps, si, tv, extra=()):
            sv = P.emit("vector",
                        (lambda e, ps=ps, dst_ap=dst_ap, width=width:
                         e.tensor_copy(dst_ap, ps[:, 0:width])),
                        waits=[("TE", tv)] + list(extra), inc="VE")
            pst_ve[si] = sv
            return sv

        def ppdst(off, n):
            return pp_in.ap()[0, off:off + n]

        W_QKV, W_O, W_Q2, W_K2, W_Q1, W_K1 = 0, 6144, 10240, 13312, 14848, 16384
        P.dma("gpsimd", wbf[:, W_QKV:W_QKV + 6144].rearrange("p (t c) -> p t c", c=512),
              w_qkv.ap().rearrange("(t p) c -> p t c", p=128), sem="XD")
        P.dma("gpsimd", wbf[:, W_O:W_O + 4096].rearrange("p (t c) -> p t c", c=128),
              w_o.ap().rearrange("(t p) c -> p t c", p=128), sem="XD")
        P.dma("gpsimd", wbf[:, W_Q2:W_Q2 + 3072].rearrange("p (t c) -> p t c", c=384),
              qc2_w.ap().rearrange("(t p) i k -> p t (i k)", p=128), sem="XD")
        P.dma("gpsimd", wbf[:, W_K2:W_K2 + 1536].rearrange("p (t c) -> p t c", c=768),
              kc2_w.ap().rearrange("(t p) i k -> p t (i k)", p=128), sem="XD")
        P.dma("gpsimd", wbf[:, W_Q1:W_Q1 + 1536].rearrange("p (t c) -> p t c", c=192),
              qc1_w.ap().rearrange("(t p) i k -> p t (i k)", p=128), sem="XD")
        P.dma("gpsimd", wbf[:, W_K1:W_K1 + 384].rearrange("p (t c) -> p t c", c=192),
              kc1_w.ap().rearrange("(t p) i k -> p t (i k)", p=128), sem="XD")
        GD_L = P.n("XD")
        P.dma("sync", scz[:, 0:128], scalesT[:, :, :])
        l2 = P.dma("sync", scz[:, 128:256], zerosT[:, :, :])

        for j in range(4):
            sgi = j % 2
            for g0 in range(0, 12, 4):
                aps = [wbf[:, W_QKV + (g0 + t) * 512 + j * 128:
                           W_QKV + (g0 + t) * 512 + (j + 1) * 128] for t in range(4)]
                ps, si, tv = tgroup(aps, [("XD", GD_L)])
                evac(stg[sgi][:, g0 * 128:(g0 + 4) * 128], 512, ps, si, tv,
                     extra=[("SA" if sgi == 0 else "SB", stg_ld[sgi])])
            sv = P.n("VE")
            ld = P.dma("sync",
                       ppdst(OQ + j * 128 * 1536, 128 * 1536)
                       .rearrange("(p c) -> p c", c=1536),
                       stg[sgi][:, 0:1536],
                       waits=[("VE", sv)], sem=("SA" if sgi == 0 else "SB"))
            stg_ld[sgi] = ld

        for g0 in range(0, 32, 4):
            sgi = 0
            aps = [wbf[:, W_O + (g0 + t) * 128: W_O + (g0 + t + 1) * 128]
                   for t in range(4)]
            ps, si, tv = tgroup(aps, [("XD", GD_L)])
            evac(stg[0][:, g0 * 128:(g0 + 4) * 128], 512, ps, si, tv,
                 extra=[("SA", stg_ld[0])])
        sv = P.n("VE")
        ld = P.dma("sync", ppdst(OW, 524288).rearrange("(p c) -> p c", c=4096),
                   stg[0][:, 0:4096], waits=[("VE", sv)], sem="SA")
        stg_ld[0] = ld

        for tau in range(3):
            rps = None
            for ot in range(8):
                rps = P.emit(
                    "vector",
                    (lambda e, tau=tau, ot=ot:
                     e.tensor_copy(big[:, ot * 128:(ot + 1) * 128],
                                   wbf[:, W_Q2 + ot * 384 + tau:
                                       W_Q2 + (ot + 1) * 384:3])),
                    waits=[("XD", GD_L), ("TE", P.n("TE"))], inc="VE")
            sgi = 1
            for g0 in range(0, 8, 4):
                aps = [big[:, (g0 + t) * 128:(g0 + t + 1) * 128] for t in range(4)]
                ps, si, tv = tgroup(aps, [("VE", rps)])
                evac(stg[1][:, g0 * 128:(g0 + 4) * 128], 512, ps, si, tv,
                     extra=[("SB", stg_ld[1])])
            sv = P.n("VE")
            ld = P.dma("sync",
                       ppdst(OC2 + tau * 131072, 131072)
                       .rearrange("(p c) -> p c", c=1024),
                       stg[1][:, 0:1024], waits=[("VE", sv)], sem="SB")
            stg_ld[1] = ld

        for r in range(2):
            rpv = {}
            for tau in range(3):
                rpv[tau] = P.emit(
                    "vector",
                    (lambda e, r=r, tau=tau:
                     e.tensor_copy(big[:, 1024 + tau * 256: 1024 + (tau + 1) * 256],
                                   wbf[:, W_K2 + r * 768 + tau:
                                       W_K2 + (r + 1) * 768:3])),
                    waits=[("XD", GD_L), ("TE", P.n("TE"))], inc="VE")
            for tau in range(3):
                base = 1024 + tau * 256
                aps = [big[:, base + i * 128: base + (i + 1) * 128] for i in range(2)]
                ps, si, tv = tgroup(aps, [("VE", rpv[tau])])
                sgi = tau % 2
                sv = evac(stg[sgi][:, 0:256], 256, ps, si, tv,
                          extra=[("SA" if sgi == 0 else "SB", stg_ld[sgi])])
                dst = kc2T_d.ap()[tau, :, r * 128:(r + 1) * 128]
                dst = dst.rearrange("(b p) c -> p b c", p=128)
                ld = P.dma("sync", dst,
                           stg[sgi][:, 0:256].rearrange("p (b c) -> p b c", c=128),
                           waits=[("VE", sv)], sem=("SA" if sgi == 0 else "SB"))
                stg_ld[sgi] = ld

        def conv1_prep(nr, srcbase, dstT):
            for r in range(nr):
                vv = None
                for tau in range(3):
                    vv = P.emit(
                        "vector",
                        (lambda e, r=r, tau=tau, srcbase=srcbase:
                         e.tensor_copy(big[:, 2048 + tau * 64: 2048 + (tau + 1) * 64],
                                       wbf[:, srcbase + r * 192 + tau:
                                           srcbase + (r + 1) * 192:3])),
                        waits=[("XD", GD_L), ("TE", P.n("TE")), ("ST", P.n("ST"))],
                        inc="VE")
                for tau in range(3):
                    si = psn[0] % 2
                    psn[0] += 1
                    ps = pst[si]
                    tv = P.emit("tensor",
                                (lambda e, tau=tau, ps=ps:
                                 e.transpose(ps[0:64, 0:128],
                                             big[:, 2048 + tau * 64: 2048 + (tau + 1) * 64],
                                             cid[:, :])),
                                waits=[("VE", vv), ("VE", pst_ve[si]), ("D0", LDC)],
                                inc="TE")
                    sv = P.emit("vector",
                                (lambda e, ps=ps, r=r, tau=tau, dstT=dstT:
                                 e.tensor_copy(dstT[0:64, (r * 3 + tau) * 64:(r * 3 + tau + 1) * 64],
                                               ps[0:64, 0:64])),
                                waits=[("TE", tv)], inc="VE")
                    sv2 = P.emit("vector",
                                 (lambda e, ps=ps, tau=tau:
                                  e.tensor_copy(big[0:64, 2240 + tau * 64: 2240 + (tau + 1) * 64],
                                                ps[0:64, 64:128])),
                                 waits=[("TE", tv)], inc="VE")
                    pst_ve[si] = sv2
                    P.dma("gpsimd",
                          dstT[64:128, (r * 3 + tau) * 64:(r * 3 + tau + 1) * 64],
                          big[0:64, 2240 + tau * 64: 2240 + (tau + 1) * 64],
                          waits=[("VE", sv2)], sem="ST")
        conv1_prep(8, W_Q1, qc1T)
        conv1_prep(2, W_K1, kc1T)

        qw_ld = [0, 0]
        dq_rd = [0, 0]
        for ot in range(32):
            qsl = ot % 2
            gq = P.dma("gpsimd", qwi[:, qsl * 512:(qsl + 1) * 512],
                       qweight[ot * 128:(ot + 1) * 128, :, :],
                       waits=[("VE", qw_ld[qsl])],
                       sem=("D2" if qsl == 0 else "D3"))
            te_wde = P.n("TE")
            dq = None
            for g in range(4):
                dq = P.emit("vector",
                            (lambda e, g=g, qsl=qsl, ot=ot:
                             e.tensor_scalar(wde[:, qsl * 512 + g * 128:
                                                 qsl * 512 + (g + 1) * 128],
                                             qwi[:, qsl * 512 + g * 128:
                                                 qsl * 512 + (g + 1) * 128],
                                             scz[:, ot * 4 + g: ot * 4 + g + 1],
                                             scz[:, 128 + ot * 4 + g: 129 + ot * 4 + g],
                                             ALU.mult, ALU.add)),
                            waits=[("D2" if qsl == 0 else "D3", gq), ("LD", l2),
                                   ("TE", dq_rd[qsl])],
                            inc="VE")
            qw_ld[qsl] = dq
            aps = [wde[:, qsl * 512 + i * 128: qsl * 512 + (i + 1) * 128]
                   for i in range(4)]
            ps, si, tv = tgroup(aps, [("VE", dq)])
            dq_rd[qsl] = P.n("TE")
            P.emit("vector",
                   (lambda e, ps=ps, ot=ot:
                    e.tensor_copy(big.ap().rearrange("p (b c) -> p b c", c=4096)
                                  [:, 0:4, ot * 128:(ot + 1) * 128],
                                  ps[:, 0:512].rearrange("p (b c) -> p b c", c=128))),
                   waits=[("TE", tv), ("SA", stg_ld[0]), ("SB", stg_ld[1])], inc="VE")
            pst_ve[si] = P.n("VE")
        WTF_VE = P.n("VE")
        if "bigw" in DEBUG_TAPS:
            mktap("bigw", [128, 16384], BF16)
            P.dma("gpsimd", taps["bigw"][:, :], big[:, :],
                  waits=[("VE", WTF_VE)], sem="ST")
        if "wdew" in DEBUG_TAPS:
            mktap("wdew", [128, 1024], BF16)
            P.dma("gpsimd", taps["wdew"][:, :], wde[:, :],
                  waits=[("VE", WTF_VE)], sem="ST")
        for j in range(4):
            P.dma("sync", ppdst(OF + j * 524288, 524288)
                  .rearrange("(p c) -> p c", c=4096),
                  big[:, j * 4096:(j + 1) * 4096],
                  waits=[("VE", WTF_VE)], sem=("SA" if j % 2 == 0 else "SB"))

    P.barrier()
    ccp = P.emit("gpsimd",
                 (lambda e: e.collective_compute(
                     "AllGather", ALU.bypass, replica_groups=groups,
                     ins=[pp_in.ap().opt()], outs=[pp_out.ap().opt()])),
                 waits=[("SA", P.n("SA")), ("SB", P.n("SB")), ("GD", P.n("GD"))],
                 inc="CC")
    CC_P0 = P.n("CC")
    if "ppw" in DEBUG_TAPS:
        mktap("ppw", [8, PSL], BF16)
        P.dma("gpsimd", taps["ppw"][:, :], pp_out[:, :],
              waits=[("CC", CC_P0)], sem="ST")
    if "ppi" in DEBUG_TAPS:
        mktap("ppi", [1, PSL], BF16)
        P.dma("gpsimd", taps["ppi"][:, :], pp_in[:, :],
              waits=[("CC", CC_P0)], sem="ST")
    LD_P0 = P.n("LD")
    GD_P0 = P.n("GD")


    _CTX.update(nc=nc, P=P, sems=sems, es=es, taps=taps)
    with ExitStack() as pa:
        qkvT = pa.enter_context(nc.sbuf_tensor("qkvT", [128, 12 * CL], BF16))

        HL = 1032
        TC3 = [(0, 512), (512, 1024), (1024, HL)]
        with ExitStack() as pa1:
            xnT = pa1.enter_context(nc.sbuf_tensor("xnT", [128, 32 * HL], BF16))
            xbuf = pa1.enter_context(nc.sbuf_tensor("a_x", [128, 2 * D], F32))
            xnt = pa1.enter_context(nc.sbuf_tensor("a_xn", [128, 2 * D], BF16))
            lnst = pa1.enter_context(nc.sbuf_tensor("a_ln", [128, 2 * 64], F32))
            wq = pa1.enter_context(nc.sbuf_tensor("a_wq", [128, 2 * 4096], BF16))
            pstr = [pa1.enter_context(nc.psum_tensor(f"a_tr{i}", [128, 512], BF16))
                    for i in range(2)]
            pstr_ve = [0, 0]
            psq = [pa1.enter_context(nc.psum_tensor(f"a_q{i}", [128, 512], F32))
                   for i in range(6)]

            evq = {}
            wqld = [0, 0]
            seapp = {}
            trdone = {}
            spill_ld = 0
            ve_tile = {}
            for h in range(2):
                for kk9 in range(9):
                    k = h * 9 + kk9
                    rows = 128 if kk9 < 8 else 8
                    r0 = h * 1024 + kk9 * 128
                    slot = k % 2
                    w = [("SE", seapp.get(k - 2, 0))]
                    l1 = P.dma("sync", xbuf[0:rows, slot * D:slot * D + D],
                               x_loc[r0:r0 + rows, :], waits=w,
                               sem=("D0" if slot == 0 else "D1"))
                    vs = None
                    for i in range(8):
                        vs = P.emit("vector",
                                    (lambda e, rows=rows, slot=slot, i=i:
                                     e.bn_stats(lnst[0:rows, slot * 64 + i * 6: slot * 64 + (i + 1) * 6],
                                                xbuf[0:rows, slot * D + i * 512: slot * D + (i + 1) * 512])),
                                    waits=[("D0" if slot == 0 else "D1", l1)], inc="VE")
                    va = P.emit("vector",
                                (lambda e, rows=rows, slot=slot:
                                 e.bn_aggr(lnst[0:rows, slot * 64 + 48: slot * 64 + 50],
                                           lnst[0:rows, slot * 64: slot * 64 + 48])),
                                waits=[("VE", vs)], inc="VE")
                    sq = P.emit("scalar",
                                (lambda e, rows=rows, slot=slot:
                                 e.activation(lnst[0:rows, slot * 64 + 50: slot * 64 + 51],
                                              lnst[0:rows, slot * 64 + 49: slot * 64 + 50],
                                              AF.Sqrt, bias=ceps[0:rows, 0:1], scale=1.0)),
                                waits=[("VE", va), ("VE", VEPS)], inc="SE")
                    vr = P.emit("vector",
                                (lambda e, rows=rows, slot=slot:
                                 e.reciprocal(lnst[0:rows, slot * 64 + 51: slot * 64 + 52],
                                              lnst[0:rows, slot * 64 + 50: slot * 64 + 51])),
                                waits=[("SE", sq)], inc="VE")
                    vn = P.emit("vector",
                                (lambda e, rows=rows, slot=slot:
                                 e.scalar_tensor_tensor(lnst[0:rows, slot * 64 + 52: slot * 64 + 53],
                                                        lnst[0:rows, slot * 64 + 48: slot * 64 + 49],
                                                        -1.0,
                                                        lnst[0:rows, slot * 64 + 51: slot * 64 + 52],
                                                        ALU.mult, ALU.mult)),
                                waits=[("VE", vr)], inc="VE")
                    sa = P.emit("scalar",
                                (lambda e, rows=rows, slot=slot:
                                 e.activation(xnt[0:rows, slot * D: slot * D + D],
                                              xbuf[0:rows, slot * D: slot * D + D],
                                              AF.Identity,
                                              bias=lnst[0:rows, slot * 64 + 52: slot * 64 + 53],
                                              scale=lnst[0:rows, slot * 64 + 51: slot * 64 + 52])),
                                waits=[("VE", vn), ("TE", trdone.get(k - 2, 0))], inc="SE")
                    seapp[k] = sa
                    for g0 in range(0, 32, 4):
                        si = g0 // 4 % 2
                        ps = pstr[si]
                        tv = None
                        for i in range(4):
                            j = g0 + i
                            if rows == 128:
                                tv = P.emit("tensor",
                                            (lambda e, j=j, i=i, ps=ps, slot=slot:
                                             e.transpose(ps[:, i * 128:(i + 1) * 128],
                                                         xnt[:, slot * D + j * 128: slot * D + (j + 1) * 128],
                                                         cid[:, :])),
                                            waits=[("SE", sa), ("VE", pstr_ve[si])], inc="TE")
                            else:
                                tv = P.emit("tensor",
                                            (lambda e, j=j, i=i, ps=ps, slot=slot:
                                             e.transpose(ps[:, i * 8:(i + 1) * 8],
                                                         xnt[0:8, slot * D + j * 128: slot * D + (j + 1) * 128],
                                                         cid[0:8, 0:8])),
                                            waits=[("SE", sa), ("VE", pstr_ve[si])], inc="TE")
                        wdt = 128 if rows == 128 else 8
                        sv = P.emit("vector",
                                    (lambda e, ps=ps, g0=g0, wdt=wdt, kk9=kk9:
                                     e.tensor_copy(
                                         xnT.ap().rearrange("p (j t) -> p j t", t=HL)
                                         [:, g0:g0 + 4, kk9 * 128: kk9 * 128 + wdt],
                                         ps[:, 0:4 * wdt].rearrange("p (b c) -> p b c", c=wdt))),
                                    waits=[("TE", tv), ("LD", spill_ld)], inc="VE")
                        pstr_ve[si] = sv
                    trdone[k] = P.n("TE")
                    ve_tile[kk9] = P.n("VE")
                VE_LN = P.n("VE")

                spill_ld = P.dma(
                    "sync",
                    xnT_d.ap().rearrange("(j p) t -> p j t", p=128)[:, :, h * 1024: h * 1024 + HL],
                    xnT[:, :].rearrange("p (j t) -> p j t", t=HL),
                    waits=[("VE", VE_LN)])

                for r in range(12):
                    slot = r % 2
                    lw = None
                    for rr8 in range(8):
                        lw = P.dma("sync",
                                   wq[:, slot * 4096 + rr8 * 512:
                                      slot * 4096 + (rr8 + 1) * 512]
                                   .rearrange("p (b c) -> p b c", c=128),
                                   pp_out.ap()[rr8, OQ:OQ + 786432]
                                   .rearrange("(b p c) -> p b c", p=128, c=1536)
                                   [:, :, r * 128:(r + 1) * 128],
                                   waits=([("CC", CC_P0), ("TE", wqld[slot])]
                                          if rr8 == 0 else []),
                                   sem=("D2" if slot == 0 else "D3"))
                    for i in range(32):
                        for tci, (c0, c1) in enumerate(TC3):
                            P.emit(
                                "tensor",
                                (lambda e, i=i, tci=tci, c0=c0, c1=c1, slot=slot:
                                 e.matmul(psq[tci][:, 0:c1 - c0],
                                          wq[:, slot * 4096 + i * 128: slot * 4096 + (i + 1) * 128],
                                          xnT[:, i * HL + c0: i * HL + c1],
                                          start=(i == 0), stop=(i == 31))),
                                waits=([("D2" if slot == 0 else "D3", lw),
                                        ("VE", ve_tile[(3, 7, 8)[tci]]),
                                        ("SE", evq.get(tci, 0))]
                                       if i == 0 else []),
                                inc="TE")
                    wqld[slot] = P.n("TE")
                    for tci, (c0, c1) in enumerate(TC3):
                        evq[tci] = P.emit(
                            "scalar",
                            (lambda e, r=r, h=h, tci=tci, c0=c0, c1=c1:
                             e.activation(qkvT[:, r * CL + h * 1024 + c0: r * CL + h * 1024 + c1],
                                          psq[tci][:, 0:c1 - c0], AF.Copy,
                                          bias=0.0, scale=1.0)),
                            waits=[("TE", P.n("TE"))], inc="SE")
        SE_QKV = P.n("SE")
        if "qkvT" in DEBUG_TAPS:
            mktap("qkvT", [1536, CL], BF16)
            P.dma("gpsimd", taps["qkvT"].ap().rearrange("(j p) t -> p j t", p=128),
                  qkvT[:, :].rearrange("p (j t) -> p j t", t=CL),
                  waits=[("SE", SE_QKV)], sem="ST")
        if "xnT" in DEBUG_TAPS:
            mktap("xnT", [D, CL], BF16)
            P.dma("gpsimd", taps["xnT"].ap().rearrange("(j p) t -> p j t", p=128),
                  xnT_d.ap().rearrange("(j p) t -> p j t", p=128),
                  waits=[("SE", SE_QKV)], sem="ST")

        if PHASES < 2:
            raise _StopBuild()
        P.barrier()
        with ExitStack() as pa2:
            qmix = pa2.enter_context(nc.sbuf_tensor("qmix", [128, 8 * TOK], BF16))
            kmix = pa2.enter_context(nc.sbuf_tensor("kmix", [128, 2 * TOK], BF16))
            w2s = pa2.enter_context(nc.sbuf_tensor("w2s", [128, 2 * 24 * 128], BF16))
            oddst = pa2.enter_context(nc.sbuf_tensor("oddst", [64, 512], BF16))
            pa2a = pa2.enter_context(ExitStack())
            c1q = pa2a.enter_context(nc.sbuf_tensor("c1q", [128, 8 * C1L], BF16))
            c1k = pa2a.enter_context(nc.sbuf_tensor("c1k", [128, 2 * C1L], BF16))
            ps1 = [pa2.enter_context(nc.psum_tensor(f"a2_p{i}", [128, 512], F32))
                   for i in range(6)]
            ps1_ev = [("SE", 0)] * 6

            def conv1(nh2, srcT, src_t0, dstT, wT, biasT):
                for r in range(nh2):
                    for tci, (c0, c1) in enumerate(TC5C):
                        wdt = c1 - c0
                        for hh in range(2):
                            pi = (r * 2 + hh) % 4
                            ps = ps1[pi]
                            base = 64 * hh
                            tv = None
                            for tau in range(3):
                                tv = P.emit(
                                    "tensor",
                                    (lambda e, r=r, hh=hh, tau=tau, c0=c0, wdt=wdt,
                                     ps=ps, base=base, srcT=srcT, src_t0=src_t0, wT=wT:
                                     e.matmul(ps[0:64, 0:wdt],
                                              wT[base:base + 64,
                                                 (r * 3 + tau) * 64:(r * 3 + tau + 1) * 64],
                                              srcT[base:base + 64,
                                                   src_t0 + r * CL + 4 + c0 + tau:
                                                   src_t0 + r * CL + 4 + c0 + tau + wdt],
                                              start=(tau == 0), stop=(tau == 2))),
                                    waits=([("SE", SE_QKV), ps1_ev[pi]]
                                           if tau == 0 else []),
                                    inc="TE")
                            h = 2 * r + hh
                            if hh == 0:
                                ev = P.emit(
                                    "scalar",
                                    (lambda e, r=r, c0=c0, wdt=wdt, ps=ps, h=h,
                                     dstT=dstT, biasT=biasT:
                                     e.activation(dstT[0:64, r * C1L + c0: r * C1L + c0 + wdt],
                                                  ps[0:64, 0:wdt], AF.Identity,
                                                  bias=biasT[0:64, h:h + 1], scale=1.0)),
                                    waits=[("TE", tv)], inc="SE")
                                ps1_ev[pi] = ("SE", ev)
                            else:
                                ev = P.emit(
                                    "scalar",
                                    (lambda e, c0=c0, wdt=wdt, ps=ps, h=h, biasT=biasT:
                                     e.activation(oddst[0:64, 0:wdt],
                                                  ps[0:64, 0:wdt], AF.Identity,
                                                  bias=biasT[0:64, h:h + 1], scale=1.0)),
                                    waits=[("TE", tv), ("GD", P.n("GD"))], inc="SE")
                                ps1_ev[pi] = ("SE", ev)
                                P.dma("gpsimd",
                                      dstT[64:128, r * C1L + c0: r * C1L + c0 + wdt],
                                      oddst[0:64, 0:wdt], waits=[("SE", ev)])
            conv1(8, qkvT, 0, c1q, qc1T, cq1b)
            conv1(2, qkvT, 8 * CL, c1k, kc1T, ck1b)
            GD_C1 = P.n("GD")
            hm = []
            for t, nt in [(c1q, 8), (c1k, 2)]:
                for r in range(nt):
                    hm.append(P.emit(
                        "vector",
                        (lambda e, t=t, r=r:
                         e.tensor_scalar_mul(t[:, r * C1L: r * C1L + 2],
                                             t[:, r * C1L: r * C1L + 2],
                                             chmask[:, 0:1])),
                        waits=[("SE", P.n("SE")), ("GD", GD_C1)], inc="VE"))
            VE_HM = P.n("VE")

            w2ld = [0, 0]

            def conv2(n_out, n_in, srcC, dstT, w2d, biasT, mean_l, mean_src_t0, mean_n):
                for o in range(n_out):
                    slot = o % 2
                    if w2d is None:
                        lw = None
                        for tau3 in range(3):
                            lw = P.dma(
                                "sync",
                                w2s[:, slot * 3072 + tau3 * n_in * 128:
                                    slot * 3072 + (tau3 + 1) * n_in * 128]
                                .rearrange("p (j c) -> p j c", c=128),
                                pp_out.ap()[:, OC2 + tau3 * 131072:
                                            OC2 + (tau3 + 1) * 131072]
                                .rearrange("r (p c) -> p r c", p=128, c=1024)
                                [:, :, o * 128:(o + 1) * 128],
                                waits=([("CC", CC_P0), ("TE", w2ld[slot])]
                                       if tau3 == 0 else []),
                                sem=("D0" if slot == 0 else "D1"))
                    else:
                        lw = P.dma("sync",
                                   w2s[:, slot * 3072: slot * 3072 + 3 * n_in * 128]
                                   .rearrange("p (b c) -> p b c", c=128),
                                   w2d.ap()[:, :, o * 128:(o + 1) * 128]
                                   .rearrange("t (j p) c -> p (t j) c", p=128),
                                   waits=[("CC", CC_P0), ("TE", w2ld[slot])],
                                   sem=("D0" if slot == 0 else "D1"))
                    for tci, (c0, c1) in enumerate(TC4):
                        wdt = c1 - c0
                        pi = 4 + (o * 4 + tci) % 2
                        ps = ps1[pi]
                        tv = None
                        first = True
                        for tau in range(3):
                            for j in range(n_in):
                                tv = P.emit(
                                    "tensor",
                                    (lambda e, tau=tau, j=j, o=o, c0=c0, wdt=wdt, ps=ps,
                                     slot=slot, srcC=srcC, n_in=n_in:
                                     e.matmul(ps[:, 0:wdt],
                                              w2s[:, slot * 3072 + (tau * n_in + j) * 128:
                                                  slot * 3072 + (tau * n_in + j + 1) * 128],
                                              srcC[:, j * C1L + c0 + tau: j * C1L + c0 + tau + wdt],
                                              start=(tau == 0 and j == 0), stop=False)),
                                    waits=([("D0" if slot == 0 else "D1", lw),
                                            ("VE", VE_HM), ps1_ev[pi]]
                                           if first else []),
                                    inc="TE")
                                first = False
                        for mi in range(mean_n):
                            tv = P.emit(
                                "tensor",
                                (lambda e, mi=mi, c0=c0, wdt=wdt, ps=ps,
                                 mean_l=mean_l, mean_src_t0=mean_src_t0:
                                 e.matmul(ps[:, 0:wdt],
                                          mean_l[:, :],
                                          qkvT[:, mean_src_t0 + mi * CL + 8 + c0:
                                               mean_src_t0 + mi * CL + 8 + c0 + wdt],
                                          start=False, stop=(mi == mean_n - 1))),
                                inc="TE")
                        ev = P.emit(
                            "scalar",
                            (lambda e, o=o, c0=c0, wdt=wdt, ps=ps, dstT=dstT, biasT=biasT:
                             e.activation(dstT[:, o * TOK + c0: o * TOK + c0 + wdt],
                                          ps[:, 0:wdt], AF.Identity,
                                          bias=biasT[:, o:o + 1], scale=1.0)),
                            waits=[("TE", tv)], inc="SE")
                        ps1_ev[pi] = ("SE", ev)
                    w2ld[slot] = P.n("TE")
            conv2(8, 8, c1q, qmix, None, cq2b, cmk, 8 * CL, 2)
            conv2(2, 2, c1k, kmix, kc2T_d, ck2b, cmq, 0, 8)
            SE_C2 = P.n("SE")
            pa2a.close()
            P.barrier()
            qhatT = pa2.enter_context(nc.sbuf_tensor("qhatT", [128, 8 * TOK], BF16))
            khatT = pa2.enter_context(nc.sbuf_tensor("khatT", [128, 2 * TOK], BF16))
            vstage = pa2.enter_context(nc.sbuf_tensor("vstage", [128, 4 * 16 * 65], BF16))
            sqs = pa2.enter_context(nc.sbuf_tensor("sqs", [128, 1024], F32))
            rsd = pa2.enter_context(nc.sbuf_tensor("rsd", [2, 2 * TOK], F32))
            rbb = pa2.enter_context(nc.sbuf_tensor("rbb", [128, 2 * 512], BF16))
            rtb = pa2.enter_context(nc.sbuf_tensor("rtb", [128, 2 * 512], BF16))
            tm1 = pa2.enter_context(nc.sbuf_tensor("tm1", [128, 2 * 512], BF16))
            if "qmix" in DEBUG_TAPS:
                mktap("qmix", [LAT, TOK], BF16)
                P.dma("gpsimd", taps["qmix"].ap().rearrange("(j p) t -> p j t", p=128),
                      qmix[:, :].rearrange("p (j t) -> p j t", t=TOK),
                      waits=[("SE", SE_C2)], sem="ST")

            def heads(n_t, srcM, dstT, is_k):
                for r in range(n_t):
                    rr = r % 2
                    for tci, (c0, c1) in enumerate(TC4):
                        sl2 = tci % 2
                        sqv = P.emit("scalar",
                                     (lambda e, r=r, c0=c0, c1=c1, sl2=sl2, srcM=srcM:
                                      e.activation(sqs[:, sl2 * 512:(sl2 + 1) * 512],
                                                   srcM[:, r * TOK + c0: r * TOK + c1],
                                                   AF.Square, bias=0.0, scale=1.0)),
                                     waits=[("SE", SE_C2), ("TE", P.n("TE"))], inc="SE")
                        pi = tci % 2
                        ps = ps1[pi]
                        tv = P.emit("tensor",
                                    (lambda e, c0=c0, c1=c1, ps=ps, sl2=sl2:
                                     e.matmul(ps[0:2, 0:c1 - c0], cbl1[:, :],
                                              sqs[:, sl2 * 512:(sl2 + 1) * 512],
                                              start=True, stop=True)),
                                    waits=[("SE", sqv), ps1_ev[pi]], inc="TE")
                        sn = P.emit("scalar",
                                    (lambda e, c0=c0, c1=c1, ps=ps, rr=rr:
                                     e.activation(rsd[0:2, rr * TOK + c0: rr * TOK + c1],
                                                  ps[0:2, 0:c1 - c0],
                                                  AF.Sqrt, bias=0.0, scale=1.0)),
                                    waits=[("TE", tv)], inc="SE")
                        ps1_ev[pi] = ("SE", sn)
                    snv = P.n("SE")
                    vm = P.emit("vector",
                                (lambda e, rr=rr:
                                 e.tensor_scalar_max(rsd[0:2, rr * TOK:(rr + 1) * TOK],
                                                     rsd[0:2, rr * TOK:(rr + 1) * TOK], 1e-12)),
                                waits=[("SE", snv)], inc="VE")
                    vrc = P.emit("vector",
                                 (lambda e, rr=rr:
                                  e.reciprocal(rsd[0:2, rr * TOK:(rr + 1) * TOK],
                                               rsd[0:2, rr * TOK:(rr + 1) * TOK])),
                                 waits=[("VE", vm)], inc="VE")
                    if is_k:
                        vrc = P.emit("vector",
                                     (lambda e, rr=rr:
                                      e.tensor_scalar_mul(rsd[0:2, rr * TOK:(rr + 1) * TOK],
                                                          rsd[0:2, rr * TOK:(rr + 1) * TOK],
                                                          ctmp8[0:2, 0:1])),
                                     waits=[("VE", vrc)], inc="VE")
                    for tci, (c0, c1) in enumerate(TC4):
                        wdt = c1 - c0
                        pib = 2 + tci % 2
                        psb = ps1[pib]
                        tb = P.emit("tensor",
                                    (lambda e, rr=rr, c0=c0, c1=c1, psb=psb:
                                     e.matmul(psb[:, 0:c1 - c0], cb2[:, :],
                                              rsd[0:2, rr * TOK + c0: rr * TOK + c1],
                                              start=True, stop=True)),
                                    waits=[("VE", vrc), ps1_ev[pib]], inc="TE")
                        sl = tci % 2
                        vb = P.emit("vector",
                                    (lambda e, psb=psb, sl=sl, wdt=wdt:
                                     e.tensor_copy(rbb[:, sl * 512: sl * 512 + wdt],
                                                   psb[:, 0:wdt])),
                                    waits=[("TE", tb)], inc="VE")
                        ps1_ev[pib] = ("VE", vb)
                        vq = P.emit("vector",
                                    (lambda e, r=r, c0=c0, wdt=wdt, sl=sl, srcM=srcM:
                                     e.tensor_mul(tm1[:, sl * 512: sl * 512 + wdt],
                                                  srcM[:, r * TOK + c0: r * TOK + c0 + wdt],
                                                  rbb[:, sl * 512: sl * 512 + wdt])),
                                    waits=[("VE", vb)], inc="VE")
                        pir = 4 + tci % 2
                        psr = ps1[pir]
                        tr = P.emit("tensor",
                                    (lambda e, sl=sl, wdt=wdt, psr=psr:
                                     e.matmul(psr[:, 0:wdt], crot[:, :],
                                              tm1[:, sl * 512: sl * 512 + wdt],
                                              start=True, stop=True)),
                                    waits=[("VE", vq), ps1_ev[pir]], inc="TE")
                        vrt = P.emit("vector",
                                     (lambda e, sl=sl, wdt=wdt, psr=psr:
                                      e.tensor_copy(rtb[:, sl * 512: sl * 512 + wdt],
                                                    psr[:, 0:wdt])),
                                     waits=[("TE", tr)], inc="VE")
                        ps1_ev[pir] = ("VE", vrt)
                        v1 = P.emit("vector",
                                    (lambda e, sl=sl, wdt=wdt, c0=c0:
                                     e.tensor_mul(tm1[:, sl * 512: sl * 512 + wdt],
                                                  tm1[:, sl * 512: sl * 512 + wdt],
                                                  ccos[:, c0:c0 + wdt])),
                                    waits=[("VE", vrt)], inc="VE")
                        v2 = P.emit("vector",
                                    (lambda e, sl=sl, wdt=wdt, c0=c0:
                                     e.tensor_mul(rtb[:, sl * 512: sl * 512 + wdt],
                                                  rtb[:, sl * 512: sl * 512 + wdt],
                                                  csin[:, c0:c0 + wdt])),
                                    inc="VE")
                        v3 = P.emit("vector",
                                    (lambda e, r=r, sl=sl, wdt=wdt, c0=c0, dstT=dstT:
                                     e.tensor_add(dstT[:, r * TOK + c0: r * TOK + c0 + wdt],
                                                  tm1[:, sl * 512: sl * 512 + wdt],
                                                  rtb[:, sl * 512: sl * 512 + wdt])),
                                    inc="VE")
            heads(8, qmix, qhatT, False)
            heads(2, kmix, khatT, True)
            VE_HEAD = P.n("VE")

            psv = [pa2.enter_context(nc.psum_tensor(f"a2_pv{i}", [128, 128], BF16))
                   for i in range(2)]
            psv_ve = [0, 0]
            for tt in range(16):
                for vt in range(2):
                    si = (tt * 2 + vt) % 2
                    ps = psv[si]
                    shift = 0 if vt == 0 else -1
                    tv = P.emit("tensor",
                                (lambda e, tt=tt, vt=vt, ps=ps, shift=shift:
                                 e.transpose(ps[:, :],
                                             qkvT[:, (10 + vt) * CL + 8 + tt * 128 + shift:
                                                  (10 + vt) * CL + 8 + (tt + 1) * 128 + shift],
                                             cid[:, :])),
                                waits=[("SE", SE_QKV), ("VE", psv_ve[si])], inc="TE")
                    for kvh in range(2):
                        k4 = vt * 2 + kvh
                        vv = P.emit("vector",
                                    (lambda e, tt=tt, ps=ps, kvh=kvh, k4=k4:
                                     e.tensor_copy(
                                         vstage[:, k4 * 1040 + tt * 65: k4 * 1040 + tt * 65 + 64],
                                         ps[:, kvh * 64:(kvh + 1) * 64])),
                                    waits=[("TE", tv)], inc="VE")
                    psv_ve[si] = P.n("VE")
            vo = None
            for k4 in range(4):
                for tt in range(16):
                    vo = P.emit("vector",
                                (lambda e, k4=k4, tt=tt:
                                 e.memset(vstage[:, k4 * 1040 + tt * 65 + 64:
                                                 k4 * 1040 + tt * 65 + 65], 1.0)),
                                inc="VE")
            VE_V = P.n("VE")

            for j8 in range(8):
                k4 = j8 % 4
                for t2 in range(2):
                    P.dma("gpsimd",
                          a1_in.ap()[j8, OFF_Q + t2 * 128 * TOK: OFF_Q + (t2 + 1) * 128 * TOK]
                          .rearrange("(p t) -> p t", p=128),
                          qhatT[:, (2 * k4 + t2) * TOK:(2 * k4 + t2 + 1) * TOK],
                          waits=[("VE", VE_HEAD)], sem="GD")
                P.dma("gpsimd",
                      a1_in.ap()[j8, OFF_K: OFF_K + 64 * TOK].rearrange("(p t) -> p t", p=64),
                      khatT[64 * (k4 % 2): 64 * (k4 % 2) + 64,
                            (k4 // 2) * TOK:(k4 // 2 + 1) * TOK],
                      waits=[("VE", VE_HEAD)], sem="GD")
                P.dma("gpsimd",
                      a1_in.ap()[j8, OFF_V: OFF_V + TOK * 65]
                      .rearrange("(tt p c) -> p tt c", p=128, c=65),
                      vstage[:, k4 * 1040:(k4 + 1) * 1040]
                      .rearrange("p (tt c) -> p tt c", c=65),
                      waits=[("VE", VE_V)], sem="GD")
            GD_A1 = P.n("GD")

    if PHASES < 3:
        raise _StopBuild()
    cc1 = P.emit("gpsimd",
                 (lambda e: e.collective_compute(
                     "AllToAll", ALU.bypass, replica_groups=groups,
                     ins=[a1_in.ap().opt()], outs=[a1_out.ap().opt()])),
                 waits=[("GD", P.n("GD"))], inc="CC")
    P.barrier()

    with ExitStack() as pb:
        qB = pb.enter_context(nc.sbuf_tensor("qB", [128, 2 * S], BF16))
        Ksb = pb.enter_context(nc.sbuf_tensor("Ksb", [128, S], BF16))
        Vsb = pb.enter_context(nc.sbuf_tensor("Vsb", [128, 64 * 65], BF16))
        probs = pb.enter_context(nc.sbuf_tensor("probs", [128, 4 * 1024], BF16))
        attnL = pb.enter_context(nc.sbuf_tensor("attnL", [128, 2 * S], BF16))
        rcs = pb.enter_context(nc.sbuf_tensor("rcs", [1, 512], F32))
        bcs = pb.enter_context(nc.sbuf_tensor("bcs", [64, 512], F32))
        cones1b = pb.enter_context(nc.sbuf_tensor("cones1b", [128, 64], BF16))
        pss = [pb.enter_context(nc.psum_tensor(f"b_s{i}", [128, 1024], F32))
               for i in range(2)]
        psa = [[pb.enter_context(nc.psum_tensor(f"b_a{i}{h}", [128, 512], F32))
                for h in range(2)] for i in range(2)]

        pbsh = ExitStack()
        qB2 = pbsh.enter_context(nc.sbuf_tensor("qB2", [128, 2 * S], BF16))
        Ksb2 = pbsh.enter_context(nc.sbuf_tensor("Ksb2", [128, S], BF16))
        Vsb2 = pbsh.enter_context(nc.sbuf_tensor("Vsb2", [128, 64 * 65], BF16))
        for dst, dst2, src in [(qB, qB2, 0), (Ksb, Ksb2, 1), (Vsb, Vsb2, 2)]:
            for half, d in [(0, dst), (1, dst2)]:
                for i in range(4):
                    blk = half * 4 + i
                    if src == 0:
                        for t2 in range(2):
                            for hh in range(2):
                                P.dma("sync",
                                      d[64 * hh:64 * hh + 64,
                                        t2 * S + i * TOK: t2 * S + (i + 1) * TOK],
                                      a1_out.ap()[blk, OFF_Q + (t2 * 2 + hh) * 64 * TOK:
                                                  OFF_Q + (t2 * 2 + hh + 1) * 64 * TOK]
                                      .rearrange("(p t) -> p t", p=64),
                                      waits=[("CC", cc1)])
                    elif src == 1:
                        for hh in range(2):
                            P.dma("sync",
                                  d[64 * hh:64 * hh + 64, i * TOK:(i + 1) * TOK],
                                  a1_out.ap()[blk, OFF_K: OFF_K + 64 * TOK]
                                  .rearrange("(p t) -> p t", p=64),
                                  waits=[("CC", cc1)])
                    else:
                        P.dma("sync",
                              d[:, i * 16 * 65:(i + 1) * 16 * 65]
                              .rearrange("p (tt c) -> p tt c", c=65),
                              a1_out.ap()[blk, OFF_V: OFF_V + TOK * 65]
                              .rearrange("(tt p c) -> p tt c", p=128, c=65),
                              waits=[("CC", cc1)])
        LD_B0 = P.n("LD")
        selv = 0
        for d, d2, w in [(qB, qB2, 2 * S), (Ksb, Ksb2, S), (Vsb, Vsb2, 64 * 65)]:
            P.emit("vector",
                   (lambda e, d=d, w=w:
                    e.tensor_scalar_mul(d[:, 0:w], d[:, 0:w], cbsel[:, 0:1])),
                   waits=[("LD", LD_B0)], inc="VE")
            selv = P.emit("vector",
                          (lambda e, d=d, d2=d2, w=w:
                           e.scalar_tensor_tensor(d[:, 0:w], d2[:, 0:w],
                                                  cbsel[:, 1:2], d[:, 0:w],
                                                  ALU.mult, ALU.add)),
                          inc="VE")
        LD_B = LD_B0

        pbsh.close()
        avstage = pb.enter_context(nc.sbuf_tensor("avstage", [128, 64 * 512], BF16))
        dtile = pb.enter_context(nc.sbuf_tensor("dtile", [128, 256], F32))
        P.emit("vector", lambda e: e.memset(cones1b[64:65, :], 1.0), inc="VE")
        CONB = P.n("VE")

        exp_done = [0, 0]
        probs_rd = [[0, 0], [0, 0]]
        av_done = [0, 0]
        for qc in range(16):
            n_kt = 4 * qc + 4
            pready = [None, None]
            for kt in range(n_kt + 1):
                pready_new = [None, None]
                if kt < n_kt:
                    diag = kt >= n_kt - 4
                    buf = kt % 2
                    for pair in range(2):
                        ps = pss[pair]
                        pidx = pair * 2 + buf
                        tv = None
                        for hh in range(2):
                            tv = P.emit(
                                "tensor",
                                (lambda e, pair=pair, qc=qc, kt=kt, hh=hh, ps=ps:
                                 e.matmul(ps[:, hh * 512:(hh + 1) * 512],
                                          Ksb[64 * hh:64 * hh + 64, kt * 128:(kt + 1) * 128],
                                          qB[64 * hh:64 * hh + 64,
                                             pair * S + qc * 512:pair * S + (qc + 1) * 512],
                                          start=True, stop=True)),
                                waits=([("LD", LD_B), ("VE", selv),
                                        ("SE", exp_done[pair])]
                                       if hh == 0 else []),
                                inc="TE")
                        ex = P.emit(
                            "scalar",
                            (lambda e, ps=ps, pidx=pidx:
                             e.activation(probs[:, pidx * 1024:(pidx + 1) * 1024],
                                          ps[:, :], AF.Exp,
                                          bias=cnbias[:, 0:1], scale=1.0)),
                            waits=[("TE", tv), ("TE", probs_rd[pair][buf])], inc="SE")
                        exp_done[pair] = ex
                        if diag:
                            v = kt - (n_kt - 4)
                            mv = 0
                            for hh in range(2):
                                mv = P.emit(
                                    "vector",
                                    (lambda e, pidx=pidx, hh=hh, v=v:
                                     e.tensor_mul(probs[:, pidx * 1024 + hh * 512:
                                                        pidx * 1024 + (hh + 1) * 512],
                                                  probs[:, pidx * 1024 + hh * 512:
                                                        pidx * 1024 + (hh + 1) * 512],
                                                  cmask01[:, v * 512:(v + 1) * 512])),
                                    waits=[("SE", ex)], inc="VE")
                            pready_new[pair] = ("VE", mv)
                        else:
                            pready_new[pair] = ("SE", ex)
                if kt >= 1:
                    akt = kt - 1
                    abuf = akt % 2
                    for pair in range(2):
                        pidx = pair * 2 + abuf
                        pr = pready[pair]
                        for hh in range(2):
                            P.emit(
                                "tensor",
                                (lambda e, akt=akt, hh=hh, pair=pair, pidx=pidx, n_kt=n_kt:
                                 e.matmul(psa[pair][hh][0:65, :],
                                          Vsb[:, akt * 65:(akt + 1) * 65],
                                          probs[:, pidx * 1024 + hh * 512:
                                                pidx * 1024 + (hh + 1) * 512],
                                          start=(akt == 0), stop=(akt == n_kt - 1))),
                                waits=([pr, ("VE", av_done[pair])]
                                       if akt == 0 else [pr]),
                                inc="TE")
                        probs_rd[pair][abuf] = P.n("TE")
                if kt < n_kt:
                    pready = pready_new
            TE_AV = P.n("TE")
            for pair in range(2):
                for hh in range(2):
                    t4 = qc * 4 + pair * 2 + hh
                    av_done[pair] = P.emit(
                        "vector",
                        (lambda e, pair=pair, hh=hh, t4=t4:
                         e.tensor_copy(avstage[0:65, t4 * 512:(t4 + 1) * 512],
                                       psa[pair][hh][0:65, :])),
                        waits=[("TE", TE_AV)], inc="VE")
        VE_EV = P.n("VE")
        P.barrier()

        dge = P.emit(
            "gpsimd",
            (lambda e: e.dma_start(out=dn_d[:], in_=avstage[64:65, :])),
            waits=[("VE", VE_EV)], inc="GD", by=16)
        dge2 = P.emit(
            "gpsimd",
            (lambda e: e.dma_start(out=dtile[:, :],
                                   in_=dn_d.ap().rearrange("(p c) -> p c", c=256))),
            waits=[("GD", P.n("GD"))], inc="GD", by=16)
        vrc = P.emit("vector",
                     (lambda e: e.reciprocal(dtile[:, :], dtile[:, :])),
                     waits=[("GD", P.n("GD"))], inc="VE")
        dge3 = P.emit(
            "gpsimd",
            (lambda e: e.dma_start(out=dn_d.ap().rearrange("(p c) -> p c", c=256),
                                   in_=dtile[:, :])),
            waits=[("VE", vrc)], inc="GD", by=16)
        dge4 = P.emit(
            "gpsimd",
            (lambda e: e.dma_start(out=avstage[64:65, :], in_=dn_d[:])),
            waits=[("GD", P.n("GD"))], inc="GD", by=16)
        GD_R = P.n("GD")
        bc_ve = [0, 0]
        for t4 in range(64):
            qc, rem = divmod(t4, 4)
            pair, hh = divmod(rem, 2)
            tb = P.emit("tensor",
                        (lambda e, pair=pair, t4=t4:
                         e.matmul(pss[pair][0:64, 0:512], cones1b[64:65, :],
                                  avstage[64:65, t4 * 512:(t4 + 1) * 512],
                                  start=True, stop=True)),
                        waits=[("GD", GD_R), ("SE", exp_done[pair]),
                               ("VE", max(bc_ve[pair], CONB))], inc="TE")
            vb = P.emit("vector",
                        (lambda e, pair=pair:
                         e.tensor_copy(bcs[:, :], pss[pair][0:64, 0:512])),
                        waits=[("TE", tb)], inc="VE")
            P.emit("vector",
                   (lambda e, pair=pair, qc=qc, hh=hh, t4=t4:
                    e.tensor_mul(attnL[64 * hh:64 * hh + 64,
                                       pair * S + qc * 512:pair * S + (qc + 1) * 512],
                                 avstage[0:64, t4 * 512:(t4 + 1) * 512],
                                 bcs[:, :])),
                   inc="VE")
            bc_ve[pair] = P.n("VE")
        P.barrier()
        VE_B = P.n("VE")
        if "attnL" in DEBUG_TAPS:
            mktap("attnL", [256, S], BF16)
            P.dma("gpsimd", taps["attnL"].ap().rearrange("(j p) t -> p j t", p=128),
                  attnL[:, :].rearrange("p (j t) -> p j t", t=S),
                  waits=[("VE", VE_B)], sem="ST")

        for j8 in range(8):
            k4 = j8 % 4
            for t2 in range(2):
                for hh in range(2):
                    P.dma("gpsimd",
                          a2_in.ap()[j8, (t2 * 2 + hh) * 64 * TOK:
                                     (t2 * 2 + hh + 1) * 64 * TOK]
                          .rearrange("(p t) -> p t", p=64),
                          attnL[64 * hh:64 * hh + 64,
                                t2 * S + k4 * TOK: t2 * S + (k4 + 1) * TOK],
                          waits=[("VE", VE_B)], sem="GD")

    GD_A2C = P.n("GD")
    P.barrier()
    cc2 = P.emit("gpsimd",
                 (lambda e: e.collective_compute(
                     "AllToAll", ALU.bypass, replica_groups=groups,
                     ins=[a2_in.ap().opt()], outs=[a2_out.ap().opt()])),
                 waits=[("GD", GD_A2C)], inc="CC")
    if "xnTd" in DEBUG_TAPS:
        mktap("xnTd", [D, CL], BF16)
        P.dma("gpsimd", taps["xnTd"][:, :], xnT_d[:, :],
              waits=[("CC", P.n("CC"))], sem="ST")
    if "a2o" in DEBUG_TAPS:
        mktap("a2o", [8, SH2_W], BF16)
        P.dma("gpsimd", taps["a2o"][:, :], a2_out[:, :],
              waits=[("CC", P.n("CC"))], sem="ST")

    if PHASES < 4:
        raise _StopBuild()
    with ExitStack() as pc:
        xnC = pc.enter_context(nc.sbuf_tensor("xnC", [128, 32 * 1024], BF16))
        afH = pc.enter_context(nc.sbuf_tensor("afH", [128, 8 * 1024], BF16))
        afH2 = pc.enter_context(nc.sbuf_tensor("afH2", [128, 8 * 1024], BF16))
        wts = pc.enter_context(nc.sbuf_tensor("wts", [128, 2 * 32 * 128], BF16))
        wos = pc.enter_context(nc.sbuf_tensor("wos", [128, 2 * 8 * 128], BF16))
        gsb = pc.enter_context(nc.sbuf_tensor("gsb", [128, 1024], F32))
        gstage = pc.enter_context(nc.sbuf_tensor("gstage", [128, 8 * 1024], BF16))
        sgs = pc.enter_context(nc.sbuf_tensor("sgs", [128, 4 * 1024], F32))
        xcs = pc.enter_context(nc.sbuf_tensor("xcs", [128, 2 * 512], F32))
        osb = pc.enter_context(nc.sbuf_tensor("osb", [128, 2 * 512], F32))
        psg = [pc.enter_context(nc.psum_tensor(f"c_g{i}", [128, 512], F32)) for i in range(2)]
        pso = [pc.enter_context(nc.psum_tensor(f"c_s{i}", [128, 512], F32)) for i in range(2)]
        pstp = [pc.enter_context(nc.psum_tensor(f"c_t{i}", [128, 512], F32)) for i in range(2)]

        psg_ev = [0, 0]
        pso_ev = [0, 0]
        gsb_ev = [0, 0]
        pst_ev = [0, 0]
        xcs_ev = [0, 0]
        ost_st = [0, 0]
        wldt = [0, 0]
        wldo = [0, 0]
        GPRE = 8
        te_prevH = 0
        for H in range(2):
            lh = P.dma("sync", xnC[:, :].rearrange("p (j t) -> p j t", t=1024),
                       xnT_d.ap().rearrange("(j p) t -> p j t", p=128)[:, :, 8 + H * 1024:
                                                                      8 + (H + 1) * 1024],
                       waits=[("TE", te_prevH)])
            for ot in range(GPRE):
                slot = ot % 2
                dsl = "D0" if slot == 0 else "D1"
                lw = None
                for rr8 in range(8):
                    lw = P.dma("sync",
                               wts[:, slot * 4096 + rr8 * 512:
                                   slot * 4096 + (rr8 + 1) * 512]
                               .rearrange("p (b c) -> p b c", c=128),
                               pp_out.ap()[rr8, OF:OF + 2097152]
                               .rearrange("(b p c) -> p b c", p=128, c=4096)
                               [:, :, ot * 128:(ot + 1) * 128],
                               waits=([("TE", wldt[slot])] if rr8 == 0 else []),
                               sem=dsl)
                for tc in range(2):
                    ps = psg[tc]
                    tv = None
                    for j in range(32):
                        tv = P.emit(
                            "tensor",
                            (lambda e, j=j, tc=tc, ps=ps, slot=slot:
                             e.matmul(ps[:, :],
                                      wts[:, slot * 4096 + j * 128: slot * 4096 + (j + 1) * 128],
                                      xnC[:, j * 1024 + tc * 512: j * 1024 + (tc + 1) * 512],
                                      start=(j == 0), stop=(j == 31))),
                            waits=([(dsl, lw), ("LD", lh), ("SE", psg_ev[tc])]
                                   if j == 0 else []),
                            inc="TE")
                    sg_ = P.emit(
                        "scalar",
                        (lambda e, ot=ot, tc=tc, ps=ps:
                         e.activation(gsb[:, tc * 512:(tc + 1) * 512], ps[:, :],
                                      AF.Silu, bias=cfb[:, ot:ot + 1], scale=1.0)),
                        waits=[("TE", tv), ("VE", gsb_ev[tc])], inc="SE")
                    psg_ev[tc] = sg_
                    gsb_ev[tc] = P.emit(
                        "vector",
                        (lambda e, ot=ot, tc=tc:
                         e.tensor_copy(gstage[:, ot * 1024 + tc * 512:
                                              ot * 1024 + (tc + 1) * 512],
                                       gsb[:, tc * 512:(tc + 1) * 512])),
                        waits=[("SE", sg_)], inc="VE")
                wldt[slot] = P.n("TE")
            for half, d in [(0, afH), (1, afH2)]:
                for i in range(4):
                    for ci in range(2):
                        P.dma("gpsimd",
                              d[:, (2 * i + ci) * 1024:(2 * i + ci + 1) * 1024],
                              a2_out.ap().rearrange("r (c p t) -> r c p t", p=128, t=TOK)
                              [half * 4 + i, ci, :, H * 1024:(H + 1) * 1024],
                              waits=[("CC", cc2), ("TE", te_prevH)])
            LD_C0 = P.n("GD")
            P.emit("vector",
                   (lambda e: e.tensor_scalar_mul(afH[:, :], afH[:, :], cbsel[:, 0:1])),
                   waits=[("GD", LD_C0), ("TE", te_prevH)], inc="VE")
            afsel = P.emit("vector",
                           (lambda e: e.scalar_tensor_tensor(afH[:, :], afH2[:, :],
                                                             cbsel[:, 1:2], afH[:, :],
                                                             ALU.mult, ALU.add)),
                           inc="VE")
            afl = LD_C0
            for ot in range(32):
                slot = ot % 2
                dsl = "D0" if slot == 0 else "D1"
                wsl = "W0" if slot == 0 else "W1"
                if ot >= GPRE:
                    lw = None
                    for rr8 in range(8):
                        lw = P.dma("sync",
                                   wts[:, slot * 4096 + rr8 * 512:
                                       slot * 4096 + (rr8 + 1) * 512]
                                   .rearrange("p (b c) -> p b c", c=128),
                                   pp_out.ap()[rr8, OF:OF + 2097152]
                                   .rearrange("(b p c) -> p b c", p=128, c=4096)
                                   [:, :, ot * 128:(ot + 1) * 128],
                                   waits=([("TE", wldt[slot])] if rr8 == 0 else []),
                                   sem=dsl)
                lwo = P.dma("sync",
                            wos[:, slot * 1024:(slot + 1) * 1024]
                            .rearrange("p (j c) -> p j c", c=128),
                            pp_out.ap()[:, OW:OW + 524288]
                            .rearrange("r (p c) -> p r c", p=128, c=4096)
                            [:, :, ot * 128:(ot + 1) * 128],
                            waits=[("TE", wldo[slot])], sem=wsl)
                for tc in range(2):
                    if ot >= GPRE:
                        ps = psg[tc]
                        tv = None
                        for j in range(32):
                            tv = P.emit(
                                "tensor",
                                (lambda e, j=j, tc=tc, ps=ps, slot=slot:
                                 e.matmul(ps[:, :],
                                          wts[:, slot * 4096 + j * 128: slot * 4096 + (j + 1) * 128],
                                          xnC[:, j * 1024 + tc * 512: j * 1024 + (tc + 1) * 512],
                                          start=(j == 0), stop=(j == 31))),
                                waits=([(dsl, lw), ("LD", lh), ("SE", psg_ev[tc])]
                                       if j == 0 else []),
                                inc="TE")
                        sg_ = P.emit(
                            "scalar",
                            (lambda e, ot=ot, tc=tc, ps=ps:
                             e.activation(gsb[:, tc * 512:(tc + 1) * 512], ps[:, :],
                                          AF.Silu, bias=cfb[:, ot:ot + 1], scale=1.0)),
                            waits=[("TE", tv), ("VE", gsb_ev[tc])], inc="SE")
                        psg_ev[tc] = sg_
                    pso_ = pso[tc]
                    tvo = None
                    for j in range(8):
                        tvo = P.emit(
                            "tensor",
                            (lambda e, j=j, tc=tc, pso_=pso_, slot=slot:
                             e.matmul(pso_[:, :],
                                      wos[:, slot * 1024 + j * 128: slot * 1024 + (j + 1) * 128],
                                      afH[:, j * 1024 + tc * 512: j * 1024 + (tc + 1) * 512],
                                      start=(j == 0), stop=(j == 7))),
                            waits=([(wsl, lwo), ("GD", afl), ("VE", afsel), ("VE", pso_ev[tc])]
                                   if j == 0 else []),
                            inc="TE")
                    if ot >= GPRE:
                        vsg = P.emit(
                            "vector",
                            (lambda e, ot=ot, tc=tc, pso_=pso_:
                             e.tensor_mul(sgs[:, (ot % 4) * 1024 + tc * 512:
                                              (ot % 4) * 1024 + (tc + 1) * 512],
                                          pso_[:, :], gsb[:, tc * 512:(tc + 1) * 512])),
                            waits=[("TE", tvo), ("SE", sg_)], inc="VE")
                        gsb_ev[tc] = vsg
                    else:
                        vsg = P.emit(
                            "vector",
                            (lambda e, ot=ot, tc=tc, pso_=pso_:
                             e.tensor_mul(sgs[:, (ot % 4) * 1024 + tc * 512:
                                              (ot % 4) * 1024 + (tc + 1) * 512],
                                          pso_[:, :],
                                          gstage[:, ot * 1024 + tc * 512:
                                                 ot * 1024 + (tc + 1) * 512])),
                            waits=[("TE", tvo)], inc="VE")
                    pso_ev[tc] = vsg
                if ot >= GPRE:
                    wldt[slot] = P.n("TE")
                wldo[slot] = P.n("TE")
                if ot % 4 == 3:
                    oc = ot // 4
                    VSG = P.n("VE")
                    for tc in range(2):
                        for tb in range(4):
                            sl = (tc * 4 + tb) % 2
                            pt = pstp[sl]
                            xsem = "D2" if sl == 0 else "D3"
                            ssem = "SA" if sl == 0 else "SB"
                            lx = P.dma("scalar", xcs[:, sl * 512:(sl + 1) * 512],
                                       x_loc[HALO + H * 1024 + tc * 512 + tb * 128:
                                             HALO + H * 1024 + tc * 512 + (tb + 1) * 128,
                                             oc * 512:(oc + 1) * 512],
                                       waits=[("VE", xcs_ev[sl])], sem=xsem)
                            tt = None
                            for i4 in range(4):
                                tt = P.emit(
                                    "tensor",
                                    (lambda e, i4=i4, tc=tc, tb=tb, pt=pt:
                                     e.transpose(pt[:, i4 * 128:(i4 + 1) * 128],
                                                 sgs[:, i4 * 1024 + tc * 512 + tb * 128:
                                                     i4 * 1024 + tc * 512 + (tb + 1) * 128],
                                                 cidf[:, :])),
                                    waits=[("VE", VSG), ("VE", pst_ev[sl])], inc="TE")
                            vv = P.emit(
                                "vector",
                                (lambda e, sl=sl, pt=pt:
                                 e.tensor_add(osb[:, sl * 512:(sl + 1) * 512],
                                              pt[:, :], xcs[:, sl * 512:(sl + 1) * 512])),
                                waits=[("TE", tt), (xsem, lx), (ssem, ost_st[sl])], inc="VE")
                            st = P.dma("gpsimd",
                                       out_loc[H * 1024 + tc * 512 + tb * 128:
                                               H * 1024 + tc * 512 + (tb + 1) * 128,
                                               oc * 512:(oc + 1) * 512],
                                       osb[:, sl * 512:(sl + 1) * 512],
                                       waits=[("VE", vv)], sem=ssem)
                            pst_ev[sl] = vv
                            xcs_ev[sl] = vv
                            ost_st[sl] = st
            te_prevH = P.n("TE")
    P.emit("gpsimd", None, waits=[("ST", P.n("ST")), ("SA", P.n("SA")),
                                  ("SB", P.n("SB"))])
    P.barrier()

    with nc.Block() as block:
        with allow_ici_all_to_all():
            _replay(P, block, sems)
    es.close()
    return nc, list(taps.keys())


_BUILT = {}


def _consts():
    ident = np.eye(128)
    rot = np.zeros((128, 128), np.float32)
    for i in range(64):
        rot[2 * i + 1, 2 * i] = -1.0
        rot[2 * i, 2 * i + 1] = 1.0
    meank = np.zeros((128, 128), np.float32)
    meanq = np.zeros((128, 128), np.float32)
    for kk in range(128):
        for m64 in range(2):
            meank[kk, m64 * 64 + kk % 64] = 0.125
            meanq[kk, m64 * 64 + kk % 64] = 0.03125
    b2 = np.zeros((2, 128), np.float32)
    b2[0, 0:64] = 1.0
    b2[1, 64:128] = 1.0
    m01 = np.zeros((128, 2048), np.float32)
    kk = np.arange(128)[:, None]
    qq = np.arange(512)[None, :]
    for v in range(4):
        m01[:, v * 512:(v + 1) * 512] = (qq >= kk + 128 * v).astype(np.float32)
    return ident, rot, meank, meanq, b2, m01


def kernel(**inputs):
    x = np.ascontiguousarray(np.asarray(inputs["L0_attn_x"], np.float32))
    ident, rot, meank, meanq, b2, m01 = _consts()
    qc1_b = np.asarray(inputs["L0_attn_qc1_b"], np.float32)
    qc2_b = np.asarray(inputs["L0_attn_qc2_b"], np.float32)
    kc1_b = np.asarray(inputs["L0_attn_kc1_b"], np.float32)
    kc2_b = np.asarray(inputs["L0_attn_kc2_b"], np.float32)
    scales = np.asarray(inputs["L0_ffn_scales"], np.float32)[:, :, 0]
    zeros = np.asarray(inputs["L0_ffn_scaled_zeros"], np.float32)[:, :, 0]
    fbias = np.asarray(inputs["L0_ffn_bias"], np.float32)
    ktemp = float(np.asarray(inputs["L0_attn_key_temp"], np.float32).reshape(-1)[0])

    inv = 1.0 / 10000.0 ** (np.arange(0, HD, 2, dtype=np.float32) / HD)

    wqkv_full = np.asarray(inputs["L0_attn_w_qkv"], np.float32)
    wo_full = np.asarray(inputs["L0_attn_w_o"], np.float32)
    qc2_full = np.asarray(inputs["L0_attn_qc2_w"], np.float32)
    qw_full = np.asarray(inputs["L0_ffn_qweight"], np.int32)
    scT_full = scales.reshape(32, 128, 32).transpose(1, 0, 2)
    zrT_full = zeros.reshape(32, 128, 32).transpose(1, 0, 2)
    shared = dict(
        qc1_w=np.ascontiguousarray(np.asarray(inputs["L0_attn_qc1_w"], np.float32)),
        kc1_w=np.ascontiguousarray(np.asarray(inputs["L0_attn_kc1_w"], np.float32)),
        kc2_w=np.ascontiguousarray(np.asarray(inputs["L0_attn_kc2_w"], np.float32)),
        qc1_bT=np.ascontiguousarray(qc1_b.reshape(16, 64).T),
        qc2_bT=np.ascontiguousarray(qc2_b.reshape(8, 128).T),
        kc1_bT=np.ascontiguousarray(kc1_b.reshape(4, 64).T),
        kc2_bT=np.ascontiguousarray(kc2_b.reshape(2, 128).T),

        ffn_biasT=np.ascontiguousarray(fbias.reshape(32, 128).T),
        key_temp_b=np.full((128, 1), ktemp, np.float32),
        ident_bf=ident.astype(BFNP),
        ident_f32=ident.astype(np.float32),
        rot_bf=rot.astype(BFNP),
        meank_l=meank.astype(BFNP),
        meanq_l=meanq.astype(BFNP),
        bcast2=b2,
        mask01=m01.astype(BFNP),
    )

    in_maps = []
    for c in range(8):
        b = c // 4
        p0 = (c % 4) * TOK
        xl = np.zeros((CL, D), np.float32)
        lo = max(0, p0 - HALO)
        xl[HALO - (p0 - lo):] = x[b, lo:p0 + TOK]
        pos = (p0 + np.arange(TOK, dtype=np.float32))[None, :]
        ang = pos * inv[(np.arange(128) % 64) // 2][:, None]
        m = dict(shared)
        m["w_qkv"] = np.ascontiguousarray(wqkv_full[:, 512 * c:512 * (c + 1)])
        m["w_o"] = np.ascontiguousarray(wo_full[:, 128 * c:128 * (c + 1)])
        m["qc2_w"] = np.ascontiguousarray(qc2_full[:, 128 * c:128 * (c + 1), :])
        m["qweight"] = np.ascontiguousarray(qw_full[:, 4 * c:4 * (c + 1), :])
        m["scalesT"] = np.ascontiguousarray(scT_full[:, :, 4 * c:4 * (c + 1)])
        m["zerosT"] = np.ascontiguousarray(zrT_full[:, :, 4 * c:4 * (c + 1)])
        m["x_loc"] = xl
        m["halo_mask"] = np.full((128, 1), 0.0 if p0 == 0 else 1.0, np.float32)
        bs = np.zeros((128, 2), np.float32)
        bs[:, 0 if b == 0 else 1] = 1.0
        m["bsel"] = bs
        m["cosT"] = np.cos(ang).astype(BFNP)
        m["sinT"] = np.sin(ang).astype(BFNP)
        in_maps.append(m)

    key = "prog"
    if key not in _BUILT:
        _BUILT[key] = build_program()
    nc, tapnames = _BUILT[key]
    res = run_bass_kernel_spmd(nc, in_maps, core_ids=list(range(8)))
    out = np.empty((B, S, D), np.float32)
    for c in range(8):
        out[c // 4, (c % 4) * TOK:(c % 4 + 1) * TOK] = res.results[c]["out_loc"]
    if DEBUG_TAPS:
        kernel.tap_results = res.results
    return out


if __name__ == "__main__":
    pass


# version 15
# speedup vs baseline: 1.2367x; 1.0030x over previous
import numpy as np
import ml_dtypes
from contextlib import ExitStack

import concourse.bass as bass
import concourse.mybir as mybir
from concourse.bass_utils import run_bass_kernel_spmd
from concourse.aot_env import allow_ici_all_to_all

BFNP = ml_dtypes.bfloat16
F32 = mybir.dt.float32
BF16 = mybir.dt.bfloat16
I32 = mybir.dt.int32
AF = mybir.ActivationFunctionType
ALU = mybir.AluOpType

B, S, D = 2, 8192, 4096
NH, NKV, HD, LAT, KVD = 16, 4, 64, 1024, 256
TOK = 2048
HALO = 8
CL = TOK + HALO
C1L = TOK + 2

OFF_Q = 0
OFF_K = 256 * TOK
OFF_V = OFF_K + 64 * TOK
SH_W = OFF_V + TOK * 65
SH2_W = 256 * TOK

DEBUG_TAPS = ()
PHASES = 4

TC5 = [(0, 512), (512, 1024), (1024, 1536), (1536, 2048), (2048, CL)]
TC5C = [(0, 512), (512, 1024), (1024, 1536), (1536, 2048), (2048, C1L)]
TC4 = [(0, 512), (512, 1024), (1024, 1536), (1536, 2048)]


class Plan:
    ENG = ("sync", "gpsimd", "tensor", "vector", "scalar")

    def __init__(self):
        self.ops = {e: [] for e in self.ENG}
        self.cnt = {}

    def n(self, sem):
        return self.cnt.get(sem, 0)

    def emit(self, eng, fn, waits=(), inc=None, by=1):
        if inc is not None:
            self.cnt[inc] = self.cnt.get(inc, 0) + by
        self.ops[eng].append((fn, tuple(waits), inc, by))
        return self.cnt.get(inc) if inc is not None else None

    def dma(self, queue, out, in_, waits=(), sem=None, **kw):
        if sem is None:
            sem = {"sync": "LD", "gpsimd": "GD", "scalar": "XD"}[queue]
        return self.emit(
            queue,
            (lambda e, out=out, in_=in_, kw=kw: e.dma_start(out=out, in_=in_, **kw)),
            waits, inc=sem, by=16)

    def barrier(self, engines=None):
        engines = engines or self.ENG
        vals = {}
        for e in engines:
            vals[e] = self.emit(e, (lambda eng: eng.nop()), (), inc="B" + e, by=1)
        for e in engines:
            waits = tuple(("B" + o, vals[o]) for o in engines if o != e)
            self.emit(e, None, waits)


def _replay(plan, block, sems):
    def mk(ename):
        def body(eng):
            hw = {}
            for fn, waits, inc, by in plan.ops[ename]:
                for s, v in waits:
                    if v > hw.get(s, 0):
                        eng.wait_ge(sems[s], v)
                        hw[s] = v
                if fn is not None:
                    ins = fn(eng)
                    if inc is not None:
                        ins.then_inc(sems[inc], by)
        return body
    block.sync(mk("sync"))
    block.gpsimd(mk("gpsimd"))
    block.tensor(mk("tensor"))
    block.vector(mk("vector"))
    block.scalar(mk("scalar"))


class _StopBuild(Exception):
    pass


_CTX = {}


def build_program():
    try:
        return _build_inner()
    except _StopBuild:
        nc, P, sems, es, taps = (_CTX["nc"], _CTX["P"], _CTX["sems"],
                                 _CTX["es"], _CTX["taps"])
        P.emit("gpsimd", None, waits=[("ST", P.n("ST")), ("SA", P.n("SA")),
                                      ("SB", P.n("SB"))])
        P.barrier()
        with nc.Block() as block:
            with allow_ici_all_to_all():
                _replay(P, block, sems)
        es.close()
        return nc, list(taps.keys())


def _build_inner():
    nc = bass.Bass("TRN2", target_bir_lowering=False, debug=False)
    P = Plan()

    def I(name, shape, dt):
        return nc.dram_tensor(name, shape, dt, kind="ExternalInput")

    x_loc = I("x_loc", [CL, D], F32)
    w_qkv = I("w_qkv", [1536, 512], F32)
    w_o = I("w_o", [D, 128], F32)
    qc1_w = I("qc1_w", [LAT, HD, 3], F32)
    qc2_w = I("qc2_w", [LAT, 128, 3], F32)
    kc1_w = I("kc1_w", [KVD, HD, 3], F32)
    kc2_w = I("kc2_w", [KVD, KVD, 3], F32)
    qc1_bT = I("qc1_bT", [64, 16], F32)
    qc2_bT = I("qc2_bT", [128, 8], F32)
    kc1_bT = I("kc1_bT", [64, 4], F32)
    kc2_bT = I("kc2_bT", [128, 2], F32)
    qweight = I("qweight", [D, 4, 128], I32)
    scalesT = I("scalesT", [128, 32, 4], F32)
    zerosT = I("zerosT", [128, 32, 4], F32)
    ffn_biasT = I("ffn_biasT", [128, 32], F32)
    key_temp_b = I("key_temp_b", [128, 1], F32)
    halo_mask = I("halo_mask", [128, 1], F32)
    ident_bf = I("ident_bf", [128, 128], BF16)
    ident_f32 = I("ident_f32", [128, 128], F32)
    rot_bf = I("rot_bf", [128, 128], BF16)
    meank_l = I("meank_l", [128, 128], BF16)
    meanq_l = I("meanq_l", [128, 128], BF16)
    bcast2 = I("bcast2", [2, 128], F32)
    cosT = I("cosT", [128, TOK], BF16)
    sinT = I("sinT", [128, TOK], BF16)
    mask01 = I("mask01", [128, 2048], BF16)
    bsel = I("bsel", [128, 2], F32)

    out_loc = nc.dram_tensor("out_loc", [TOK, D], F32, kind="ExternalOutput")
    taps = {}
    def mktap(name, shape, dt):
        if name in DEBUG_TAPS:
            taps[name] = nc.dram_tensor("tap_" + name, shape, dt, kind="ExternalOutput")

    kc2T_d = nc.dram_tensor("kc2T_d", [3, KVD, KVD], BF16)
    OQ, OW, OC2, OF = 0, 786432, 1310720, 1703936
    PSL = 3801088
    pp_in = nc.dram_tensor("pp_in", [1, PSL], BF16)
    pp_out = nc.dram_tensor("pp_out", [8, PSL], BF16)
    xnT_d = nc.dram_tensor("xnT_d", [D, CL], BF16)
    a1_in = nc.dram_tensor("a1_in", [8, SH_W], BF16)
    a1_out = nc.dram_tensor("a1_out", [8, SH_W], BF16)
    a2_in = nc.dram_tensor("a2_in", [8, SH2_W], BF16)
    a2_out = nc.dram_tensor("a2_out", [8, SH2_W], BF16)
    dn_d = nc.dram_tensor("dn_d", [64 * 512], BF16)
    groups = [[0, 1, 2, 3, 4, 5, 6, 7]]

    es = ExitStack()
    sems = {}
    for s in ["LD", "GD", "ST", "XD", "CC", "TE", "VE", "SE",
              "D0", "D1", "D2", "D3", "SA", "SB", "W0", "W1",
              "Bsync", "Bgpsimd", "Btensor", "Bvector", "Bscalar"]:
        sems[s] = es.enter_context(nc.semaphore(s))

    sb = lambda *a: es.enter_context(nc.sbuf_tensor(*a))
    cid = sb("c_id", [128, 128], BF16)
    cidf = sb("c_idf", [128, 128], F32)
    crot = sb("c_rot", [128, 128], BF16)
    cmk = sb("c_mk", [128, 128], BF16)
    cmq = sb("c_mq", [128, 128], BF16)
    cb2 = sb("c_b2", [2, 128], F32)
    ccos = sb("c_cos", [128, TOK], BF16)
    csin = sb("c_sin", [128, TOK], BF16)
    cq1b = sb("c_q1b", [64, 16], F32)
    cq2b = sb("c_q2b", [128, 8], F32)
    ck1b = sb("c_k1b", [64, 4], F32)
    ck2b = sb("c_k2b", [128, 2], F32)
    cfb = sb("c_fb", [128, 32], F32)
    ctmp = sb("c_tmp", [128, 1], F32)
    ctmp8 = sb("c_tmp8", [128, 1], F32)
    cnbias = sb("c_nbias", [128, 1], F32)
    chmask = sb("c_hmask", [128, 1], F32)
    cbl1 = sb("c_bl1", [128, 2], F32)
    cones1 = sb("c_ones1", [1, 64], F32)
    cmask01 = sb("c_mask01", [128, 2048], BF16)
    cbsel = sb("c_bsel", [128, 2], F32)
    ceps = sb("c_eps", [128, 1], F32)
    qc1T = sb("qc1T", [128, 8 * 3 * 64], BF16)
    kc1T = sb("kc1T", [128, 2 * 3 * 64], BF16)

    for t, src in [(cid, ident_bf), (cidf, ident_f32), (crot, rot_bf),
                   (cmk, meank_l), (cmq, meanq_l), (cb2, bcast2),
                   (ccos, cosT), (csin, sinT), (cq1b, qc1_bT), (cq2b, qc2_bT),
                   (ck1b, kc1_bT), (ck2b, kc2_bT), (cfb, ffn_biasT),
                   (chmask, halo_mask), (cmask01, mask01), (ctmp, key_temp_b),
                   (cbsel, bsel)]:
        P.dma("sync", t[:, :], src[:, :], sem="D0")
    LDC = P.n("D0")
    P.emit("scalar", lambda e: e.activation(ctmp8[:, :], ctmp[:, :], AF.Copy, bias=0.0, scale=0.125),
           waits=[("D0", LDC)], inc="SE")
    sA = P.emit("scalar", lambda e: e.activation(cnbias[:, :], ctmp[:, :], AF.Abs, bias=0.0, scale=0.125),
                waits=[("D0", LDC)], inc="SE")
    P.emit("vector", lambda e: e.tensor_scalar_mul(cnbias[:, :], cnbias[:, :], -1.0),
           waits=[("SE", sA)], inc="VE")
    P.emit("vector", lambda e: e.memset(cbl1[:, :], 0.0), inc="VE")
    P.emit("vector", lambda e: e.memset(cbl1[0:64, 0:1], 1.0), inc="VE")
    P.emit("vector", lambda e: e.memset(cbl1[64:128, 1:2], 1.0), inc="VE")
    P.emit("vector", lambda e: e.memset(cones1[:, :], 1.0), inc="VE")
    VEPS = P.emit("vector", lambda e: e.memset(ceps[:, :], 1e-5), inc="VE")

    with ExitStack() as p0:
        wbf = p0.enter_context(nc.sbuf_tensor("p0_wbf", [128, 17408], BF16))
        big = p0.enter_context(nc.sbuf_tensor("p0_big", [128, 16384], BF16))
        stg = [p0.enter_context(nc.sbuf_tensor(f"p0_stg{i}", [128, 4096], BF16))
               for i in range(2)]
        stg_ld = [0, 0]
        qwi = p0.enter_context(nc.sbuf_tensor("p0_qwi", [128, 2 * 512], I32))
        scz = p0.enter_context(nc.sbuf_tensor("p0_scz", [128, 256], F32))
        wde = p0.enter_context(nc.sbuf_tensor("p0_wde", [128, 2 * 512], BF16))
        pst = [p0.enter_context(nc.psum_tensor(f"p0_ps{i}", [128, 512], BF16))
               for i in range(2)]
        pst_ve = [0, 0]
        psn = [0]

        def tgroup(src_aps, ready_waits):
            si = psn[0] % 2
            psn[0] += 1
            ps = pst[si]
            tv = None
            for i, ap in enumerate(src_aps):
                w = list(ready_waits) + [("VE", pst_ve[si]), ("D0", LDC)]
                tv = P.emit("tensor",
                            (lambda e, ap=ap, ps=ps, i=i:
                             e.transpose(ps[:, i * 128:(i + 1) * 128], ap, cid[:, :])),
                            waits=w, inc="TE")
            return ps, si, tv

        def evac(dst_ap, width, ps, si, tv, extra=()):
            sv = P.emit("vector",
                        (lambda e, ps=ps, dst_ap=dst_ap, width=width:
                         e.tensor_copy(dst_ap, ps[:, 0:width])),
                        waits=[("TE", tv)] + list(extra), inc="VE")
            pst_ve[si] = sv
            return sv

        def ppdst(off, n):
            return pp_in.ap()[0, off:off + n]

        W_QKV, W_O, W_Q2, W_K2, W_Q1, W_K1 = 0, 6144, 10240, 13312, 14848, 16384
        P.dma("gpsimd", wbf[:, W_QKV:W_QKV + 6144].rearrange("p (t c) -> p t c", c=512),
              w_qkv.ap().rearrange("(t p) c -> p t c", p=128), sem="XD")
        P.dma("gpsimd", wbf[:, W_O:W_O + 4096].rearrange("p (t c) -> p t c", c=128),
              w_o.ap().rearrange("(t p) c -> p t c", p=128), sem="XD")
        P.dma("gpsimd", wbf[:, W_Q2:W_Q2 + 3072].rearrange("p (t c) -> p t c", c=384),
              qc2_w.ap().rearrange("(t p) i k -> p t (i k)", p=128), sem="XD")
        P.dma("gpsimd", wbf[:, W_K2:W_K2 + 1536].rearrange("p (t c) -> p t c", c=768),
              kc2_w.ap().rearrange("(t p) i k -> p t (i k)", p=128), sem="XD")
        P.dma("gpsimd", wbf[:, W_Q1:W_Q1 + 1536].rearrange("p (t c) -> p t c", c=192),
              qc1_w.ap().rearrange("(t p) i k -> p t (i k)", p=128), sem="XD")
        P.dma("gpsimd", wbf[:, W_K1:W_K1 + 384].rearrange("p (t c) -> p t c", c=192),
              kc1_w.ap().rearrange("(t p) i k -> p t (i k)", p=128), sem="XD")
        GD_L = P.n("XD")
        P.dma("sync", scz[:, 0:128], scalesT[:, :, :])
        l2 = P.dma("sync", scz[:, 128:256], zerosT[:, :, :])

        for j in range(4):
            sgi = j % 2
            for g0 in range(0, 12, 4):
                aps = [wbf[:, W_QKV + (g0 + t) * 512 + j * 128:
                           W_QKV + (g0 + t) * 512 + (j + 1) * 128] for t in range(4)]
                ps, si, tv = tgroup(aps, [("XD", GD_L)])
                evac(stg[sgi][:, g0 * 128:(g0 + 4) * 128], 512, ps, si, tv,
                     extra=[("SA" if sgi == 0 else "SB", stg_ld[sgi])])
            sv = P.n("VE")
            ld = P.dma("sync",
                       ppdst(OQ + j * 128 * 1536, 128 * 1536)
                       .rearrange("(p c) -> p c", c=1536),
                       stg[sgi][:, 0:1536],
                       waits=[("VE", sv)], sem=("SA" if sgi == 0 else "SB"))
            stg_ld[sgi] = ld

        for g0 in range(0, 32, 4):
            sgi = 0
            aps = [wbf[:, W_O + (g0 + t) * 128: W_O + (g0 + t + 1) * 128]
                   for t in range(4)]
            ps, si, tv = tgroup(aps, [("XD", GD_L)])
            evac(stg[0][:, g0 * 128:(g0 + 4) * 128], 512, ps, si, tv,
                 extra=[("SA", stg_ld[0])])
        sv = P.n("VE")
        ld = P.dma("sync", ppdst(OW, 524288).rearrange("(p c) -> p c", c=4096),
                   stg[0][:, 0:4096], waits=[("VE", sv)], sem="SA")
        stg_ld[0] = ld

        for tau in range(3):
            rps = None
            for ot in range(8):
                rps = P.emit(
                    "vector",
                    (lambda e, tau=tau, ot=ot:
                     e.tensor_copy(big[:, ot * 128:(ot + 1) * 128],
                                   wbf[:, W_Q2 + ot * 384 + tau:
                                       W_Q2 + (ot + 1) * 384:3])),
                    waits=[("XD", GD_L), ("TE", P.n("TE"))], inc="VE")
            sgi = 1
            for g0 in range(0, 8, 4):
                aps = [big[:, (g0 + t) * 128:(g0 + t + 1) * 128] for t in range(4)]
                ps, si, tv = tgroup(aps, [("VE", rps)])
                evac(stg[1][:, g0 * 128:(g0 + 4) * 128], 512, ps, si, tv,
                     extra=[("SB", stg_ld[1])])
            sv = P.n("VE")
            ld = P.dma("sync",
                       ppdst(OC2 + tau * 131072, 131072)
                       .rearrange("(p c) -> p c", c=1024),
                       stg[1][:, 0:1024], waits=[("VE", sv)], sem="SB")
            stg_ld[1] = ld

        for r in range(2):
            rpv = {}
            for tau in range(3):
                rpv[tau] = P.emit(
                    "vector",
                    (lambda e, r=r, tau=tau:
                     e.tensor_copy(big[:, 1024 + tau * 256: 1024 + (tau + 1) * 256],
                                   wbf[:, W_K2 + r * 768 + tau:
                                       W_K2 + (r + 1) * 768:3])),
                    waits=[("XD", GD_L), ("TE", P.n("TE"))], inc="VE")
            for tau in range(3):
                base = 1024 + tau * 256
                aps = [big[:, base + i * 128: base + (i + 1) * 128] for i in range(2)]
                ps, si, tv = tgroup(aps, [("VE", rpv[tau])])
                sgi = tau % 2
                sv = evac(stg[sgi][:, 0:256], 256, ps, si, tv,
                          extra=[("SA" if sgi == 0 else "SB", stg_ld[sgi])])
                dst = kc2T_d.ap()[tau, :, r * 128:(r + 1) * 128]
                dst = dst.rearrange("(b p) c -> p b c", p=128)
                ld = P.dma("sync", dst,
                           stg[sgi][:, 0:256].rearrange("p (b c) -> p b c", c=128),
                           waits=[("VE", sv)], sem=("SA" if sgi == 0 else "SB"))
                stg_ld[sgi] = ld

        def conv1_prep(nr, srcbase, dstT):
            for r in range(nr):
                vv = None
                for tau in range(3):
                    vv = P.emit(
                        "vector",
                        (lambda e, r=r, tau=tau, srcbase=srcbase:
                         e.tensor_copy(big[:, 2048 + tau * 64: 2048 + (tau + 1) * 64],
                                       wbf[:, srcbase + r * 192 + tau:
                                           srcbase + (r + 1) * 192:3])),
                        waits=[("XD", GD_L), ("TE", P.n("TE")), ("ST", P.n("ST"))],
                        inc="VE")
                for tau in range(3):
                    si = psn[0] % 2
                    psn[0] += 1
                    ps = pst[si]
                    tv = P.emit("tensor",
                                (lambda e, tau=tau, ps=ps:
                                 e.transpose(ps[0:64, 0:128],
                                             big[:, 2048 + tau * 64: 2048 + (tau + 1) * 64],
                                             cid[:, :])),
                                waits=[("VE", vv), ("VE", pst_ve[si]), ("D0", LDC)],
                                inc="TE")
                    sv = P.emit("vector",
                                (lambda e, ps=ps, r=r, tau=tau, dstT=dstT:
                                 e.tensor_copy(dstT[0:64, (r * 3 + tau) * 64:(r * 3 + tau + 1) * 64],
                                               ps[0:64, 0:64])),
                                waits=[("TE", tv)], inc="VE")
                    sv2 = P.emit("vector",
                                 (lambda e, ps=ps, tau=tau:
                                  e.tensor_copy(big[0:64, 2240 + tau * 64: 2240 + (tau + 1) * 64],
                                                ps[0:64, 64:128])),
                                 waits=[("TE", tv)], inc="VE")
                    pst_ve[si] = sv2
                    P.dma("gpsimd",
                          dstT[64:128, (r * 3 + tau) * 64:(r * 3 + tau + 1) * 64],
                          big[0:64, 2240 + tau * 64: 2240 + (tau + 1) * 64],
                          waits=[("VE", sv2)], sem="ST")
        conv1_prep(8, W_Q1, qc1T)
        conv1_prep(2, W_K1, kc1T)

        qw_ld = [0, 0]
        dq_rd = [0, 0]
        for ot in range(32):
            qsl = ot % 2
            gq = P.dma("gpsimd", qwi[:, qsl * 512:(qsl + 1) * 512],
                       qweight[ot * 128:(ot + 1) * 128, :, :],
                       waits=[("VE", qw_ld[qsl])],
                       sem=("D2" if qsl == 0 else "D3"))
            te_wde = P.n("TE")
            dq = None
            for g in range(4):
                dq = P.emit("vector",
                            (lambda e, g=g, qsl=qsl, ot=ot:
                             e.tensor_scalar(wde[:, qsl * 512 + g * 128:
                                                 qsl * 512 + (g + 1) * 128],
                                             qwi[:, qsl * 512 + g * 128:
                                                 qsl * 512 + (g + 1) * 128],
                                             scz[:, ot * 4 + g: ot * 4 + g + 1],
                                             scz[:, 128 + ot * 4 + g: 129 + ot * 4 + g],
                                             ALU.mult, ALU.add)),
                            waits=[("D2" if qsl == 0 else "D3", gq), ("LD", l2),
                                   ("TE", dq_rd[qsl])],
                            inc="VE")
            qw_ld[qsl] = dq
            aps = [wde[:, qsl * 512 + i * 128: qsl * 512 + (i + 1) * 128]
                   for i in range(4)]
            ps, si, tv = tgroup(aps, [("VE", dq)])
            dq_rd[qsl] = P.n("TE")
            P.emit("vector",
                   (lambda e, ps=ps, ot=ot:
                    e.tensor_copy(big.ap().rearrange("p (b c) -> p b c", c=4096)
                                  [:, 0:4, ot * 128:(ot + 1) * 128],
                                  ps[:, 0:512].rearrange("p (b c) -> p b c", c=128))),
                   waits=[("TE", tv), ("SA", stg_ld[0]), ("SB", stg_ld[1])], inc="VE")
            pst_ve[si] = P.n("VE")
        WTF_VE = P.n("VE")
        if "bigw" in DEBUG_TAPS:
            mktap("bigw", [128, 16384], BF16)
            P.dma("gpsimd", taps["bigw"][:, :], big[:, :],
                  waits=[("VE", WTF_VE)], sem="ST")
        if "wdew" in DEBUG_TAPS:
            mktap("wdew", [128, 1024], BF16)
            P.dma("gpsimd", taps["wdew"][:, :], wde[:, :],
                  waits=[("VE", WTF_VE)], sem="ST")
        for j in range(4):
            P.dma("sync", ppdst(OF + j * 524288, 524288)
                  .rearrange("(p c) -> p c", c=4096),
                  big[:, j * 4096:(j + 1) * 4096],
                  waits=[("VE", WTF_VE)], sem=("SA" if j % 2 == 0 else "SB"))

    P.barrier()
    ccp = P.emit("gpsimd",
                 (lambda e: e.collective_compute(
                     "AllGather", ALU.bypass, replica_groups=groups,
                     ins=[pp_in.ap().opt()], outs=[pp_out.ap().opt()])),
                 waits=[("SA", P.n("SA")), ("SB", P.n("SB")), ("GD", P.n("GD"))],
                 inc="CC")
    CC_P0 = P.n("CC")
    if "ppw" in DEBUG_TAPS:
        mktap("ppw", [8, PSL], BF16)
        P.dma("gpsimd", taps["ppw"][:, :], pp_out[:, :],
              waits=[("CC", CC_P0)], sem="ST")
    if "ppi" in DEBUG_TAPS:
        mktap("ppi", [1, PSL], BF16)
        P.dma("gpsimd", taps["ppi"][:, :], pp_in[:, :],
              waits=[("CC", CC_P0)], sem="ST")
    LD_P0 = P.n("LD")
    GD_P0 = P.n("GD")


    _CTX.update(nc=nc, P=P, sems=sems, es=es, taps=taps)
    with ExitStack() as pa:
        qkvT = pa.enter_context(nc.sbuf_tensor("qkvT", [128, 12 * CL], BF16))

        HL = 1032
        TC3 = [(0, 512), (512, 1024), (1024, HL)]
        with ExitStack() as pa1:
            xnT = pa1.enter_context(nc.sbuf_tensor("xnT", [128, 32 * HL], BF16))
            xbuf = pa1.enter_context(nc.sbuf_tensor("a_x", [128, 2 * D], F32))
            xnt = pa1.enter_context(nc.sbuf_tensor("a_xn", [128, 2 * D], BF16))
            lnst = pa1.enter_context(nc.sbuf_tensor("a_ln", [128, 2 * 64], F32))
            wq = pa1.enter_context(nc.sbuf_tensor("a_wq", [128, 2 * 4096], BF16))
            pstr = [pa1.enter_context(nc.psum_tensor(f"a_tr{i}", [128, 512], BF16))
                    for i in range(2)]
            pstr_ve = [0, 0]
            psq = [pa1.enter_context(nc.psum_tensor(f"a_q{i}", [128, 512], F32))
                   for i in range(6)]

            evq = {}
            wqld = [0, 0]
            seapp = {}
            trdone = {}
            spill_ld = 0
            ve_tile = {}
            for h in range(2):
                for kk9 in range(9):
                    k = h * 9 + kk9
                    rows = 128 if kk9 < 8 else 8
                    r0 = h * 1024 + kk9 * 128
                    slot = k % 2
                    w = [("SE", seapp.get(k - 2, 0))]
                    l1 = P.dma("sync", xbuf[0:rows, slot * D:slot * D + D],
                               x_loc[r0:r0 + rows, :], waits=w,
                               sem=("D0" if slot == 0 else "D1"))
                    vs = None
                    for i in range(8):
                        vs = P.emit("vector",
                                    (lambda e, rows=rows, slot=slot, i=i:
                                     e.bn_stats(lnst[0:rows, slot * 64 + i * 6: slot * 64 + (i + 1) * 6],
                                                xbuf[0:rows, slot * D + i * 512: slot * D + (i + 1) * 512])),
                                    waits=[("D0" if slot == 0 else "D1", l1)], inc="VE")
                    va = P.emit("vector",
                                (lambda e, rows=rows, slot=slot:
                                 e.bn_aggr(lnst[0:rows, slot * 64 + 48: slot * 64 + 50],
                                           lnst[0:rows, slot * 64: slot * 64 + 48])),
                                waits=[("VE", vs)], inc="VE")
                    sq = P.emit("scalar",
                                (lambda e, rows=rows, slot=slot:
                                 e.activation(lnst[0:rows, slot * 64 + 50: slot * 64 + 51],
                                              lnst[0:rows, slot * 64 + 49: slot * 64 + 50],
                                              AF.Sqrt, bias=ceps[0:rows, 0:1], scale=1.0)),
                                waits=[("VE", va), ("VE", VEPS)], inc="SE")
                    vr = P.emit("vector",
                                (lambda e, rows=rows, slot=slot:
                                 e.reciprocal(lnst[0:rows, slot * 64 + 51: slot * 64 + 52],
                                              lnst[0:rows, slot * 64 + 50: slot * 64 + 51])),
                                waits=[("SE", sq)], inc="VE")
                    vn = P.emit("vector",
                                (lambda e, rows=rows, slot=slot:
                                 e.scalar_tensor_tensor(lnst[0:rows, slot * 64 + 52: slot * 64 + 53],
                                                        lnst[0:rows, slot * 64 + 48: slot * 64 + 49],
                                                        -1.0,
                                                        lnst[0:rows, slot * 64 + 51: slot * 64 + 52],
                                                        ALU.mult, ALU.mult)),
                                waits=[("VE", vr)], inc="VE")
                    sa = P.emit("scalar",
                                (lambda e, rows=rows, slot=slot:
                                 e.activation(xnt[0:rows, slot * D: slot * D + D],
                                              xbuf[0:rows, slot * D: slot * D + D],
                                              AF.Identity,
                                              bias=lnst[0:rows, slot * 64 + 52: slot * 64 + 53],
                                              scale=lnst[0:rows, slot * 64 + 51: slot * 64 + 52])),
                                waits=[("VE", vn), ("TE", trdone.get(k - 2, 0))], inc="SE")
                    seapp[k] = sa
                    for g0 in range(0, 32, 4):
                        si = g0 // 4 % 2
                        ps = pstr[si]
                        tv = None
                        for i in range(4):
                            j = g0 + i
                            if rows == 128:
                                tv = P.emit("tensor",
                                            (lambda e, j=j, i=i, ps=ps, slot=slot:
                                             e.transpose(ps[:, i * 128:(i + 1) * 128],
                                                         xnt[:, slot * D + j * 128: slot * D + (j + 1) * 128],
                                                         cid[:, :])),
                                            waits=[("SE", sa), ("VE", pstr_ve[si])], inc="TE")
                            else:
                                tv = P.emit("tensor",
                                            (lambda e, j=j, i=i, ps=ps, slot=slot:
                                             e.transpose(ps[:, i * 8:(i + 1) * 8],
                                                         xnt[0:8, slot * D + j * 128: slot * D + (j + 1) * 128],
                                                         cid[0:8, 0:8])),
                                            waits=[("SE", sa), ("VE", pstr_ve[si])], inc="TE")
                        wdt = 128 if rows == 128 else 8
                        sv = P.emit("vector",
                                    (lambda e, ps=ps, g0=g0, wdt=wdt, kk9=kk9:
                                     e.tensor_copy(
                                         xnT.ap().rearrange("p (j t) -> p j t", t=HL)
                                         [:, g0:g0 + 4, kk9 * 128: kk9 * 128 + wdt],
                                         ps[:, 0:4 * wdt].rearrange("p (b c) -> p b c", c=wdt))),
                                    waits=[("TE", tv), ("LD", spill_ld)], inc="VE")
                        pstr_ve[si] = sv
                    trdone[k] = P.n("TE")
                    ve_tile[kk9] = P.n("VE")
                VE_LN = P.n("VE")

                spill_ld = P.dma(
                    "sync",
                    xnT_d.ap().rearrange("(j p) t -> p j t", p=128)[:, :, h * 1024: h * 1024 + HL],
                    xnT[:, :].rearrange("p (j t) -> p j t", t=HL),
                    waits=[("VE", VE_LN)])

                for r in range(12):
                    slot = r % 2
                    lw = None
                    for rr8 in range(8):
                        lw = P.dma("sync",
                                   wq[:, slot * 4096 + rr8 * 512:
                                      slot * 4096 + (rr8 + 1) * 512]
                                   .rearrange("p (b c) -> p b c", c=128),
                                   pp_out.ap()[rr8, OQ:OQ + 786432]
                                   .rearrange("(b p c) -> p b c", p=128, c=1536)
                                   [:, :, r * 128:(r + 1) * 128],
                                   waits=([("CC", CC_P0), ("TE", wqld[slot])]
                                          if rr8 == 0 else []),
                                   sem=("D2" if slot == 0 else "D3"))
                    for i in range(32):
                        for tci, (c0, c1) in enumerate(TC3):
                            P.emit(
                                "tensor",
                                (lambda e, i=i, tci=tci, c0=c0, c1=c1, slot=slot:
                                 e.matmul(psq[tci][:, 0:c1 - c0],
                                          wq[:, slot * 4096 + i * 128: slot * 4096 + (i + 1) * 128],
                                          xnT[:, i * HL + c0: i * HL + c1],
                                          start=(i == 0), stop=(i == 31))),
                                waits=([("D2" if slot == 0 else "D3", lw),
                                        ("VE", ve_tile[(3, 7, 8)[tci]]),
                                        ("SE", evq.get(tci, 0))]
                                       if i == 0 else []),
                                inc="TE")
                    wqld[slot] = P.n("TE")
                    for tci, (c0, c1) in enumerate(TC3):
                        evq[tci] = P.emit(
                            "scalar",
                            (lambda e, r=r, h=h, tci=tci, c0=c0, c1=c1:
                             e.activation(qkvT[:, r * CL + h * 1024 + c0: r * CL + h * 1024 + c1],
                                          psq[tci][:, 0:c1 - c0], AF.Copy,
                                          bias=0.0, scale=1.0)),
                            waits=[("TE", P.n("TE"))], inc="SE")
        SE_QKV = P.n("SE")
        if "qkvT" in DEBUG_TAPS:
            mktap("qkvT", [1536, CL], BF16)
            P.dma("gpsimd", taps["qkvT"].ap().rearrange("(j p) t -> p j t", p=128),
                  qkvT[:, :].rearrange("p (j t) -> p j t", t=CL),
                  waits=[("SE", SE_QKV)], sem="ST")
        if "xnT" in DEBUG_TAPS:
            mktap("xnT", [D, CL], BF16)
            P.dma("gpsimd", taps["xnT"].ap().rearrange("(j p) t -> p j t", p=128),
                  xnT_d.ap().rearrange("(j p) t -> p j t", p=128),
                  waits=[("SE", SE_QKV)], sem="ST")

        if PHASES < 2:
            raise _StopBuild()
        P.barrier()
        with ExitStack() as pa2:
            qmix = pa2.enter_context(nc.sbuf_tensor("qmix", [128, 8 * TOK], BF16))
            kmix = pa2.enter_context(nc.sbuf_tensor("kmix", [128, 2 * TOK], BF16))
            w2s = pa2.enter_context(nc.sbuf_tensor("w2s", [128, 2 * 24 * 128], BF16))
            oddst = pa2.enter_context(nc.sbuf_tensor("oddst", [64, 512], BF16))
            pa2a = pa2.enter_context(ExitStack())
            c1q = pa2a.enter_context(nc.sbuf_tensor("c1q", [128, 8 * C1L], BF16))
            c1k = pa2a.enter_context(nc.sbuf_tensor("c1k", [128, 2 * C1L], BF16))
            ps1 = [pa2.enter_context(nc.psum_tensor(f"a2_p{i}", [128, 512], F32))
                   for i in range(6)]
            ps1_ev = [("SE", 0)] * 6

            def conv1(nh2, srcT, src_t0, dstT, wT, biasT):
                for r in range(nh2):
                    for tci, (c0, c1) in enumerate(TC5C):
                        wdt = c1 - c0
                        for hh in range(2):
                            pi = (r * 2 + hh) % 4
                            ps = ps1[pi]
                            base = 64 * hh
                            tv = None
                            for tau in range(3):
                                tv = P.emit(
                                    "tensor",
                                    (lambda e, r=r, hh=hh, tau=tau, c0=c0, wdt=wdt,
                                     ps=ps, base=base, srcT=srcT, src_t0=src_t0, wT=wT:
                                     e.matmul(ps[0:64, 0:wdt],
                                              wT[base:base + 64,
                                                 (r * 3 + tau) * 64:(r * 3 + tau + 1) * 64],
                                              srcT[base:base + 64,
                                                   src_t0 + r * CL + 4 + c0 + tau:
                                                   src_t0 + r * CL + 4 + c0 + tau + wdt],
                                              start=(tau == 0), stop=(tau == 2))),
                                    waits=([("SE", SE_QKV), ps1_ev[pi]]
                                           if tau == 0 else []),
                                    inc="TE")
                            h = 2 * r + hh
                            if hh == 0:
                                ev = P.emit(
                                    "scalar",
                                    (lambda e, r=r, c0=c0, wdt=wdt, ps=ps, h=h,
                                     dstT=dstT, biasT=biasT:
                                     e.activation(dstT[0:64, r * C1L + c0: r * C1L + c0 + wdt],
                                                  ps[0:64, 0:wdt], AF.Identity,
                                                  bias=biasT[0:64, h:h + 1], scale=1.0)),
                                    waits=[("TE", tv)], inc="SE")
                                ps1_ev[pi] = ("SE", ev)
                            else:
                                ev = P.emit(
                                    "scalar",
                                    (lambda e, c0=c0, wdt=wdt, ps=ps, h=h, biasT=biasT:
                                     e.activation(oddst[0:64, 0:wdt],
                                                  ps[0:64, 0:wdt], AF.Identity,
                                                  bias=biasT[0:64, h:h + 1], scale=1.0)),
                                    waits=[("TE", tv), ("GD", P.n("GD"))], inc="SE")
                                ps1_ev[pi] = ("SE", ev)
                                P.dma("gpsimd",
                                      dstT[64:128, r * C1L + c0: r * C1L + c0 + wdt],
                                      oddst[0:64, 0:wdt], waits=[("SE", ev)])
            conv1(8, qkvT, 0, c1q, qc1T, cq1b)
            conv1(2, qkvT, 8 * CL, c1k, kc1T, ck1b)
            GD_C1 = P.n("GD")
            hm = []
            for t, nt in [(c1q, 8), (c1k, 2)]:
                for r in range(nt):
                    hm.append(P.emit(
                        "vector",
                        (lambda e, t=t, r=r:
                         e.tensor_scalar_mul(t[:, r * C1L: r * C1L + 2],
                                             t[:, r * C1L: r * C1L + 2],
                                             chmask[:, 0:1])),
                        waits=[("SE", P.n("SE")), ("GD", GD_C1)], inc="VE"))
            VE_HM = P.n("VE")

            w2ld = [0, 0]

            def conv2(n_out, n_in, srcC, dstT, w2d, biasT, mean_l, mean_src_t0, mean_n):
                for o in range(n_out):
                    slot = o % 2
                    if w2d is None:
                        lw = None
                        for tau3 in range(3):
                            lw = P.dma(
                                "sync",
                                w2s[:, slot * 3072 + tau3 * n_in * 128:
                                    slot * 3072 + (tau3 + 1) * n_in * 128]
                                .rearrange("p (j c) -> p j c", c=128),
                                pp_out.ap()[:, OC2 + tau3 * 131072:
                                            OC2 + (tau3 + 1) * 131072]
                                .rearrange("r (p c) -> p r c", p=128, c=1024)
                                [:, :, o * 128:(o + 1) * 128],
                                waits=([("CC", CC_P0), ("TE", w2ld[slot])]
                                       if tau3 == 0 else []),
                                sem=("D0" if slot == 0 else "D1"))
                    else:
                        lw = P.dma("sync",
                                   w2s[:, slot * 3072: slot * 3072 + 3 * n_in * 128]
                                   .rearrange("p (b c) -> p b c", c=128),
                                   w2d.ap()[:, :, o * 128:(o + 1) * 128]
                                   .rearrange("t (j p) c -> p (t j) c", p=128),
                                   waits=[("CC", CC_P0), ("TE", w2ld[slot])],
                                   sem=("D0" if slot == 0 else "D1"))
                    for tci, (c0, c1) in enumerate(TC4):
                        wdt = c1 - c0
                        pi = 4 + (o * 4 + tci) % 2
                        ps = ps1[pi]
                        tv = None
                        first = True
                        for tau in range(3):
                            for j in range(n_in):
                                tv = P.emit(
                                    "tensor",
                                    (lambda e, tau=tau, j=j, o=o, c0=c0, wdt=wdt, ps=ps,
                                     slot=slot, srcC=srcC, n_in=n_in:
                                     e.matmul(ps[:, 0:wdt],
                                              w2s[:, slot * 3072 + (tau * n_in + j) * 128:
                                                  slot * 3072 + (tau * n_in + j + 1) * 128],
                                              srcC[:, j * C1L + c0 + tau: j * C1L + c0 + tau + wdt],
                                              start=(tau == 0 and j == 0), stop=False)),
                                    waits=([("D0" if slot == 0 else "D1", lw),
                                            ("VE", VE_HM), ps1_ev[pi]]
                                           if first else []),
                                    inc="TE")
                                first = False
                        for mi in range(mean_n):
                            tv = P.emit(
                                "tensor",
                                (lambda e, mi=mi, c0=c0, wdt=wdt, ps=ps,
                                 mean_l=mean_l, mean_src_t0=mean_src_t0:
                                 e.matmul(ps[:, 0:wdt],
                                          mean_l[:, :],
                                          qkvT[:, mean_src_t0 + mi * CL + 8 + c0:
                                               mean_src_t0 + mi * CL + 8 + c0 + wdt],
                                          start=False, stop=(mi == mean_n - 1))),
                                inc="TE")
                        ev = P.emit(
                            "scalar",
                            (lambda e, o=o, c0=c0, wdt=wdt, ps=ps, dstT=dstT, biasT=biasT:
                             e.activation(dstT[:, o * TOK + c0: o * TOK + c0 + wdt],
                                          ps[:, 0:wdt], AF.Identity,
                                          bias=biasT[:, o:o + 1], scale=1.0)),
                            waits=[("TE", tv)], inc="SE")
                        ps1_ev[pi] = ("SE", ev)
                    w2ld[slot] = P.n("TE")
            conv2(8, 8, c1q, qmix, None, cq2b, cmk, 8 * CL, 2)
            conv2(2, 2, c1k, kmix, kc2T_d, ck2b, cmq, 0, 8)
            SE_C2 = P.n("SE")
            pa2a.close()
            P.barrier()
            qhatT = pa2.enter_context(nc.sbuf_tensor("qhatT", [128, 8 * TOK], BF16))
            khatT = pa2.enter_context(nc.sbuf_tensor("khatT", [128, 2 * TOK], BF16))
            vstage = pa2.enter_context(nc.sbuf_tensor("vstage", [128, 4 * 16 * 65], BF16))
            sqs = pa2.enter_context(nc.sbuf_tensor("sqs", [128, 1024], F32))
            rsd = pa2.enter_context(nc.sbuf_tensor("rsd", [2, 2 * TOK], F32))
            rbb = pa2.enter_context(nc.sbuf_tensor("rbb", [128, 2 * 512], BF16))
            rtb = pa2.enter_context(nc.sbuf_tensor("rtb", [128, 2 * 512], BF16))
            tm1 = pa2.enter_context(nc.sbuf_tensor("tm1", [128, 2 * 512], BF16))
            if "qmix" in DEBUG_TAPS:
                mktap("qmix", [LAT, TOK], BF16)
                P.dma("gpsimd", taps["qmix"].ap().rearrange("(j p) t -> p j t", p=128),
                      qmix[:, :].rearrange("p (j t) -> p j t", t=TOK),
                      waits=[("SE", SE_C2)], sem="ST")

            def heads(n_t, srcM, dstT, is_k):
                for r in range(n_t):
                    rr = r % 2
                    for tci, (c0, c1) in enumerate(TC4):
                        sl2 = tci % 2
                        sqv = P.emit("scalar",
                                     (lambda e, r=r, c0=c0, c1=c1, sl2=sl2, srcM=srcM:
                                      e.activation(sqs[:, sl2 * 512:(sl2 + 1) * 512],
                                                   srcM[:, r * TOK + c0: r * TOK + c1],
                                                   AF.Square, bias=0.0, scale=1.0)),
                                     waits=[("SE", SE_C2), ("TE", P.n("TE"))], inc="SE")
                        pi = tci % 2
                        ps = ps1[pi]
                        tv = P.emit("tensor",
                                    (lambda e, c0=c0, c1=c1, ps=ps, sl2=sl2:
                                     e.matmul(ps[0:2, 0:c1 - c0], cbl1[:, :],
                                              sqs[:, sl2 * 512:(sl2 + 1) * 512],
                                              start=True, stop=True)),
                                    waits=[("SE", sqv), ps1_ev[pi]], inc="TE")
                        sn = P.emit("scalar",
                                    (lambda e, c0=c0, c1=c1, ps=ps, rr=rr:
                                     e.activation(rsd[0:2, rr * TOK + c0: rr * TOK + c1],
                                                  ps[0:2, 0:c1 - c0],
                                                  AF.Sqrt, bias=0.0, scale=1.0)),
                                    waits=[("TE", tv)], inc="SE")
                        ps1_ev[pi] = ("SE", sn)
                    snv = P.n("SE")
                    vm = P.emit("vector",
                                (lambda e, rr=rr:
                                 e.tensor_scalar_max(rsd[0:2, rr * TOK:(rr + 1) * TOK],
                                                     rsd[0:2, rr * TOK:(rr + 1) * TOK], 1e-12)),
                                waits=[("SE", snv)], inc="VE")
                    vrc = P.emit("vector",
                                 (lambda e, rr=rr:
                                  e.reciprocal(rsd[0:2, rr * TOK:(rr + 1) * TOK],
                                               rsd[0:2, rr * TOK:(rr + 1) * TOK])),
                                 waits=[("VE", vm)], inc="VE")
                    if is_k:
                        vrc = P.emit("vector",
                                     (lambda e, rr=rr:
                                      e.tensor_scalar_mul(rsd[0:2, rr * TOK:(rr + 1) * TOK],
                                                          rsd[0:2, rr * TOK:(rr + 1) * TOK],
                                                          ctmp8[0:2, 0:1])),
                                     waits=[("VE", vrc)], inc="VE")
                    for tci, (c0, c1) in enumerate(TC4):
                        wdt = c1 - c0
                        pib = 2 + tci % 2
                        psb = ps1[pib]
                        tb = P.emit("tensor",
                                    (lambda e, rr=rr, c0=c0, c1=c1, psb=psb:
                                     e.matmul(psb[:, 0:c1 - c0], cb2[:, :],
                                              rsd[0:2, rr * TOK + c0: rr * TOK + c1],
                                              start=True, stop=True)),
                                    waits=[("VE", vrc), ps1_ev[pib]], inc="TE")
                        sl = tci % 2
                        vb = P.emit("vector",
                                    (lambda e, psb=psb, sl=sl, wdt=wdt:
                                     e.tensor_copy(rbb[:, sl * 512: sl * 512 + wdt],
                                                   psb[:, 0:wdt])),
                                    waits=[("TE", tb)], inc="VE")
                        ps1_ev[pib] = ("VE", vb)
                        vq = P.emit("vector",
                                    (lambda e, r=r, c0=c0, wdt=wdt, sl=sl, srcM=srcM:
                                     e.tensor_mul(tm1[:, sl * 512: sl * 512 + wdt],
                                                  srcM[:, r * TOK + c0: r * TOK + c0 + wdt],
                                                  rbb[:, sl * 512: sl * 512 + wdt])),
                                    waits=[("VE", vb)], inc="VE")
                        pir = 4 + tci % 2
                        psr = ps1[pir]
                        tr = P.emit("tensor",
                                    (lambda e, sl=sl, wdt=wdt, psr=psr:
                                     e.matmul(psr[:, 0:wdt], crot[:, :],
                                              tm1[:, sl * 512: sl * 512 + wdt],
                                              start=True, stop=True)),
                                    waits=[("VE", vq), ps1_ev[pir]], inc="TE")
                        vrt = P.emit("vector",
                                     (lambda e, sl=sl, wdt=wdt, psr=psr:
                                      e.tensor_copy(rtb[:, sl * 512: sl * 512 + wdt],
                                                    psr[:, 0:wdt])),
                                     waits=[("TE", tr)], inc="VE")
                        ps1_ev[pir] = ("VE", vrt)
                        v1 = P.emit("vector",
                                    (lambda e, sl=sl, wdt=wdt, c0=c0:
                                     e.tensor_mul(tm1[:, sl * 512: sl * 512 + wdt],
                                                  tm1[:, sl * 512: sl * 512 + wdt],
                                                  ccos[:, c0:c0 + wdt])),
                                    waits=[("VE", vrt)], inc="VE")
                        v2 = P.emit("vector",
                                    (lambda e, sl=sl, wdt=wdt, c0=c0:
                                     e.tensor_mul(rtb[:, sl * 512: sl * 512 + wdt],
                                                  rtb[:, sl * 512: sl * 512 + wdt],
                                                  csin[:, c0:c0 + wdt])),
                                    inc="VE")
                        v3 = P.emit("vector",
                                    (lambda e, r=r, sl=sl, wdt=wdt, c0=c0, dstT=dstT:
                                     e.tensor_add(dstT[:, r * TOK + c0: r * TOK + c0 + wdt],
                                                  tm1[:, sl * 512: sl * 512 + wdt],
                                                  rtb[:, sl * 512: sl * 512 + wdt])),
                                    inc="VE")
            heads(8, qmix, qhatT, False)
            heads(2, kmix, khatT, True)
            VE_HEAD = P.n("VE")

            psv = [pa2.enter_context(nc.psum_tensor(f"a2_pv{i}", [128, 128], BF16))
                   for i in range(2)]
            psv_ve = [0, 0]
            for tt in range(16):
                for vt in range(2):
                    si = (tt * 2 + vt) % 2
                    ps = psv[si]
                    shift = 0 if vt == 0 else -1
                    tv = P.emit("tensor",
                                (lambda e, tt=tt, vt=vt, ps=ps, shift=shift:
                                 e.transpose(ps[:, :],
                                             qkvT[:, (10 + vt) * CL + 8 + tt * 128 + shift:
                                                  (10 + vt) * CL + 8 + (tt + 1) * 128 + shift],
                                             cid[:, :])),
                                waits=[("SE", SE_QKV), ("VE", psv_ve[si])], inc="TE")
                    for kvh in range(2):
                        k4 = vt * 2 + kvh
                        vv = P.emit("vector",
                                    (lambda e, tt=tt, ps=ps, kvh=kvh, k4=k4:
                                     e.tensor_copy(
                                         vstage[:, k4 * 1040 + tt * 65: k4 * 1040 + tt * 65 + 64],
                                         ps[:, kvh * 64:(kvh + 1) * 64])),
                                    waits=[("TE", tv)], inc="VE")
                    psv_ve[si] = P.n("VE")
            vo = None
            for k4 in range(4):
                for tt in range(16):
                    vo = P.emit("vector",
                                (lambda e, k4=k4, tt=tt:
                                 e.memset(vstage[:, k4 * 1040 + tt * 65 + 64:
                                                 k4 * 1040 + tt * 65 + 65], 1.0)),
                                inc="VE")
            VE_V = P.n("VE")

            for j8 in range(8):
                k4 = j8 % 4
                for t2 in range(2):
                    P.dma("gpsimd",
                          a1_in.ap()[j8, OFF_Q + t2 * 128 * TOK: OFF_Q + (t2 + 1) * 128 * TOK]
                          .rearrange("(p t) -> p t", p=128),
                          qhatT[:, (2 * k4 + t2) * TOK:(2 * k4 + t2 + 1) * TOK],
                          waits=[("VE", VE_HEAD)], sem="GD")
                P.dma("gpsimd",
                      a1_in.ap()[j8, OFF_K: OFF_K + 64 * TOK].rearrange("(p t) -> p t", p=64),
                      khatT[64 * (k4 % 2): 64 * (k4 % 2) + 64,
                            (k4 // 2) * TOK:(k4 // 2 + 1) * TOK],
                      waits=[("VE", VE_HEAD)], sem="GD")
                P.dma("gpsimd",
                      a1_in.ap()[j8, OFF_V: OFF_V + TOK * 65]
                      .rearrange("(tt p c) -> p tt c", p=128, c=65),
                      vstage[:, k4 * 1040:(k4 + 1) * 1040]
                      .rearrange("p (tt c) -> p tt c", c=65),
                      waits=[("VE", VE_V)], sem="GD")
            GD_A1 = P.n("GD")

    if PHASES < 3:
        raise _StopBuild()
    cc1 = P.emit("gpsimd",
                 (lambda e: e.collective_compute(
                     "AllToAll", ALU.bypass, replica_groups=groups,
                     ins=[a1_in.ap().opt()], outs=[a1_out.ap().opt()])),
                 waits=[("GD", P.n("GD"))], inc="CC")
    P.barrier()

    with ExitStack() as pb:
        qB = pb.enter_context(nc.sbuf_tensor("qB", [128, 2 * S], BF16))
        Ksb = pb.enter_context(nc.sbuf_tensor("Ksb", [128, S], BF16))
        Vsb = pb.enter_context(nc.sbuf_tensor("Vsb", [128, 64 * 65], BF16))
        probs = pb.enter_context(nc.sbuf_tensor("probs", [128, 4 * 1024], BF16))
        attnL = pb.enter_context(nc.sbuf_tensor("attnL", [128, 2 * S], BF16))
        rcs = pb.enter_context(nc.sbuf_tensor("rcs", [1, 512], F32))
        bcs = pb.enter_context(nc.sbuf_tensor("bcs", [64, 512], F32))
        cones1b = pb.enter_context(nc.sbuf_tensor("cones1b", [128, 64], BF16))
        pss = [pb.enter_context(nc.psum_tensor(f"b_s{i}", [128, 1024], F32))
               for i in range(2)]
        psa = [[pb.enter_context(nc.psum_tensor(f"b_a{i}{h}", [128, 512], F32))
                for h in range(2)] for i in range(2)]

        pbsh = ExitStack()
        qB2 = pbsh.enter_context(nc.sbuf_tensor("qB2", [128, 2 * S], BF16))
        Ksb2 = pbsh.enter_context(nc.sbuf_tensor("Ksb2", [128, S], BF16))
        Vsb2 = pbsh.enter_context(nc.sbuf_tensor("Vsb2", [128, 64 * 65], BF16))
        def _ld_k(sem):
            for half, d in [(0, Ksb), (1, Ksb2)]:
                for i in range(4):
                    blk = half * 4 + i
                    for hh in range(2):
                        P.dma("sync",
                              d[64 * hh:64 * hh + 64, i * TOK:(i + 1) * TOK],
                              a1_out.ap()[blk, OFF_K: OFF_K + 64 * TOK]
                              .rearrange("(p t) -> p t", p=64),
                              waits=[("CC", cc1)], sem=sem)

        def _ld_q(t2, sem):
            for half, d in [(0, qB), (1, qB2)]:
                for i in range(4):
                    blk = half * 4 + i
                    for hh in range(2):
                        P.dma("sync",
                              d[64 * hh:64 * hh + 64,
                                t2 * S + i * TOK: t2 * S + (i + 1) * TOK],
                              a1_out.ap()[blk, OFF_Q + (t2 * 2 + hh) * 64 * TOK:
                                          OFF_Q + (t2 * 2 + hh + 1) * 64 * TOK]
                              .rearrange("(p t) -> p t", p=64),
                              waits=[("CC", cc1)], sem=sem)

        def _ld_v(sem):
            for half, d in [(0, Vsb), (1, Vsb2)]:
                for i in range(4):
                    blk = half * 4 + i
                    P.dma("sync",
                          d[:, i * 16 * 65:(i + 1) * 16 * 65]
                          .rearrange("p (tt c) -> p tt c", c=65),
                          a1_out.ap()[blk, OFF_V: OFF_V + TOK * 65]
                          .rearrange("(tt p c) -> p tt c", p=128, c=65),
                          waits=[("CC", cc1)], sem=sem)

        _ld_k("D2")
        KLD = P.n("D2")
        _ld_q(0, "D3")
        Q0LD = P.n("D3")
        _ld_v("SA")
        VLD = P.n("SA")
        _ld_q(1, "SB")
        Q1LD = P.n("SB")

        def _sel(d, d2, c0, w, waits):
            P.emit("vector",
                   (lambda e: e.tensor_scalar_mul(d[:, c0:c0 + w], d[:, c0:c0 + w],
                                                  cbsel[:, 0:1])),
                   waits=waits, inc="VE")
            return P.emit("vector",
                          (lambda e: e.scalar_tensor_tensor(d[:, c0:c0 + w],
                                                            d2[:, c0:c0 + w],
                                                            cbsel[:, 1:2],
                                                            d[:, c0:c0 + w],
                                                            ALU.mult, ALU.add)),
                          inc="VE")

        _sel(Ksb, Ksb2, 0, S, [("D2", KLD)])
        selv_q0 = _sel(qB, qB2, 0, S, [("D3", Q0LD)])
        _sel(Vsb, Vsb2, 0, 64 * 65, [("SA", VLD)])
        selv_q1 = _sel(qB, qB2, S, S, [("SB", Q1LD)])
        selv_pair = [selv_q0, selv_q1]

        pbsh.close()
        avstage = pb.enter_context(nc.sbuf_tensor("avstage", [128, 64 * 512], BF16))
        dtile = pb.enter_context(nc.sbuf_tensor("dtile", [128, 256], F32))
        P.emit("vector", lambda e: e.memset(cones1b[64:65, :], 1.0), inc="VE")
        CONB = P.n("VE")

        exp_done = [0, 0]
        probs_rd = [[0, 0], [0, 0]]
        av_done = [0, 0]
        for qc in range(16):
            n_kt = 4 * qc + 4
            pready = [None, None]
            for kt in range(n_kt + 1):
                pready_new = [None, None]
                if kt < n_kt:
                    diag = kt >= n_kt - 4
                    buf = kt % 2
                    for pair in range(2):
                        ps = pss[pair]
                        pidx = pair * 2 + buf
                        tv = None
                        for hh in range(2):
                            tv = P.emit(
                                "tensor",
                                (lambda e, pair=pair, qc=qc, kt=kt, hh=hh, ps=ps:
                                 e.matmul(ps[:, hh * 512:(hh + 1) * 512],
                                          Ksb[64 * hh:64 * hh + 64, kt * 128:(kt + 1) * 128],
                                          qB[64 * hh:64 * hh + 64,
                                             pair * S + qc * 512:pair * S + (qc + 1) * 512],
                                          start=True, stop=True)),
                                waits=(([("VE", selv_pair[pair])]
                                        if qc == 0 and kt == 0 else [])
                                       + [("SE", exp_done[pair])]
                                       if hh == 0 else []),
                                inc="TE")
                        ex = P.emit(
                            "scalar",
                            (lambda e, ps=ps, pidx=pidx:
                             e.activation(probs[:, pidx * 1024:(pidx + 1) * 1024],
                                          ps[:, :], AF.Exp,
                                          bias=cnbias[:, 0:1], scale=1.0)),
                            waits=[("TE", tv), ("TE", probs_rd[pair][buf])], inc="SE")
                        exp_done[pair] = ex
                        if diag:
                            v = kt - (n_kt - 4)
                            mv = 0
                            for hh in range(2):
                                mv = P.emit(
                                    "vector",
                                    (lambda e, pidx=pidx, hh=hh, v=v:
                                     e.tensor_mul(probs[:, pidx * 1024 + hh * 512:
                                                        pidx * 1024 + (hh + 1) * 512],
                                                  probs[:, pidx * 1024 + hh * 512:
                                                        pidx * 1024 + (hh + 1) * 512],
                                                  cmask01[:, v * 512:(v + 1) * 512])),
                                    waits=[("SE", ex)], inc="VE")
                            pready_new[pair] = ("VE", mv)
                        else:
                            pready_new[pair] = ("SE", ex)
                if kt >= 1:
                    akt = kt - 1
                    abuf = akt % 2
                    for pair in range(2):
                        pidx = pair * 2 + abuf
                        pr = pready[pair]
                        for hh in range(2):
                            P.emit(
                                "tensor",
                                (lambda e, akt=akt, hh=hh, pair=pair, pidx=pidx, n_kt=n_kt:
                                 e.matmul(psa[pair][hh][0:65, :],
                                          Vsb[:, akt * 65:(akt + 1) * 65],
                                          probs[:, pidx * 1024 + hh * 512:
                                                pidx * 1024 + (hh + 1) * 512],
                                          start=(akt == 0), stop=(akt == n_kt - 1))),
                                waits=([pr, ("VE", av_done[pair])]
                                       if akt == 0 else [pr]),
                                inc="TE")
                        probs_rd[pair][abuf] = P.n("TE")
                if kt < n_kt:
                    pready = pready_new
            TE_AV = P.n("TE")
            for pair in range(2):
                for hh in range(2):
                    t4 = qc * 4 + pair * 2 + hh
                    av_done[pair] = P.emit(
                        "vector",
                        (lambda e, pair=pair, hh=hh, t4=t4:
                         e.tensor_copy(avstage[0:65, t4 * 512:(t4 + 1) * 512],
                                       psa[pair][hh][0:65, :])),
                        waits=[("TE", TE_AV)], inc="VE")
        VE_EV = P.n("VE")
        P.barrier()

        dge = P.emit(
            "gpsimd",
            (lambda e: e.dma_start(out=dn_d[:], in_=avstage[64:65, :])),
            waits=[("VE", VE_EV)], inc="GD", by=16)
        dge2 = P.emit(
            "gpsimd",
            (lambda e: e.dma_start(out=dtile[:, :],
                                   in_=dn_d.ap().rearrange("(p c) -> p c", c=256))),
            waits=[("GD", P.n("GD"))], inc="GD", by=16)
        vrc = P.emit("vector",
                     (lambda e: e.reciprocal(dtile[:, :], dtile[:, :])),
                     waits=[("GD", P.n("GD"))], inc="VE")
        dge3 = P.emit(
            "gpsimd",
            (lambda e: e.dma_start(out=dn_d.ap().rearrange("(p c) -> p c", c=256),
                                   in_=dtile[:, :])),
            waits=[("VE", vrc)], inc="GD", by=16)
        dge4 = P.emit(
            "gpsimd",
            (lambda e: e.dma_start(out=avstage[64:65, :], in_=dn_d[:])),
            waits=[("GD", P.n("GD"))], inc="GD", by=16)
        GD_R = P.n("GD")
        bc_ve = [0, 0]
        for t4 in range(64):
            qc, rem = divmod(t4, 4)
            pair, hh = divmod(rem, 2)
            tb = P.emit("tensor",
                        (lambda e, pair=pair, t4=t4:
                         e.matmul(pss[pair][0:64, 0:512], cones1b[64:65, :],
                                  avstage[64:65, t4 * 512:(t4 + 1) * 512],
                                  start=True, stop=True)),
                        waits=[("GD", GD_R), ("SE", exp_done[pair]),
                               ("VE", max(bc_ve[pair], CONB))], inc="TE")
            vb = P.emit("vector",
                        (lambda e, pair=pair:
                         e.tensor_copy(bcs[:, :], pss[pair][0:64, 0:512])),
                        waits=[("TE", tb)], inc="VE")
            P.emit("vector",
                   (lambda e, pair=pair, qc=qc, hh=hh, t4=t4:
                    e.tensor_mul(attnL[64 * hh:64 * hh + 64,
                                       pair * S + qc * 512:pair * S + (qc + 1) * 512],
                                 avstage[0:64, t4 * 512:(t4 + 1) * 512],
                                 bcs[:, :])),
                   inc="VE")
            bc_ve[pair] = P.n("VE")
        P.barrier()
        VE_B = P.n("VE")
        if "attnL" in DEBUG_TAPS:
            mktap("attnL", [256, S], BF16)
            P.dma("gpsimd", taps["attnL"].ap().rearrange("(j p) t -> p j t", p=128),
                  attnL[:, :].rearrange("p (j t) -> p j t", t=S),
                  waits=[("VE", VE_B)], sem="ST")

        for j8 in range(8):
            k4 = j8 % 4
            for t2 in range(2):
                for hh in range(2):
                    P.dma("gpsimd",
                          a2_in.ap()[j8, (t2 * 2 + hh) * 64 * TOK:
                                     (t2 * 2 + hh + 1) * 64 * TOK]
                          .rearrange("(p t) -> p t", p=64),
                          attnL[64 * hh:64 * hh + 64,
                                t2 * S + k4 * TOK: t2 * S + (k4 + 1) * TOK],
                          waits=[("VE", VE_B)], sem="GD")

    GD_A2C = P.n("GD")
    P.barrier()
    cc2 = P.emit("gpsimd",
                 (lambda e: e.collective_compute(
                     "AllToAll", ALU.bypass, replica_groups=groups,
                     ins=[a2_in.ap().opt()], outs=[a2_out.ap().opt()])),
                 waits=[("GD", GD_A2C)], inc="CC")
    if "xnTd" in DEBUG_TAPS:
        mktap("xnTd", [D, CL], BF16)
        P.dma("gpsimd", taps["xnTd"][:, :], xnT_d[:, :],
              waits=[("CC", P.n("CC"))], sem="ST")
    if "a2o" in DEBUG_TAPS:
        mktap("a2o", [8, SH2_W], BF16)
        P.dma("gpsimd", taps["a2o"][:, :], a2_out[:, :],
              waits=[("CC", P.n("CC"))], sem="ST")

    if PHASES < 4:
        raise _StopBuild()
    with ExitStack() as pc:
        xnC = pc.enter_context(nc.sbuf_tensor("xnC", [128, 32 * 1024], BF16))
        afH = pc.enter_context(nc.sbuf_tensor("afH", [128, 8 * 1024], BF16))
        afH2 = pc.enter_context(nc.sbuf_tensor("afH2", [128, 8 * 1024], BF16))
        wts = pc.enter_context(nc.sbuf_tensor("wts", [128, 2 * 32 * 128], BF16))
        wos = pc.enter_context(nc.sbuf_tensor("wos", [128, 2 * 8 * 128], BF16))
        gsb = pc.enter_context(nc.sbuf_tensor("gsb", [128, 1024], F32))
        gstage = pc.enter_context(nc.sbuf_tensor("gstage", [128, 8 * 1024], BF16))
        sgs = pc.enter_context(nc.sbuf_tensor("sgs", [128, 4 * 1024], F32))
        xcs = pc.enter_context(nc.sbuf_tensor("xcs", [128, 2 * 512], F32))
        osb = pc.enter_context(nc.sbuf_tensor("osb", [128, 2 * 512], F32))
        psg = [pc.enter_context(nc.psum_tensor(f"c_g{i}", [128, 512], F32)) for i in range(2)]
        pso = [pc.enter_context(nc.psum_tensor(f"c_s{i}", [128, 512], F32)) for i in range(2)]
        pstp = [pc.enter_context(nc.psum_tensor(f"c_t{i}", [128, 512], F32)) for i in range(2)]

        psg_ev = [0, 0]
        pso_ev = [0, 0]
        gsb_ev = [0, 0]
        pst_ev = [0, 0]
        xcs_ev = [0, 0]
        ost_st = [0, 0]
        wldt = [0, 0]
        wldo = [0, 0]
        GPRE = 8
        te_prevH = 0
        for H in range(2):
            lh = P.dma("sync", xnC[:, :].rearrange("p (j t) -> p j t", t=1024),
                       xnT_d.ap().rearrange("(j p) t -> p j t", p=128)[:, :, 8 + H * 1024:
                                                                      8 + (H + 1) * 1024],
                       waits=[("TE", te_prevH)])
            for ot in range(GPRE):
                slot = ot % 2
                dsl = "D0" if slot == 0 else "D1"
                lw = None
                for rr8 in range(8):
                    lw = P.dma("sync",
                               wts[:, slot * 4096 + rr8 * 512:
                                   slot * 4096 + (rr8 + 1) * 512]
                               .rearrange("p (b c) -> p b c", c=128),
                               pp_out.ap()[rr8, OF:OF + 2097152]
                               .rearrange("(b p c) -> p b c", p=128, c=4096)
                               [:, :, ot * 128:(ot + 1) * 128],
                               waits=([("TE", wldt[slot])] if rr8 == 0 else []),
                               sem=dsl)
                for tc in range(2):
                    ps = psg[tc]
                    tv = None
                    for j in range(32):
                        tv = P.emit(
                            "tensor",
                            (lambda e, j=j, tc=tc, ps=ps, slot=slot:
                             e.matmul(ps[:, :],
                                      wts[:, slot * 4096 + j * 128: slot * 4096 + (j + 1) * 128],
                                      xnC[:, j * 1024 + tc * 512: j * 1024 + (tc + 1) * 512],
                                      start=(j == 0), stop=(j == 31))),
                            waits=([(dsl, lw), ("LD", lh), ("SE", psg_ev[tc])]
                                   if j == 0 else []),
                            inc="TE")
                    sg_ = P.emit(
                        "scalar",
                        (lambda e, ot=ot, tc=tc, ps=ps:
                         e.activation(gsb[:, tc * 512:(tc + 1) * 512], ps[:, :],
                                      AF.Silu, bias=cfb[:, ot:ot + 1], scale=1.0)),
                        waits=[("TE", tv), ("VE", gsb_ev[tc])], inc="SE")
                    psg_ev[tc] = sg_
                    gsb_ev[tc] = P.emit(
                        "vector",
                        (lambda e, ot=ot, tc=tc:
                         e.tensor_copy(gstage[:, ot * 1024 + tc * 512:
                                              ot * 1024 + (tc + 1) * 512],
                                       gsb[:, tc * 512:(tc + 1) * 512])),
                        waits=[("SE", sg_)], inc="VE")
                wldt[slot] = P.n("TE")
            for half, d in [(0, afH), (1, afH2)]:
                for i in range(4):
                    for ci in range(2):
                        P.dma("gpsimd",
                              d[:, (2 * i + ci) * 1024:(2 * i + ci + 1) * 1024],
                              a2_out.ap().rearrange("r (c p t) -> r c p t", p=128, t=TOK)
                              [half * 4 + i, ci, :, H * 1024:(H + 1) * 1024],
                              waits=[("CC", cc2), ("TE", te_prevH)])
            LD_C0 = P.n("GD")
            P.emit("vector",
                   (lambda e: e.tensor_scalar_mul(afH[:, :], afH[:, :], cbsel[:, 0:1])),
                   waits=[("GD", LD_C0), ("TE", te_prevH)], inc="VE")
            afsel = P.emit("vector",
                           (lambda e: e.scalar_tensor_tensor(afH[:, :], afH2[:, :],
                                                             cbsel[:, 1:2], afH[:, :],
                                                             ALU.mult, ALU.add)),
                           inc="VE")
            afl = LD_C0
            for ot in range(32):
                slot = ot % 2
                dsl = "D0" if slot == 0 else "D1"
                wsl = "W0" if slot == 0 else "W1"
                if ot >= GPRE:
                    lw = None
                    for rr8 in range(8):
                        lw = P.dma("sync",
                                   wts[:, slot * 4096 + rr8 * 512:
                                       slot * 4096 + (rr8 + 1) * 512]
                                   .rearrange("p (b c) -> p b c", c=128),
                                   pp_out.ap()[rr8, OF:OF + 2097152]
                                   .rearrange("(b p c) -> p b c", p=128, c=4096)
                                   [:, :, ot * 128:(ot + 1) * 128],
                                   waits=([("TE", wldt[slot])] if rr8 == 0 else []),
                                   sem=dsl)
                lwo = P.dma("sync",
                            wos[:, slot * 1024:(slot + 1) * 1024]
                            .rearrange("p (j c) -> p j c", c=128),
                            pp_out.ap()[:, OW:OW + 524288]
                            .rearrange("r (p c) -> p r c", p=128, c=4096)
                            [:, :, ot * 128:(ot + 1) * 128],
                            waits=[("TE", wldo[slot])], sem=wsl)
                for tc in range(2):
                    if ot >= GPRE:
                        ps = psg[tc]
                        tv = None
                        for j in range(32):
                            tv = P.emit(
                                "tensor",
                                (lambda e, j=j, tc=tc, ps=ps, slot=slot:
                                 e.matmul(ps[:, :],
                                          wts[:, slot * 4096 + j * 128: slot * 4096 + (j + 1) * 128],
                                          xnC[:, j * 1024 + tc * 512: j * 1024 + (tc + 1) * 512],
                                          start=(j == 0), stop=(j == 31))),
                                waits=([(dsl, lw), ("LD", lh), ("SE", psg_ev[tc])]
                                       if j == 0 else []),
                                inc="TE")
                        sg_ = P.emit(
                            "scalar",
                            (lambda e, ot=ot, tc=tc, ps=ps:
                             e.activation(gsb[:, tc * 512:(tc + 1) * 512], ps[:, :],
                                          AF.Silu, bias=cfb[:, ot:ot + 1], scale=1.0)),
                            waits=[("TE", tv), ("VE", gsb_ev[tc])], inc="SE")
                        psg_ev[tc] = sg_
                    pso_ = pso[tc]
                    tvo = None
                    for j in range(8):
                        tvo = P.emit(
                            "tensor",
                            (lambda e, j=j, tc=tc, pso_=pso_, slot=slot:
                             e.matmul(pso_[:, :],
                                      wos[:, slot * 1024 + j * 128: slot * 1024 + (j + 1) * 128],
                                      afH[:, j * 1024 + tc * 512: j * 1024 + (tc + 1) * 512],
                                      start=(j == 0), stop=(j == 7))),
                            waits=([(wsl, lwo), ("GD", afl), ("VE", afsel), ("VE", pso_ev[tc])]
                                   if j == 0 else []),
                            inc="TE")
                    if ot >= GPRE:
                        vsg = P.emit(
                            "vector",
                            (lambda e, ot=ot, tc=tc, pso_=pso_:
                             e.tensor_mul(sgs[:, (ot % 4) * 1024 + tc * 512:
                                              (ot % 4) * 1024 + (tc + 1) * 512],
                                          pso_[:, :], gsb[:, tc * 512:(tc + 1) * 512])),
                            waits=[("TE", tvo), ("SE", sg_)], inc="VE")
                        gsb_ev[tc] = vsg
                    else:
                        vsg = P.emit(
                            "vector",
                            (lambda e, ot=ot, tc=tc, pso_=pso_:
                             e.tensor_mul(sgs[:, (ot % 4) * 1024 + tc * 512:
                                              (ot % 4) * 1024 + (tc + 1) * 512],
                                          pso_[:, :],
                                          gstage[:, ot * 1024 + tc * 512:
                                                 ot * 1024 + (tc + 1) * 512])),
                            waits=[("TE", tvo)], inc="VE")
                    pso_ev[tc] = vsg
                if ot >= GPRE:
                    wldt[slot] = P.n("TE")
                wldo[slot] = P.n("TE")
                if ot % 4 == 3:
                    oc = ot // 4
                    VSG = P.n("VE")
                    for tc in range(2):
                        for tb in range(4):
                            sl = (tc * 4 + tb) % 2
                            pt = pstp[sl]
                            xsem = "D2" if sl == 0 else "D3"
                            ssem = "SA" if sl == 0 else "SB"
                            lx = P.dma("scalar", xcs[:, sl * 512:(sl + 1) * 512],
                                       x_loc[HALO + H * 1024 + tc * 512 + tb * 128:
                                             HALO + H * 1024 + tc * 512 + (tb + 1) * 128,
                                             oc * 512:(oc + 1) * 512],
                                       waits=[("VE", xcs_ev[sl])], sem=xsem)
                            tt = None
                            for i4 in range(4):
                                tt = P.emit(
                                    "tensor",
                                    (lambda e, i4=i4, tc=tc, tb=tb, pt=pt:
                                     e.transpose(pt[:, i4 * 128:(i4 + 1) * 128],
                                                 sgs[:, i4 * 1024 + tc * 512 + tb * 128:
                                                     i4 * 1024 + tc * 512 + (tb + 1) * 128],
                                                 cidf[:, :])),
                                    waits=[("VE", VSG), ("VE", pst_ev[sl])], inc="TE")
                            vv = P.emit(
                                "vector",
                                (lambda e, sl=sl, pt=pt:
                                 e.tensor_add(osb[:, sl * 512:(sl + 1) * 512],
                                              pt[:, :], xcs[:, sl * 512:(sl + 1) * 512])),
                                waits=[("TE", tt), (xsem, lx), (ssem, ost_st[sl])], inc="VE")
                            st = P.dma("gpsimd",
                                       out_loc[H * 1024 + tc * 512 + tb * 128:
                                               H * 1024 + tc * 512 + (tb + 1) * 128,
                                               oc * 512:(oc + 1) * 512],
                                       osb[:, sl * 512:(sl + 1) * 512],
                                       waits=[("VE", vv)], sem=ssem)
                            pst_ev[sl] = vv
                            xcs_ev[sl] = vv
                            ost_st[sl] = st
            te_prevH = P.n("TE")
    P.emit("gpsimd", None, waits=[("ST", P.n("ST")), ("SA", P.n("SA")),
                                  ("SB", P.n("SB"))])
    P.barrier()

    with nc.Block() as block:
        with allow_ici_all_to_all():
            _replay(P, block, sems)
    es.close()
    return nc, list(taps.keys())


_BUILT = {}


def _consts():
    ident = np.eye(128)
    rot = np.zeros((128, 128), np.float32)
    for i in range(64):
        rot[2 * i + 1, 2 * i] = -1.0
        rot[2 * i, 2 * i + 1] = 1.0
    meank = np.zeros((128, 128), np.float32)
    meanq = np.zeros((128, 128), np.float32)
    for kk in range(128):
        for m64 in range(2):
            meank[kk, m64 * 64 + kk % 64] = 0.125
            meanq[kk, m64 * 64 + kk % 64] = 0.03125
    b2 = np.zeros((2, 128), np.float32)
    b2[0, 0:64] = 1.0
    b2[1, 64:128] = 1.0
    m01 = np.zeros((128, 2048), np.float32)
    kk = np.arange(128)[:, None]
    qq = np.arange(512)[None, :]
    for v in range(4):
        m01[:, v * 512:(v + 1) * 512] = (qq >= kk + 128 * v).astype(np.float32)
    return ident, rot, meank, meanq, b2, m01


def kernel(**inputs):
    x = np.ascontiguousarray(np.asarray(inputs["L0_attn_x"], np.float32))
    ident, rot, meank, meanq, b2, m01 = _consts()
    qc1_b = np.asarray(inputs["L0_attn_qc1_b"], np.float32)
    qc2_b = np.asarray(inputs["L0_attn_qc2_b"], np.float32)
    kc1_b = np.asarray(inputs["L0_attn_kc1_b"], np.float32)
    kc2_b = np.asarray(inputs["L0_attn_kc2_b"], np.float32)
    scales = np.asarray(inputs["L0_ffn_scales"], np.float32)[:, :, 0]
    zeros = np.asarray(inputs["L0_ffn_scaled_zeros"], np.float32)[:, :, 0]
    fbias = np.asarray(inputs["L0_ffn_bias"], np.float32)
    ktemp = float(np.asarray(inputs["L0_attn_key_temp"], np.float32).reshape(-1)[0])

    inv = 1.0 / 10000.0 ** (np.arange(0, HD, 2, dtype=np.float32) / HD)

    wqkv_full = np.asarray(inputs["L0_attn_w_qkv"], np.float32)
    wo_full = np.asarray(inputs["L0_attn_w_o"], np.float32)
    qc2_full = np.asarray(inputs["L0_attn_qc2_w"], np.float32)
    qw_full = np.asarray(inputs["L0_ffn_qweight"], np.int32)
    scT_full = scales.reshape(32, 128, 32).transpose(1, 0, 2)
    zrT_full = zeros.reshape(32, 128, 32).transpose(1, 0, 2)
    shared = dict(
        qc1_w=np.ascontiguousarray(np.asarray(inputs["L0_attn_qc1_w"], np.float32)),
        kc1_w=np.ascontiguousarray(np.asarray(inputs["L0_attn_kc1_w"], np.float32)),
        kc2_w=np.ascontiguousarray(np.asarray(inputs["L0_attn_kc2_w"], np.float32)),
        qc1_bT=np.ascontiguousarray(qc1_b.reshape(16, 64).T),
        qc2_bT=np.ascontiguousarray(qc2_b.reshape(8, 128).T),
        kc1_bT=np.ascontiguousarray(kc1_b.reshape(4, 64).T),
        kc2_bT=np.ascontiguousarray(kc2_b.reshape(2, 128).T),

        ffn_biasT=np.ascontiguousarray(fbias.reshape(32, 128).T),
        key_temp_b=np.full((128, 1), ktemp, np.float32),
        ident_bf=ident.astype(BFNP),
        ident_f32=ident.astype(np.float32),
        rot_bf=rot.astype(BFNP),
        meank_l=meank.astype(BFNP),
        meanq_l=meanq.astype(BFNP),
        bcast2=b2,
        mask01=m01.astype(BFNP),
    )

    in_maps = []
    for c in range(8):
        b = c // 4
        p0 = (c % 4) * TOK
        xl = np.zeros((CL, D), np.float32)
        lo = max(0, p0 - HALO)
        xl[HALO - (p0 - lo):] = x[b, lo:p0 + TOK]
        pos = (p0 + np.arange(TOK, dtype=np.float32))[None, :]
        ang = pos * inv[(np.arange(128) % 64) // 2][:, None]
        m = dict(shared)
        m["w_qkv"] = np.ascontiguousarray(wqkv_full[:, 512 * c:512 * (c + 1)])
        m["w_o"] = np.ascontiguousarray(wo_full[:, 128 * c:128 * (c + 1)])
        m["qc2_w"] = np.ascontiguousarray(qc2_full[:, 128 * c:128 * (c + 1), :])
        m["qweight"] = np.ascontiguousarray(qw_full[:, 4 * c:4 * (c + 1), :])
        m["scalesT"] = np.ascontiguousarray(scT_full[:, :, 4 * c:4 * (c + 1)])
        m["zerosT"] = np.ascontiguousarray(zrT_full[:, :, 4 * c:4 * (c + 1)])
        m["x_loc"] = xl
        m["halo_mask"] = np.full((128, 1), 0.0 if p0 == 0 else 1.0, np.float32)
        bs = np.zeros((128, 2), np.float32)
        bs[:, 0 if b == 0 else 1] = 1.0
        m["bsel"] = bs
        m["cosT"] = np.cos(ang).astype(BFNP)
        m["sinT"] = np.sin(ang).astype(BFNP)
        in_maps.append(m)

    key = "prog"
    if key not in _BUILT:
        _BUILT[key] = build_program()
    nc, tapnames = _BUILT[key]
    res = run_bass_kernel_spmd(nc, in_maps, core_ids=list(range(8)))
    out = np.empty((B, S, D), np.float32)
    for c in range(8):
        out[c // 4, (c % 4) * TOK:(c % 4 + 1) * TOK] = res.results[c]["out_loc"]
    if DEBUG_TAPS:
        kernel.tap_results = res.results
    return out


if __name__ == "__main__":
    pass


# version 16 (final)
# speedup vs baseline: 1.2570x; 1.0164x over previous
import numpy as np
import ml_dtypes
from contextlib import ExitStack

import concourse.bass as bass
import concourse.mybir as mybir
from concourse.bass_utils import run_bass_kernel_spmd
from concourse.aot_env import allow_ici_all_to_all

BFNP = ml_dtypes.bfloat16
F32 = mybir.dt.float32
BF16 = mybir.dt.bfloat16
I32 = mybir.dt.int32
AF = mybir.ActivationFunctionType
ALU = mybir.AluOpType

B, S, D = 2, 8192, 4096
NH, NKV, HD, LAT, KVD = 16, 4, 64, 1024, 256
TOK = 2048
HALO = 8
CL = TOK + HALO
C1L = TOK + 2

OFF_Q = 0
OFF_K = 256 * TOK
OFF_V = OFF_K + 64 * TOK
SH_W = OFF_V + TOK * 65
SH2_W = 256 * TOK

DEBUG_TAPS = ()
PHASES = 4

TC5 = [(0, 512), (512, 1024), (1024, 1536), (1536, 2048), (2048, CL)]
TC5C = [(0, 512), (512, 1024), (1024, 1536), (1536, 2048), (2048, C1L)]
TC4 = [(0, 512), (512, 1024), (1024, 1536), (1536, 2048)]


class Plan:
    ENG = ("sync", "gpsimd", "tensor", "vector", "scalar")

    def __init__(self):
        self.ops = {e: [] for e in self.ENG}
        self.cnt = {}

    def n(self, sem):
        return self.cnt.get(sem, 0)

    def emit(self, eng, fn, waits=(), inc=None, by=1):
        if inc is not None:
            self.cnt[inc] = self.cnt.get(inc, 0) + by
        self.ops[eng].append((fn, tuple(waits), inc, by))
        return self.cnt.get(inc) if inc is not None else None

    def dma(self, queue, out, in_, waits=(), sem=None, **kw):
        if sem is None:
            sem = {"sync": "LD", "gpsimd": "GD", "scalar": "XD"}[queue]
        return self.emit(
            queue,
            (lambda e, out=out, in_=in_, kw=kw: e.dma_start(out=out, in_=in_, **kw)),
            waits, inc=sem, by=16)

    def barrier(self, engines=None):
        engines = engines or self.ENG
        vals = {}
        for e in engines:
            vals[e] = self.emit(e, (lambda eng: eng.nop()), (), inc="B" + e, by=1)
        for e in engines:
            waits = tuple(("B" + o, vals[o]) for o in engines if o != e)
            self.emit(e, None, waits)


def _replay(plan, block, sems):
    def mk(ename):
        def body(eng):
            hw = {}
            for fn, waits, inc, by in plan.ops[ename]:
                for s, v in waits:
                    if v > hw.get(s, 0):
                        eng.wait_ge(sems[s], v)
                        hw[s] = v
                if fn is not None:
                    ins = fn(eng)
                    if inc is not None:
                        ins.then_inc(sems[inc], by)
        return body
    block.sync(mk("sync"))
    block.gpsimd(mk("gpsimd"))
    block.tensor(mk("tensor"))
    block.vector(mk("vector"))
    block.scalar(mk("scalar"))


class _StopBuild(Exception):
    pass


_CTX = {}


def build_program():
    try:
        return _build_inner()
    except _StopBuild:
        nc, P, sems, es, taps = (_CTX["nc"], _CTX["P"], _CTX["sems"],
                                 _CTX["es"], _CTX["taps"])
        P.emit("gpsimd", None, waits=[("ST", P.n("ST")), ("SA", P.n("SA")),
                                      ("SB", P.n("SB"))])
        P.barrier()
        with nc.Block() as block:
            with allow_ici_all_to_all():
                _replay(P, block, sems)
        es.close()
        return nc, list(taps.keys())


def _build_inner():
    nc = bass.Bass("TRN2", target_bir_lowering=False, debug=False)
    P = Plan()

    def I(name, shape, dt):
        return nc.dram_tensor(name, shape, dt, kind="ExternalInput")

    x_loc = I("x_loc", [CL, D], F32)
    w_qkv = I("w_qkv", [1536, 512], F32)
    w_o = I("w_o", [D, 128], F32)
    qc1_w = I("qc1_w", [LAT, HD, 3], F32)
    qc2_w = I("qc2_w", [LAT, 128, 3], F32)
    kc1_w = I("kc1_w", [KVD, HD, 3], F32)
    kc2_w = I("kc2_w", [KVD, KVD, 3], F32)
    qc1_bT = I("qc1_bT", [64, 16], F32)
    qc2_bT = I("qc2_bT", [128, 8], F32)
    kc1_bT = I("kc1_bT", [64, 4], F32)
    kc2_bT = I("kc2_bT", [128, 2], F32)
    qweight = I("qweight", [D, 4, 128], I32)
    scalesT = I("scalesT", [128, 32, 4], F32)
    zerosT = I("zerosT", [128, 32, 4], F32)
    ffn_biasT = I("ffn_biasT", [128, 32], F32)
    key_temp_b = I("key_temp_b", [128, 1], F32)
    halo_mask = I("halo_mask", [128, 1], F32)
    ident_bf = I("ident_bf", [128, 128], BF16)
    ident_f32 = I("ident_f32", [128, 128], F32)
    rot_bf = I("rot_bf", [128, 128], BF16)
    meank_l = I("meank_l", [128, 128], BF16)
    meanq_l = I("meanq_l", [128, 128], BF16)
    bcast2 = I("bcast2", [2, 128], F32)
    cosT = I("cosT", [128, TOK], BF16)
    sinT = I("sinT", [128, TOK], BF16)
    mask01 = I("mask01", [128, 2048], BF16)
    bsel = I("bsel", [128, 2], F32)

    out_loc = nc.dram_tensor("out_loc", [TOK, D], F32, kind="ExternalOutput")
    taps = {}
    def mktap(name, shape, dt):
        if name in DEBUG_TAPS:
            taps[name] = nc.dram_tensor("tap_" + name, shape, dt, kind="ExternalOutput")

    kc2T_d = nc.dram_tensor("kc2T_d", [3, KVD, KVD], BF16)
    OQ, OW, OC2, OF = 0, 786432, 1310720, 1703936
    PSL = 3801088
    pp_in = nc.dram_tensor("pp_in", [1, PSL], BF16)
    pp_out = nc.dram_tensor("pp_out", [8, PSL], BF16)
    xnT_d = nc.dram_tensor("xnT_d", [D, CL], BF16)
    a1_in = nc.dram_tensor("a1_in", [8, SH_W], BF16)
    a1_out = nc.dram_tensor("a1_out", [8, SH_W], BF16)
    a2_in = nc.dram_tensor("a2_in", [8, SH2_W], BF16)
    a2_out = nc.dram_tensor("a2_out", [8, SH2_W], BF16)
    dn_d = nc.dram_tensor("dn_d", [64 * 512], BF16)
    groups = [[0, 1, 2, 3, 4, 5, 6, 7]]

    es = ExitStack()
    sems = {}
    for s in ["LD", "GD", "ST", "XD", "CC", "TE", "VE", "SE",
              "D0", "D1", "D2", "D3", "SA", "SB", "W0", "W1",
              "Bsync", "Bgpsimd", "Btensor", "Bvector", "Bscalar"]:
        sems[s] = es.enter_context(nc.semaphore(s))

    sb = lambda *a: es.enter_context(nc.sbuf_tensor(*a))
    cid = sb("c_id", [128, 128], BF16)
    cidf = sb("c_idf", [128, 128], F32)
    crot = sb("c_rot", [128, 128], BF16)
    cmk = sb("c_mk", [128, 128], BF16)
    cmq = sb("c_mq", [128, 128], BF16)
    cb2 = sb("c_b2", [2, 128], F32)
    ccos = sb("c_cos", [128, TOK], BF16)
    csin = sb("c_sin", [128, TOK], BF16)
    cq1b = sb("c_q1b", [64, 16], F32)
    cq2b = sb("c_q2b", [128, 8], F32)
    ck1b = sb("c_k1b", [64, 4], F32)
    ck2b = sb("c_k2b", [128, 2], F32)
    cfb = sb("c_fb", [128, 32], F32)
    ctmp = sb("c_tmp", [128, 1], F32)
    ctmp8 = sb("c_tmp8", [128, 1], F32)
    cnbias = sb("c_nbias", [128, 1], F32)
    chmask = sb("c_hmask", [128, 1], F32)
    cbl1 = sb("c_bl1", [128, 2], F32)
    cones1 = sb("c_ones1", [1, 64], F32)
    cmask01 = sb("c_mask01", [128, 2048], BF16)
    cbsel = sb("c_bsel", [128, 2], F32)
    ceps = sb("c_eps", [128, 1], F32)
    qc1T = sb("qc1T", [128, 8 * 3 * 64], BF16)
    kc1T = sb("kc1T", [128, 2 * 3 * 64], BF16)

    for t, src in [(cid, ident_bf), (cidf, ident_f32), (crot, rot_bf),
                   (cmk, meank_l), (cmq, meanq_l), (cb2, bcast2),
                   (ccos, cosT), (csin, sinT), (cq1b, qc1_bT), (cq2b, qc2_bT),
                   (ck1b, kc1_bT), (ck2b, kc2_bT), (cfb, ffn_biasT),
                   (chmask, halo_mask), (cmask01, mask01), (ctmp, key_temp_b),
                   (cbsel, bsel)]:
        P.dma("sync", t[:, :], src[:, :], sem="D0")
    LDC = P.n("D0")
    P.emit("scalar", lambda e: e.activation(ctmp8[:, :], ctmp[:, :], AF.Copy, bias=0.0, scale=0.125),
           waits=[("D0", LDC)], inc="SE")
    sA = P.emit("scalar", lambda e: e.activation(cnbias[:, :], ctmp[:, :], AF.Abs, bias=0.0, scale=0.125),
                waits=[("D0", LDC)], inc="SE")
    P.emit("vector", lambda e: e.tensor_scalar_mul(cnbias[:, :], cnbias[:, :], -1.0),
           waits=[("SE", sA)], inc="VE")
    P.emit("vector", lambda e: e.memset(cbl1[:, :], 0.0), inc="VE")
    P.emit("vector", lambda e: e.memset(cbl1[0:64, 0:1], 1.0), inc="VE")
    P.emit("vector", lambda e: e.memset(cbl1[64:128, 1:2], 1.0), inc="VE")
    P.emit("vector", lambda e: e.memset(cones1[:, :], 1.0), inc="VE")
    VEPS = P.emit("vector", lambda e: e.memset(ceps[:, :], 1e-5), inc="VE")

    with ExitStack() as p0:
        wbf = p0.enter_context(nc.sbuf_tensor("p0_wbf", [128, 17408], BF16))
        big = p0.enter_context(nc.sbuf_tensor("p0_big", [128, 16384], BF16))
        stg = [p0.enter_context(nc.sbuf_tensor(f"p0_stg{i}", [128, 4096], BF16))
               for i in range(2)]
        stg_ld = [0, 0]
        qwi = p0.enter_context(nc.sbuf_tensor("p0_qwi", [128, 2 * 512], I32))
        scz = p0.enter_context(nc.sbuf_tensor("p0_scz", [128, 256], F32))
        wde = p0.enter_context(nc.sbuf_tensor("p0_wde", [128, 2 * 512], BF16))
        pst = [p0.enter_context(nc.psum_tensor(f"p0_ps{i}", [128, 512], BF16))
               for i in range(2)]
        pst_ve = [0, 0]
        psn = [0]

        def tgroup(src_aps, ready_waits):
            si = psn[0] % 2
            psn[0] += 1
            ps = pst[si]
            tv = None
            for i, ap in enumerate(src_aps):
                w = list(ready_waits) + [("VE", pst_ve[si]), ("D0", LDC)]
                tv = P.emit("tensor",
                            (lambda e, ap=ap, ps=ps, i=i:
                             e.transpose(ps[:, i * 128:(i + 1) * 128], ap, cid[:, :])),
                            waits=w, inc="TE")
            return ps, si, tv

        def evac(dst_ap, width, ps, si, tv, extra=()):
            sv = P.emit("vector",
                        (lambda e, ps=ps, dst_ap=dst_ap, width=width:
                         e.tensor_copy(dst_ap, ps[:, 0:width])),
                        waits=[("TE", tv)] + list(extra), inc="VE")
            pst_ve[si] = sv
            return sv

        def ppdst(off, n):
            return pp_in.ap()[0, off:off + n]

        W_QKV, W_O, W_Q2, W_K2, W_Q1, W_K1 = 0, 6144, 10240, 13312, 14848, 16384
        P.dma("gpsimd", wbf[:, W_QKV:W_QKV + 6144].rearrange("p (t c) -> p t c", c=512),
              w_qkv.ap().rearrange("(t p) c -> p t c", p=128), sem="XD")
        P.dma("gpsimd", wbf[:, W_O:W_O + 4096].rearrange("p (t c) -> p t c", c=128),
              w_o.ap().rearrange("(t p) c -> p t c", p=128), sem="XD")
        P.dma("gpsimd", wbf[:, W_Q2:W_Q2 + 3072].rearrange("p (t c) -> p t c", c=384),
              qc2_w.ap().rearrange("(t p) i k -> p t (i k)", p=128), sem="XD")
        P.dma("gpsimd", wbf[:, W_K2:W_K2 + 1536].rearrange("p (t c) -> p t c", c=768),
              kc2_w.ap().rearrange("(t p) i k -> p t (i k)", p=128), sem="XD")
        P.dma("gpsimd", wbf[:, W_Q1:W_Q1 + 1536].rearrange("p (t c) -> p t c", c=192),
              qc1_w.ap().rearrange("(t p) i k -> p t (i k)", p=128), sem="XD")
        P.dma("gpsimd", wbf[:, W_K1:W_K1 + 384].rearrange("p (t c) -> p t c", c=192),
              kc1_w.ap().rearrange("(t p) i k -> p t (i k)", p=128), sem="XD")
        GD_L = P.n("XD")
        P.dma("sync", scz[:, 0:128], scalesT[:, :, :])
        l2 = P.dma("sync", scz[:, 128:256], zerosT[:, :, :])

        for j in range(4):
            sgi = j % 2
            for g0 in range(0, 12, 4):
                aps = [wbf[:, W_QKV + (g0 + t) * 512 + j * 128:
                           W_QKV + (g0 + t) * 512 + (j + 1) * 128] for t in range(4)]
                ps, si, tv = tgroup(aps, [("XD", GD_L)])
                evac(stg[sgi][:, g0 * 128:(g0 + 4) * 128], 512, ps, si, tv,
                     extra=[("SA" if sgi == 0 else "SB", stg_ld[sgi])])
            sv = P.n("VE")
            ld = P.dma("sync",
                       ppdst(OQ + j * 128 * 1536, 128 * 1536)
                       .rearrange("(p c) -> p c", c=1536),
                       stg[sgi][:, 0:1536],
                       waits=[("VE", sv)], sem=("SA" if sgi == 0 else "SB"))
            stg_ld[sgi] = ld

        for g0 in range(0, 32, 4):
            sgi = 0
            aps = [wbf[:, W_O + (g0 + t) * 128: W_O + (g0 + t + 1) * 128]
                   for t in range(4)]
            ps, si, tv = tgroup(aps, [("XD", GD_L)])
            evac(stg[0][:, g0 * 128:(g0 + 4) * 128], 512, ps, si, tv,
                 extra=[("SA", stg_ld[0])])
        sv = P.n("VE")
        ld = P.dma("sync", ppdst(OW, 524288).rearrange("(p c) -> p c", c=4096),
                   stg[0][:, 0:4096], waits=[("VE", sv)], sem="SA")
        stg_ld[0] = ld

        for tau in range(3):
            rps = None
            for ot in range(8):
                rps = P.emit(
                    "vector",
                    (lambda e, tau=tau, ot=ot:
                     e.tensor_copy(big[:, ot * 128:(ot + 1) * 128],
                                   wbf[:, W_Q2 + ot * 384 + tau:
                                       W_Q2 + (ot + 1) * 384:3])),
                    waits=[("XD", GD_L), ("TE", P.n("TE"))], inc="VE")
            sgi = 1
            for g0 in range(0, 8, 4):
                aps = [big[:, (g0 + t) * 128:(g0 + t + 1) * 128] for t in range(4)]
                ps, si, tv = tgroup(aps, [("VE", rps)])
                evac(stg[1][:, g0 * 128:(g0 + 4) * 128], 512, ps, si, tv,
                     extra=[("SB", stg_ld[1])])
            sv = P.n("VE")
            ld = P.dma("sync",
                       ppdst(OC2 + tau * 131072, 131072)
                       .rearrange("(p c) -> p c", c=1024),
                       stg[1][:, 0:1024], waits=[("VE", sv)], sem="SB")
            stg_ld[1] = ld

        for r in range(2):
            rpv = {}
            for tau in range(3):
                rpv[tau] = P.emit(
                    "vector",
                    (lambda e, r=r, tau=tau:
                     e.tensor_copy(big[:, 1024 + tau * 256: 1024 + (tau + 1) * 256],
                                   wbf[:, W_K2 + r * 768 + tau:
                                       W_K2 + (r + 1) * 768:3])),
                    waits=[("XD", GD_L), ("TE", P.n("TE"))], inc="VE")
            for tau in range(3):
                base = 1024 + tau * 256
                aps = [big[:, base + i * 128: base + (i + 1) * 128] for i in range(2)]
                ps, si, tv = tgroup(aps, [("VE", rpv[tau])])
                sgi = tau % 2
                sv = evac(stg[sgi][:, 0:256], 256, ps, si, tv,
                          extra=[("SA" if sgi == 0 else "SB", stg_ld[sgi])])
                dst = kc2T_d.ap()[tau, :, r * 128:(r + 1) * 128]
                dst = dst.rearrange("(b p) c -> p b c", p=128)
                ld = P.dma("sync", dst,
                           stg[sgi][:, 0:256].rearrange("p (b c) -> p b c", c=128),
                           waits=[("VE", sv)], sem=("SA" if sgi == 0 else "SB"))
                stg_ld[sgi] = ld

        def conv1_prep(nr, srcbase, dstT):
            for r in range(nr):
                vv = None
                for tau in range(3):
                    vv = P.emit(
                        "vector",
                        (lambda e, r=r, tau=tau, srcbase=srcbase:
                         e.tensor_copy(big[:, 2048 + tau * 64: 2048 + (tau + 1) * 64],
                                       wbf[:, srcbase + r * 192 + tau:
                                           srcbase + (r + 1) * 192:3])),
                        waits=[("XD", GD_L), ("TE", P.n("TE")), ("ST", P.n("ST"))],
                        inc="VE")
                for tau in range(3):
                    si = psn[0] % 2
                    psn[0] += 1
                    ps = pst[si]
                    tv = P.emit("tensor",
                                (lambda e, tau=tau, ps=ps:
                                 e.transpose(ps[0:64, 0:128],
                                             big[:, 2048 + tau * 64: 2048 + (tau + 1) * 64],
                                             cid[:, :])),
                                waits=[("VE", vv), ("VE", pst_ve[si]), ("D0", LDC)],
                                inc="TE")
                    sv = P.emit("vector",
                                (lambda e, ps=ps, r=r, tau=tau, dstT=dstT:
                                 e.tensor_copy(dstT[0:64, (r * 3 + tau) * 64:(r * 3 + tau + 1) * 64],
                                               ps[0:64, 0:64])),
                                waits=[("TE", tv)], inc="VE")
                    sv2 = P.emit("vector",
                                 (lambda e, ps=ps, tau=tau:
                                  e.tensor_copy(big[0:64, 2240 + tau * 64: 2240 + (tau + 1) * 64],
                                                ps[0:64, 64:128])),
                                 waits=[("TE", tv)], inc="VE")
                    pst_ve[si] = sv2
                    P.dma("gpsimd",
                          dstT[64:128, (r * 3 + tau) * 64:(r * 3 + tau + 1) * 64],
                          big[0:64, 2240 + tau * 64: 2240 + (tau + 1) * 64],
                          waits=[("VE", sv2)], sem="ST")
        conv1_prep(8, W_Q1, qc1T)
        conv1_prep(2, W_K1, kc1T)

        qw_ld = [0, 0]
        dq_rd = [0, 0]
        for ot in range(32):
            qsl = ot % 2
            gq = P.dma("gpsimd", qwi[:, qsl * 512:(qsl + 1) * 512],
                       qweight[ot * 128:(ot + 1) * 128, :, :],
                       waits=[("VE", qw_ld[qsl])],
                       sem=("D2" if qsl == 0 else "D3"))
            te_wde = P.n("TE")
            dq = None
            for g in range(4):
                dq = P.emit("vector",
                            (lambda e, g=g, qsl=qsl, ot=ot:
                             e.tensor_scalar(wde[:, qsl * 512 + g * 128:
                                                 qsl * 512 + (g + 1) * 128],
                                             qwi[:, qsl * 512 + g * 128:
                                                 qsl * 512 + (g + 1) * 128],
                                             scz[:, ot * 4 + g: ot * 4 + g + 1],
                                             scz[:, 128 + ot * 4 + g: 129 + ot * 4 + g],
                                             ALU.mult, ALU.add)),
                            waits=[("D2" if qsl == 0 else "D3", gq), ("LD", l2),
                                   ("TE", dq_rd[qsl])],
                            inc="VE")
            qw_ld[qsl] = dq
            aps = [wde[:, qsl * 512 + i * 128: qsl * 512 + (i + 1) * 128]
                   for i in range(4)]
            ps, si, tv = tgroup(aps, [("VE", dq)])
            dq_rd[qsl] = P.n("TE")
            P.emit("vector",
                   (lambda e, ps=ps, ot=ot:
                    e.tensor_copy(big.ap().rearrange("p (b c) -> p b c", c=4096)
                                  [:, 0:4, ot * 128:(ot + 1) * 128],
                                  ps[:, 0:512].rearrange("p (b c) -> p b c", c=128))),
                   waits=[("TE", tv), ("SA", stg_ld[0]), ("SB", stg_ld[1])], inc="VE")
            pst_ve[si] = P.n("VE")
        WTF_VE = P.n("VE")
        if "bigw" in DEBUG_TAPS:
            mktap("bigw", [128, 16384], BF16)
            P.dma("gpsimd", taps["bigw"][:, :], big[:, :],
                  waits=[("VE", WTF_VE)], sem="ST")
        if "wdew" in DEBUG_TAPS:
            mktap("wdew", [128, 1024], BF16)
            P.dma("gpsimd", taps["wdew"][:, :], wde[:, :],
                  waits=[("VE", WTF_VE)], sem="ST")
        for j in range(4):
            P.dma("sync", ppdst(OF + j * 524288, 524288)
                  .rearrange("(p c) -> p c", c=4096),
                  big[:, j * 4096:(j + 1) * 4096],
                  waits=[("VE", WTF_VE)], sem=("SA" if j % 2 == 0 else "SB"))

    P.barrier()
    ccp = P.emit("gpsimd",
                 (lambda e: e.collective_compute(
                     "AllGather", ALU.bypass, replica_groups=groups,
                     ins=[pp_in.ap().opt()], outs=[pp_out.ap().opt()])),
                 waits=[("SA", P.n("SA")), ("SB", P.n("SB")), ("GD", P.n("GD"))],
                 inc="CC")
    CC_P0 = P.n("CC")
    if "ppw" in DEBUG_TAPS:
        mktap("ppw", [8, PSL], BF16)
        P.dma("gpsimd", taps["ppw"][:, :], pp_out[:, :],
              waits=[("CC", CC_P0)], sem="ST")
    if "ppi" in DEBUG_TAPS:
        mktap("ppi", [1, PSL], BF16)
        P.dma("gpsimd", taps["ppi"][:, :], pp_in[:, :],
              waits=[("CC", CC_P0)], sem="ST")
    LD_P0 = P.n("LD")
    GD_P0 = P.n("GD")


    _CTX.update(nc=nc, P=P, sems=sems, es=es, taps=taps)
    with ExitStack() as pa:
        qkvT = pa.enter_context(nc.sbuf_tensor("qkvT", [128, 12 * CL], BF16))

        HL = 1032
        TC3 = [(0, 512), (512, 1024), (1024, HL)]
        with ExitStack() as pa1:
            xnT = pa1.enter_context(nc.sbuf_tensor("xnT", [128, 32 * HL], BF16))
            xbuf = pa1.enter_context(nc.sbuf_tensor("a_x", [128, 2 * D], F32))
            xnt = pa1.enter_context(nc.sbuf_tensor("a_xn", [128, 2 * D], BF16))
            lnst = pa1.enter_context(nc.sbuf_tensor("a_ln", [128, 2 * 64], F32))
            wq = pa1.enter_context(nc.sbuf_tensor("a_wq", [128, 2 * 4096], BF16))
            pstr = [pa1.enter_context(nc.psum_tensor(f"a_tr{i}", [128, 512], BF16))
                    for i in range(2)]
            pstr_ve = [0, 0]
            psq = [pa1.enter_context(nc.psum_tensor(f"a_q{i}", [128, 512], F32))
                   for i in range(6)]

            evq = {}
            wqld = [0, 0]
            seapp = {}
            trdone = {}
            spill_ld = 0
            ve_tile = {}
            for h in range(2):
                for kk9 in range(9):
                    k = h * 9 + kk9
                    rows = 128 if kk9 < 8 else 8
                    r0 = h * 1024 + kk9 * 128
                    slot = k % 2
                    w = [("SE", seapp.get(k - 2, 0))]
                    l1 = P.dma("sync", xbuf[0:rows, slot * D:slot * D + D],
                               x_loc[r0:r0 + rows, :], waits=w,
                               sem=("D0" if slot == 0 else "D1"))
                    vs = None
                    for i in range(8):
                        vs = P.emit("vector",
                                    (lambda e, rows=rows, slot=slot, i=i:
                                     e.bn_stats(lnst[0:rows, slot * 64 + i * 6: slot * 64 + (i + 1) * 6],
                                                xbuf[0:rows, slot * D + i * 512: slot * D + (i + 1) * 512])),
                                    waits=[("D0" if slot == 0 else "D1", l1)], inc="VE")
                    va = P.emit("vector",
                                (lambda e, rows=rows, slot=slot:
                                 e.bn_aggr(lnst[0:rows, slot * 64 + 48: slot * 64 + 50],
                                           lnst[0:rows, slot * 64: slot * 64 + 48])),
                                waits=[("VE", vs)], inc="VE")
                    sq = P.emit("scalar",
                                (lambda e, rows=rows, slot=slot:
                                 e.activation(lnst[0:rows, slot * 64 + 50: slot * 64 + 51],
                                              lnst[0:rows, slot * 64 + 49: slot * 64 + 50],
                                              AF.Sqrt, bias=ceps[0:rows, 0:1], scale=1.0)),
                                waits=[("VE", va), ("VE", VEPS)], inc="SE")
                    vr = P.emit("vector",
                                (lambda e, rows=rows, slot=slot:
                                 e.reciprocal(lnst[0:rows, slot * 64 + 51: slot * 64 + 52],
                                              lnst[0:rows, slot * 64 + 50: slot * 64 + 51])),
                                waits=[("SE", sq)], inc="VE")
                    vn = P.emit("vector",
                                (lambda e, rows=rows, slot=slot:
                                 e.scalar_tensor_tensor(lnst[0:rows, slot * 64 + 52: slot * 64 + 53],
                                                        lnst[0:rows, slot * 64 + 48: slot * 64 + 49],
                                                        -1.0,
                                                        lnst[0:rows, slot * 64 + 51: slot * 64 + 52],
                                                        ALU.mult, ALU.mult)),
                                waits=[("VE", vr)], inc="VE")
                    sa = P.emit("scalar",
                                (lambda e, rows=rows, slot=slot:
                                 e.activation(xnt[0:rows, slot * D: slot * D + D],
                                              xbuf[0:rows, slot * D: slot * D + D],
                                              AF.Identity,
                                              bias=lnst[0:rows, slot * 64 + 52: slot * 64 + 53],
                                              scale=lnst[0:rows, slot * 64 + 51: slot * 64 + 52])),
                                waits=[("VE", vn), ("TE", trdone.get(k - 2, 0))], inc="SE")
                    seapp[k] = sa
                    for g0 in range(0, 32, 4):
                        si = g0 // 4 % 2
                        ps = pstr[si]
                        tv = None
                        for i in range(4):
                            j = g0 + i
                            if rows == 128:
                                tv = P.emit("tensor",
                                            (lambda e, j=j, i=i, ps=ps, slot=slot:
                                             e.transpose(ps[:, i * 128:(i + 1) * 128],
                                                         xnt[:, slot * D + j * 128: slot * D + (j + 1) * 128],
                                                         cid[:, :])),
                                            waits=[("SE", sa), ("VE", pstr_ve[si])], inc="TE")
                            else:
                                tv = P.emit("tensor",
                                            (lambda e, j=j, i=i, ps=ps, slot=slot:
                                             e.transpose(ps[:, i * 8:(i + 1) * 8],
                                                         xnt[0:8, slot * D + j * 128: slot * D + (j + 1) * 128],
                                                         cid[0:8, 0:8])),
                                            waits=[("SE", sa), ("VE", pstr_ve[si])], inc="TE")
                        wdt = 128 if rows == 128 else 8
                        sv = P.emit("vector",
                                    (lambda e, ps=ps, g0=g0, wdt=wdt, kk9=kk9:
                                     e.tensor_copy(
                                         xnT.ap().rearrange("p (j t) -> p j t", t=HL)
                                         [:, g0:g0 + 4, kk9 * 128: kk9 * 128 + wdt],
                                         ps[:, 0:4 * wdt].rearrange("p (b c) -> p b c", c=wdt))),
                                    waits=[("TE", tv), ("LD", spill_ld)], inc="VE")
                        pstr_ve[si] = sv
                    trdone[k] = P.n("TE")
                    ve_tile[kk9] = P.n("VE")
                VE_LN = P.n("VE")

                spill_ld = P.dma(
                    "sync",
                    xnT_d.ap().rearrange("(j p) t -> p j t", p=128)[:, :, h * 1024: h * 1024 + HL],
                    xnT[:, :].rearrange("p (j t) -> p j t", t=HL),
                    waits=[("VE", VE_LN)])

                for r in range(12):
                    slot = r % 2
                    lw = None
                    for rr8 in range(8):
                        lw = P.dma("sync",
                                   wq[:, slot * 4096 + rr8 * 512:
                                      slot * 4096 + (rr8 + 1) * 512]
                                   .rearrange("p (b c) -> p b c", c=128),
                                   pp_out.ap()[rr8, OQ:OQ + 786432]
                                   .rearrange("(b p c) -> p b c", p=128, c=1536)
                                   [:, :, r * 128:(r + 1) * 128],
                                   waits=([("CC", CC_P0), ("TE", wqld[slot])]
                                          if rr8 == 0 else []),
                                   sem=("D2" if slot == 0 else "D3"))
                    for i in range(32):
                        for tci, (c0, c1) in enumerate(TC3):
                            P.emit(
                                "tensor",
                                (lambda e, i=i, tci=tci, c0=c0, c1=c1, slot=slot:
                                 e.matmul(psq[tci][:, 0:c1 - c0],
                                          wq[:, slot * 4096 + i * 128: slot * 4096 + (i + 1) * 128],
                                          xnT[:, i * HL + c0: i * HL + c1],
                                          start=(i == 0), stop=(i == 31))),
                                waits=([("D2" if slot == 0 else "D3", lw),
                                        ("VE", ve_tile[(3, 7, 8)[tci]]),
                                        ("SE", evq.get(tci, 0))]
                                       if i == 0 else []),
                                inc="TE")
                    wqld[slot] = P.n("TE")
                    for tci, (c0, c1) in enumerate(TC3):
                        evq[tci] = P.emit(
                            "scalar",
                            (lambda e, r=r, h=h, tci=tci, c0=c0, c1=c1:
                             e.activation(qkvT[:, r * CL + h * 1024 + c0: r * CL + h * 1024 + c1],
                                          psq[tci][:, 0:c1 - c0], AF.Copy,
                                          bias=0.0, scale=1.0)),
                            waits=[("TE", P.n("TE"))], inc="SE")
        SE_QKV = P.n("SE")
        if "qkvT" in DEBUG_TAPS:
            mktap("qkvT", [1536, CL], BF16)
            P.dma("gpsimd", taps["qkvT"].ap().rearrange("(j p) t -> p j t", p=128),
                  qkvT[:, :].rearrange("p (j t) -> p j t", t=CL),
                  waits=[("SE", SE_QKV)], sem="ST")
        if "xnT" in DEBUG_TAPS:
            mktap("xnT", [D, CL], BF16)
            P.dma("gpsimd", taps["xnT"].ap().rearrange("(j p) t -> p j t", p=128),
                  xnT_d.ap().rearrange("(j p) t -> p j t", p=128),
                  waits=[("SE", SE_QKV)], sem="ST")

        if PHASES < 2:
            raise _StopBuild()
        P.barrier()
        with ExitStack() as pa2:
            qmix = pa2.enter_context(nc.sbuf_tensor("qmix", [128, 8 * TOK], BF16))
            kmix = pa2.enter_context(nc.sbuf_tensor("kmix", [128, 2 * TOK], BF16))
            w2s = pa2.enter_context(nc.sbuf_tensor("w2s", [128, 2 * 24 * 128], BF16))
            oddst = pa2.enter_context(nc.sbuf_tensor("oddst", [64, 512], BF16))
            pa2a = pa2.enter_context(ExitStack())
            c1q = pa2a.enter_context(nc.sbuf_tensor("c1q", [128, 8 * C1L], BF16))
            c1k = pa2a.enter_context(nc.sbuf_tensor("c1k", [128, 2 * C1L], BF16))
            ps1 = [pa2.enter_context(nc.psum_tensor(f"a2_p{i}", [128, 512], F32))
                   for i in range(6)]
            ps1_ev = [("SE", 0)] * 6

            def conv1(nh2, srcT, src_t0, dstT, wT, biasT):
                for r in range(nh2):
                    for tci, (c0, c1) in enumerate(TC5C):
                        wdt = c1 - c0
                        for hh in range(2):
                            pi = (r * 2 + hh) % 4
                            ps = ps1[pi]
                            base = 64 * hh
                            tv = None
                            for tau in range(3):
                                tv = P.emit(
                                    "tensor",
                                    (lambda e, r=r, hh=hh, tau=tau, c0=c0, wdt=wdt,
                                     ps=ps, base=base, srcT=srcT, src_t0=src_t0, wT=wT:
                                     e.matmul(ps[0:64, 0:wdt],
                                              wT[base:base + 64,
                                                 (r * 3 + tau) * 64:(r * 3 + tau + 1) * 64],
                                              srcT[base:base + 64,
                                                   src_t0 + r * CL + 4 + c0 + tau:
                                                   src_t0 + r * CL + 4 + c0 + tau + wdt],
                                              start=(tau == 0), stop=(tau == 2))),
                                    waits=([("SE", SE_QKV), ps1_ev[pi]]
                                           if tau == 0 else []),
                                    inc="TE")
                            h = 2 * r + hh
                            if hh == 0:
                                ev = P.emit(
                                    "scalar",
                                    (lambda e, r=r, c0=c0, wdt=wdt, ps=ps, h=h,
                                     dstT=dstT, biasT=biasT:
                                     e.activation(dstT[0:64, r * C1L + c0: r * C1L + c0 + wdt],
                                                  ps[0:64, 0:wdt], AF.Identity,
                                                  bias=biasT[0:64, h:h + 1], scale=1.0)),
                                    waits=[("TE", tv)], inc="SE")
                                ps1_ev[pi] = ("SE", ev)
                            else:
                                ev = P.emit(
                                    "scalar",
                                    (lambda e, c0=c0, wdt=wdt, ps=ps, h=h, biasT=biasT:
                                     e.activation(oddst[0:64, 0:wdt],
                                                  ps[0:64, 0:wdt], AF.Identity,
                                                  bias=biasT[0:64, h:h + 1], scale=1.0)),
                                    waits=[("TE", tv), ("GD", P.n("GD"))], inc="SE")
                                ps1_ev[pi] = ("SE", ev)
                                P.dma("gpsimd",
                                      dstT[64:128, r * C1L + c0: r * C1L + c0 + wdt],
                                      oddst[0:64, 0:wdt], waits=[("SE", ev)])
            conv1(8, qkvT, 0, c1q, qc1T, cq1b)
            conv1(2, qkvT, 8 * CL, c1k, kc1T, ck1b)
            GD_C1 = P.n("GD")
            hm = []
            for t, nt in [(c1q, 8), (c1k, 2)]:
                for r in range(nt):
                    hm.append(P.emit(
                        "vector",
                        (lambda e, t=t, r=r:
                         e.tensor_scalar_mul(t[:, r * C1L: r * C1L + 2],
                                             t[:, r * C1L: r * C1L + 2],
                                             chmask[:, 0:1])),
                        waits=[("SE", P.n("SE")), ("GD", GD_C1)], inc="VE"))
            VE_HM = P.n("VE")

            w2ld = [0, 0]

            def conv2(n_out, n_in, srcC, dstT, w2d, biasT, mean_l, mean_src_t0, mean_n):
                for o in range(n_out):
                    slot = o % 2
                    if w2d is None:
                        lw = None
                        for tau3 in range(3):
                            lw = P.dma(
                                "sync",
                                w2s[:, slot * 3072 + tau3 * n_in * 128:
                                    slot * 3072 + (tau3 + 1) * n_in * 128]
                                .rearrange("p (j c) -> p j c", c=128),
                                pp_out.ap()[:, OC2 + tau3 * 131072:
                                            OC2 + (tau3 + 1) * 131072]
                                .rearrange("r (p c) -> p r c", p=128, c=1024)
                                [:, :, o * 128:(o + 1) * 128],
                                waits=([("CC", CC_P0), ("TE", w2ld[slot])]
                                       if tau3 == 0 else []),
                                sem=("D0" if slot == 0 else "D1"))
                    else:
                        lw = P.dma("sync",
                                   w2s[:, slot * 3072: slot * 3072 + 3 * n_in * 128]
                                   .rearrange("p (b c) -> p b c", c=128),
                                   w2d.ap()[:, :, o * 128:(o + 1) * 128]
                                   .rearrange("t (j p) c -> p (t j) c", p=128),
                                   waits=[("CC", CC_P0), ("TE", w2ld[slot])],
                                   sem=("D0" if slot == 0 else "D1"))
                    for tci, (c0, c1) in enumerate(TC4):
                        wdt = c1 - c0
                        pi = 4 + (o * 4 + tci) % 2
                        ps = ps1[pi]
                        tv = None
                        first = True
                        for tau in range(3):
                            for j in range(n_in):
                                tv = P.emit(
                                    "tensor",
                                    (lambda e, tau=tau, j=j, o=o, c0=c0, wdt=wdt, ps=ps,
                                     slot=slot, srcC=srcC, n_in=n_in:
                                     e.matmul(ps[:, 0:wdt],
                                              w2s[:, slot * 3072 + (tau * n_in + j) * 128:
                                                  slot * 3072 + (tau * n_in + j + 1) * 128],
                                              srcC[:, j * C1L + c0 + tau: j * C1L + c0 + tau + wdt],
                                              start=(tau == 0 and j == 0), stop=False)),
                                    waits=([("D0" if slot == 0 else "D1", lw),
                                            ("VE", VE_HM), ps1_ev[pi]]
                                           if first else []),
                                    inc="TE")
                                first = False
                        for mi in range(mean_n):
                            tv = P.emit(
                                "tensor",
                                (lambda e, mi=mi, c0=c0, wdt=wdt, ps=ps,
                                 mean_l=mean_l, mean_src_t0=mean_src_t0:
                                 e.matmul(ps[:, 0:wdt],
                                          mean_l[:, :],
                                          qkvT[:, mean_src_t0 + mi * CL + 8 + c0:
                                               mean_src_t0 + mi * CL + 8 + c0 + wdt],
                                          start=False, stop=(mi == mean_n - 1))),
                                inc="TE")
                        ev = P.emit(
                            "scalar",
                            (lambda e, o=o, c0=c0, wdt=wdt, ps=ps, dstT=dstT, biasT=biasT:
                             e.activation(dstT[:, o * TOK + c0: o * TOK + c0 + wdt],
                                          ps[:, 0:wdt], AF.Identity,
                                          bias=biasT[:, o:o + 1], scale=1.0)),
                            waits=[("TE", tv)], inc="SE")
                        ps1_ev[pi] = ("SE", ev)
                    w2ld[slot] = P.n("TE")
            conv2(8, 8, c1q, qmix, None, cq2b, cmk, 8 * CL, 2)
            conv2(2, 2, c1k, kmix, kc2T_d, ck2b, cmq, 0, 8)
            SE_C2 = P.n("SE")
            pa2a.close()
            P.barrier()
            qhatT = pa2.enter_context(nc.sbuf_tensor("qhatT", [128, 8 * TOK], BF16))
            khatT = pa2.enter_context(nc.sbuf_tensor("khatT", [128, 2 * TOK], BF16))
            vstage = pa2.enter_context(nc.sbuf_tensor("vstage", [128, 4 * 16 * 65], BF16))
            sqs = pa2.enter_context(nc.sbuf_tensor("sqs", [128, 1024], F32))
            rsd = pa2.enter_context(nc.sbuf_tensor("rsd", [2, 2 * TOK], F32))
            rbb = pa2.enter_context(nc.sbuf_tensor("rbb", [128, 2 * 512], BF16))
            rtb = pa2.enter_context(nc.sbuf_tensor("rtb", [128, 2 * 512], BF16))
            tm1 = pa2.enter_context(nc.sbuf_tensor("tm1", [128, 2 * 512], BF16))
            if "qmix" in DEBUG_TAPS:
                mktap("qmix", [LAT, TOK], BF16)
                P.dma("gpsimd", taps["qmix"].ap().rearrange("(j p) t -> p j t", p=128),
                      qmix[:, :].rearrange("p (j t) -> p j t", t=TOK),
                      waits=[("SE", SE_C2)], sem="ST")

            def heads(n_t, srcM, dstT, is_k):
                for r in range(n_t):
                    rr = r % 2
                    for tci, (c0, c1) in enumerate(TC4):
                        sl2 = tci % 2
                        sqv = P.emit("vector",
                                     (lambda e, r=r, c0=c0, c1=c1, sl2=sl2, srcM=srcM:
                                      e.tensor_mul(sqs[:, sl2 * 512:(sl2 + 1) * 512],
                                                   srcM[:, r * TOK + c0: r * TOK + c1],
                                                   srcM[:, r * TOK + c0: r * TOK + c1])),
                                     waits=[("SE", SE_C2), ("TE", P.n("TE"))], inc="VE")
                        pi = tci % 2
                        ps = ps1[pi]
                        tv = P.emit("tensor",
                                    (lambda e, c0=c0, c1=c1, ps=ps, sl2=sl2:
                                     e.matmul(ps[0:2, 0:c1 - c0], cbl1[:, :],
                                              sqs[:, sl2 * 512:(sl2 + 1) * 512],
                                              start=True, stop=True)),
                                    waits=[("VE", sqv), ps1_ev[pi]], inc="TE")
                        cl_ = P.emit("vector",
                                     (lambda e, c0=c0, c1=c1, ps=ps:
                                      e.tensor_scalar_max(ps[0:2, 0:c1 - c0],
                                                          ps[0:2, 0:c1 - c0], 1e-24)),
                                     waits=[("TE", tv)], inc="VE")
                        sn = P.emit("scalar",
                                    (lambda e, c0=c0, c1=c1, ps=ps, rr=rr:
                                     e.activation(rsd[0:2, rr * TOK + c0: rr * TOK + c1],
                                                  ps[0:2, 0:c1 - c0],
                                                  AF.Ln, bias=0.0, scale=1.0)),
                                    waits=[("VE", cl_)], inc="SE")
                        ps1_ev[pi] = ("SE", sn)
                        P.emit("scalar",
                               (lambda e, c0=c0, c1=c1, rr=rr:
                                e.activation(rsd[0:2, rr * TOK + c0: rr * TOK + c1],
                                             rsd[0:2, rr * TOK + c0: rr * TOK + c1],
                                             AF.Exp, bias=0.0, scale=-0.5)),
                               inc="SE")
                    snv = P.n("SE")
                    vrc = ("SE", snv)
                    if is_k:
                        vm2 = P.emit("vector",
                                     (lambda e, rr=rr:
                                      e.tensor_scalar_mul(rsd[0:2, rr * TOK:(rr + 1) * TOK],
                                                          rsd[0:2, rr * TOK:(rr + 1) * TOK],
                                                          ctmp8[0:2, 0:1])),
                                     waits=[("SE", snv)], inc="VE")
                        vrc = ("VE", vm2)
                    for tci, (c0, c1) in enumerate(TC4):
                        wdt = c1 - c0
                        pib = 2 + tci % 2
                        psb = ps1[pib]
                        tb = P.emit("tensor",
                                    (lambda e, rr=rr, c0=c0, c1=c1, psb=psb:
                                     e.matmul(psb[:, 0:c1 - c0], cb2[:, :],
                                              rsd[0:2, rr * TOK + c0: rr * TOK + c1],
                                              start=True, stop=True)),
                                    waits=[vrc, ps1_ev[pib]], inc="TE")
                        sl = tci % 2
                        vb = P.emit("vector",
                                    (lambda e, psb=psb, sl=sl, wdt=wdt:
                                     e.tensor_copy(rbb[:, sl * 512: sl * 512 + wdt],
                                                   psb[:, 0:wdt])),
                                    waits=[("TE", tb)], inc="VE")
                        ps1_ev[pib] = ("VE", vb)
                        vq = P.emit("vector",
                                    (lambda e, r=r, c0=c0, wdt=wdt, sl=sl, srcM=srcM:
                                     e.tensor_mul(tm1[:, sl * 512: sl * 512 + wdt],
                                                  srcM[:, r * TOK + c0: r * TOK + c0 + wdt],
                                                  rbb[:, sl * 512: sl * 512 + wdt])),
                                    waits=[("VE", vb)], inc="VE")
                        pir = 4 + tci % 2
                        psr = ps1[pir]
                        tr = P.emit("tensor",
                                    (lambda e, sl=sl, wdt=wdt, psr=psr:
                                     e.matmul(psr[:, 0:wdt], crot[:, :],
                                              tm1[:, sl * 512: sl * 512 + wdt],
                                              start=True, stop=True)),
                                    waits=[("VE", vq), ps1_ev[pir]], inc="TE")
                        vrt = P.emit("vector",
                                     (lambda e, sl=sl, wdt=wdt, psr=psr:
                                      e.tensor_copy(rtb[:, sl * 512: sl * 512 + wdt],
                                                    psr[:, 0:wdt])),
                                     waits=[("TE", tr)], inc="VE")
                        ps1_ev[pir] = ("VE", vrt)
                        v1 = P.emit("vector",
                                    (lambda e, sl=sl, wdt=wdt, c0=c0:
                                     e.tensor_mul(tm1[:, sl * 512: sl * 512 + wdt],
                                                  tm1[:, sl * 512: sl * 512 + wdt],
                                                  ccos[:, c0:c0 + wdt])),
                                    waits=[("VE", vrt)], inc="VE")
                        v2 = P.emit("vector",
                                    (lambda e, sl=sl, wdt=wdt, c0=c0:
                                     e.tensor_mul(rtb[:, sl * 512: sl * 512 + wdt],
                                                  rtb[:, sl * 512: sl * 512 + wdt],
                                                  csin[:, c0:c0 + wdt])),
                                    inc="VE")
                        v3 = P.emit("vector",
                                    (lambda e, r=r, sl=sl, wdt=wdt, c0=c0, dstT=dstT:
                                     e.tensor_add(dstT[:, r * TOK + c0: r * TOK + c0 + wdt],
                                                  tm1[:, sl * 512: sl * 512 + wdt],
                                                  rtb[:, sl * 512: sl * 512 + wdt])),
                                    inc="VE")
            heads(8, qmix, qhatT, False)
            heads(2, kmix, khatT, True)
            VE_HEAD = P.n("VE")

            psv = [pa2.enter_context(nc.psum_tensor(f"a2_pv{i}", [128, 128], BF16))
                   for i in range(2)]
            psv_ve = [0, 0]
            for tt in range(16):
                for vt in range(2):
                    si = (tt * 2 + vt) % 2
                    ps = psv[si]
                    shift = 0 if vt == 0 else -1
                    tv = P.emit("tensor",
                                (lambda e, tt=tt, vt=vt, ps=ps, shift=shift:
                                 e.transpose(ps[:, :],
                                             qkvT[:, (10 + vt) * CL + 8 + tt * 128 + shift:
                                                  (10 + vt) * CL + 8 + (tt + 1) * 128 + shift],
                                             cid[:, :])),
                                waits=[("SE", SE_QKV), ("VE", psv_ve[si])], inc="TE")
                    for kvh in range(2):
                        k4 = vt * 2 + kvh
                        vv = P.emit("vector",
                                    (lambda e, tt=tt, ps=ps, kvh=kvh, k4=k4:
                                     e.tensor_copy(
                                         vstage[:, k4 * 1040 + tt * 65: k4 * 1040 + tt * 65 + 64],
                                         ps[:, kvh * 64:(kvh + 1) * 64])),
                                    waits=[("TE", tv)], inc="VE")
                    psv_ve[si] = P.n("VE")
            vo = None
            for k4 in range(4):
                for tt in range(16):
                    vo = P.emit("vector",
                                (lambda e, k4=k4, tt=tt:
                                 e.memset(vstage[:, k4 * 1040 + tt * 65 + 64:
                                                 k4 * 1040 + tt * 65 + 65], 1.0)),
                                inc="VE")
            VE_V = P.n("VE")

            for j8 in range(8):
                k4 = j8 % 4
                for t2 in range(2):
                    P.dma("gpsimd",
                          a1_in.ap()[j8, OFF_Q + t2 * 128 * TOK: OFF_Q + (t2 + 1) * 128 * TOK]
                          .rearrange("(p t) -> p t", p=128),
                          qhatT[:, (2 * k4 + t2) * TOK:(2 * k4 + t2 + 1) * TOK],
                          waits=[("VE", VE_HEAD)], sem="GD")
                P.dma("gpsimd",
                      a1_in.ap()[j8, OFF_K: OFF_K + 64 * TOK].rearrange("(p t) -> p t", p=64),
                      khatT[64 * (k4 % 2): 64 * (k4 % 2) + 64,
                            (k4 // 2) * TOK:(k4 // 2 + 1) * TOK],
                      waits=[("VE", VE_HEAD)], sem="GD")
                P.dma("gpsimd",
                      a1_in.ap()[j8, OFF_V: OFF_V + TOK * 65]
                      .rearrange("(tt p c) -> p tt c", p=128, c=65),
                      vstage[:, k4 * 1040:(k4 + 1) * 1040]
                      .rearrange("p (tt c) -> p tt c", c=65),
                      waits=[("VE", VE_V)], sem="GD")
            GD_A1 = P.n("GD")

    if PHASES < 3:
        raise _StopBuild()
    cc1 = P.emit("gpsimd",
                 (lambda e: e.collective_compute(
                     "AllToAll", ALU.bypass, replica_groups=groups,
                     ins=[a1_in.ap().opt()], outs=[a1_out.ap().opt()])),
                 waits=[("GD", P.n("GD"))], inc="CC")
    P.barrier()

    with ExitStack() as pb:
        qB = pb.enter_context(nc.sbuf_tensor("qB", [128, 2 * S], BF16))
        Ksb = pb.enter_context(nc.sbuf_tensor("Ksb", [128, S], BF16))
        Vsb = pb.enter_context(nc.sbuf_tensor("Vsb", [128, 64 * 65], BF16))
        probs = pb.enter_context(nc.sbuf_tensor("probs", [128, 4 * 1024], BF16))
        attnL = pb.enter_context(nc.sbuf_tensor("attnL", [128, 2 * S], BF16))
        rcs = pb.enter_context(nc.sbuf_tensor("rcs", [1, 512], F32))
        bcs = pb.enter_context(nc.sbuf_tensor("bcs", [64, 512], F32))
        cones1b = pb.enter_context(nc.sbuf_tensor("cones1b", [128, 64], BF16))
        pss = [pb.enter_context(nc.psum_tensor(f"b_s{i}", [128, 1024], F32))
               for i in range(2)]
        psa = [[pb.enter_context(nc.psum_tensor(f"b_a{i}{h}", [128, 512], F32))
                for h in range(2)] for i in range(2)]

        pbsh = ExitStack()
        qB2 = pbsh.enter_context(nc.sbuf_tensor("qB2", [128, 2 * S], BF16))
        Ksb2 = pbsh.enter_context(nc.sbuf_tensor("Ksb2", [128, S], BF16))
        Vsb2 = pbsh.enter_context(nc.sbuf_tensor("Vsb2", [128, 64 * 65], BF16))
        def _ld_k(sem):
            for half, d in [(0, Ksb), (1, Ksb2)]:
                for i in range(4):
                    blk = half * 4 + i
                    for hh in range(2):
                        P.dma("sync",
                              d[64 * hh:64 * hh + 64, i * TOK:(i + 1) * TOK],
                              a1_out.ap()[blk, OFF_K: OFF_K + 64 * TOK]
                              .rearrange("(p t) -> p t", p=64),
                              waits=[("CC", cc1)], sem=sem)

        def _ld_q(t2, sem):
            for half, d in [(0, qB), (1, qB2)]:
                for i in range(4):
                    blk = half * 4 + i
                    for hh in range(2):
                        P.dma("sync",
                              d[64 * hh:64 * hh + 64,
                                t2 * S + i * TOK: t2 * S + (i + 1) * TOK],
                              a1_out.ap()[blk, OFF_Q + (t2 * 2 + hh) * 64 * TOK:
                                          OFF_Q + (t2 * 2 + hh + 1) * 64 * TOK]
                              .rearrange("(p t) -> p t", p=64),
                              waits=[("CC", cc1)], sem=sem)

        def _ld_v(sem):
            for half, d in [(0, Vsb), (1, Vsb2)]:
                for i in range(4):
                    blk = half * 4 + i
                    P.dma("sync",
                          d[:, i * 16 * 65:(i + 1) * 16 * 65]
                          .rearrange("p (tt c) -> p tt c", c=65),
                          a1_out.ap()[blk, OFF_V: OFF_V + TOK * 65]
                          .rearrange("(tt p c) -> p tt c", p=128, c=65),
                          waits=[("CC", cc1)], sem=sem)

        _ld_k("D2")
        KLD = P.n("D2")
        _ld_q(0, "D3")
        Q0LD = P.n("D3")
        _ld_v("SA")
        VLD = P.n("SA")
        _ld_q(1, "SB")
        Q1LD = P.n("SB")

        def _sel(d, d2, c0, w, waits):
            P.emit("vector",
                   (lambda e: e.tensor_scalar_mul(d[:, c0:c0 + w], d[:, c0:c0 + w],
                                                  cbsel[:, 0:1])),
                   waits=waits, inc="VE")
            return P.emit("vector",
                          (lambda e: e.scalar_tensor_tensor(d[:, c0:c0 + w],
                                                            d2[:, c0:c0 + w],
                                                            cbsel[:, 1:2],
                                                            d[:, c0:c0 + w],
                                                            ALU.mult, ALU.add)),
                          inc="VE")

        _sel(Ksb, Ksb2, 0, S, [("D2", KLD)])
        selv_q0 = _sel(qB, qB2, 0, S, [("D3", Q0LD)])
        _sel(Vsb, Vsb2, 0, 64 * 65, [("SA", VLD)])
        selv_q1 = _sel(qB, qB2, S, S, [("SB", Q1LD)])
        selv_pair = [selv_q0, selv_q1]

        pbsh.close()
        avstage = pb.enter_context(nc.sbuf_tensor("avstage", [128, 64 * 512], BF16))
        dtile = pb.enter_context(nc.sbuf_tensor("dtile", [128, 256], F32))
        P.emit("vector", lambda e: e.memset(cones1b[64:65, :], 1.0), inc="VE")
        CONB = P.n("VE")

        exp_done = [0, 0]
        probs_rd = [[0, 0], [0, 0]]
        av_done = [0, 0]
        for qc in range(16):
            n_kt = 4 * qc + 4
            pready = [None, None]
            for kt in range(n_kt + 1):
                pready_new = [None, None]
                if kt < n_kt:
                    diag = kt >= n_kt - 4
                    buf = kt % 2
                    for pair in range(2):
                        ps = pss[pair]
                        pidx = pair * 2 + buf
                        tv = None
                        for hh in range(2):
                            tv = P.emit(
                                "tensor",
                                (lambda e, pair=pair, qc=qc, kt=kt, hh=hh, ps=ps:
                                 e.matmul(ps[:, hh * 512:(hh + 1) * 512],
                                          Ksb[64 * hh:64 * hh + 64, kt * 128:(kt + 1) * 128],
                                          qB[64 * hh:64 * hh + 64,
                                             pair * S + qc * 512:pair * S + (qc + 1) * 512],
                                          start=True, stop=True)),
                                waits=(([("VE", selv_pair[pair])]
                                        if qc == 0 and kt == 0 else [])
                                       + [("SE", exp_done[pair])]
                                       if hh == 0 else []),
                                inc="TE")
                        ex = P.emit(
                            "scalar",
                            (lambda e, ps=ps, pidx=pidx:
                             e.activation(probs[:, pidx * 1024:(pidx + 1) * 1024],
                                          ps[:, :], AF.Exp,
                                          bias=cnbias[:, 0:1], scale=1.0)),
                            waits=[("TE", tv), ("TE", probs_rd[pair][buf])], inc="SE")
                        exp_done[pair] = ex
                        if diag:
                            v = kt - (n_kt - 4)
                            mv = 0
                            for hh in range(2):
                                mv = P.emit(
                                    "vector",
                                    (lambda e, pidx=pidx, hh=hh, v=v:
                                     e.tensor_mul(probs[:, pidx * 1024 + hh * 512:
                                                        pidx * 1024 + (hh + 1) * 512],
                                                  probs[:, pidx * 1024 + hh * 512:
                                                        pidx * 1024 + (hh + 1) * 512],
                                                  cmask01[:, v * 512:(v + 1) * 512])),
                                    waits=[("SE", ex)], inc="VE")
                            pready_new[pair] = ("VE", mv)
                        else:
                            pready_new[pair] = ("SE", ex)
                if kt >= 1:
                    akt = kt - 1
                    abuf = akt % 2
                    for pair in range(2):
                        pidx = pair * 2 + abuf
                        pr = pready[pair]
                        for hh in range(2):
                            P.emit(
                                "tensor",
                                (lambda e, akt=akt, hh=hh, pair=pair, pidx=pidx, n_kt=n_kt:
                                 e.matmul(psa[pair][hh][0:65, :],
                                          Vsb[:, akt * 65:(akt + 1) * 65],
                                          probs[:, pidx * 1024 + hh * 512:
                                                pidx * 1024 + (hh + 1) * 512],
                                          start=(akt == 0), stop=(akt == n_kt - 1))),
                                waits=([pr, ("VE", av_done[pair])]
                                       if akt == 0 else [pr]),
                                inc="TE")
                        probs_rd[pair][abuf] = P.n("TE")
                if kt < n_kt:
                    pready = pready_new
            TE_AV = P.n("TE")
            for pair in range(2):
                for hh in range(2):
                    t4 = qc * 4 + pair * 2 + hh
                    av_done[pair] = P.emit(
                        "vector",
                        (lambda e, pair=pair, hh=hh, t4=t4:
                         e.tensor_copy(avstage[0:65, t4 * 512:(t4 + 1) * 512],
                                       psa[pair][hh][0:65, :])),
                        waits=[("TE", TE_AV)], inc="VE")
        VE_EV = P.n("VE")
        P.barrier()

        dge = P.emit(
            "gpsimd",
            (lambda e: e.dma_start(out=dn_d[:], in_=avstage[64:65, :])),
            waits=[("VE", VE_EV)], inc="GD", by=16)
        dge2 = P.emit(
            "gpsimd",
            (lambda e: e.dma_start(out=dtile[:, :],
                                   in_=dn_d.ap().rearrange("(p c) -> p c", c=256))),
            waits=[("GD", P.n("GD"))], inc="GD", by=16)
        vrc = P.emit("vector",
                     (lambda e: e.reciprocal(dtile[:, :], dtile[:, :])),
                     waits=[("GD", P.n("GD"))], inc="VE")
        dge3 = P.emit(
            "gpsimd",
            (lambda e: e.dma_start(out=dn_d.ap().rearrange("(p c) -> p c", c=256),
                                   in_=dtile[:, :])),
            waits=[("VE", vrc)], inc="GD", by=16)
        dge4 = P.emit(
            "gpsimd",
            (lambda e: e.dma_start(out=avstage[64:65, :], in_=dn_d[:])),
            waits=[("GD", P.n("GD"))], inc="GD", by=16)
        GD_R = P.n("GD")
        bc_ve = [0, 0]
        for t4 in range(64):
            qc, rem = divmod(t4, 4)
            pair, hh = divmod(rem, 2)
            tb = P.emit("tensor",
                        (lambda e, pair=pair, t4=t4:
                         e.matmul(pss[pair][0:64, 0:512], cones1b[64:65, :],
                                  avstage[64:65, t4 * 512:(t4 + 1) * 512],
                                  start=True, stop=True)),
                        waits=[("GD", GD_R), ("SE", exp_done[pair]),
                               ("VE", max(bc_ve[pair], CONB))], inc="TE")
            vb = P.emit("vector",
                        (lambda e, pair=pair:
                         e.tensor_copy(bcs[:, :], pss[pair][0:64, 0:512])),
                        waits=[("TE", tb)], inc="VE")
            P.emit("vector",
                   (lambda e, pair=pair, qc=qc, hh=hh, t4=t4:
                    e.tensor_mul(attnL[64 * hh:64 * hh + 64,
                                       pair * S + qc * 512:pair * S + (qc + 1) * 512],
                                 avstage[0:64, t4 * 512:(t4 + 1) * 512],
                                 bcs[:, :])),
                   inc="VE")
            bc_ve[pair] = P.n("VE")
        P.barrier()
        VE_B = P.n("VE")
        if "attnL" in DEBUG_TAPS:
            mktap("attnL", [256, S], BF16)
            P.dma("gpsimd", taps["attnL"].ap().rearrange("(j p) t -> p j t", p=128),
                  attnL[:, :].rearrange("p (j t) -> p j t", t=S),
                  waits=[("VE", VE_B)], sem="ST")

        for j8 in range(8):
            k4 = j8 % 4
            for t2 in range(2):
                for hh in range(2):
                    P.dma("gpsimd",
                          a2_in.ap()[j8, (t2 * 2 + hh) * 64 * TOK:
                                     (t2 * 2 + hh + 1) * 64 * TOK]
                          .rearrange("(p t) -> p t", p=64),
                          attnL[64 * hh:64 * hh + 64,
                                t2 * S + k4 * TOK: t2 * S + (k4 + 1) * TOK],
                          waits=[("VE", VE_B)], sem="GD")

    GD_A2C = P.n("GD")
    P.barrier()
    cc2 = P.emit("gpsimd",
                 (lambda e: e.collective_compute(
                     "AllToAll", ALU.bypass, replica_groups=groups,
                     ins=[a2_in.ap().opt()], outs=[a2_out.ap().opt()])),
                 waits=[("GD", GD_A2C)], inc="CC")
    if "xnTd" in DEBUG_TAPS:
        mktap("xnTd", [D, CL], BF16)
        P.dma("gpsimd", taps["xnTd"][:, :], xnT_d[:, :],
              waits=[("CC", P.n("CC"))], sem="ST")
    if "a2o" in DEBUG_TAPS:
        mktap("a2o", [8, SH2_W], BF16)
        P.dma("gpsimd", taps["a2o"][:, :], a2_out[:, :],
              waits=[("CC", P.n("CC"))], sem="ST")

    if PHASES < 4:
        raise _StopBuild()
    with ExitStack() as pc:
        xnC = pc.enter_context(nc.sbuf_tensor("xnC", [128, 32 * 1024], BF16))
        afH = pc.enter_context(nc.sbuf_tensor("afH", [128, 8 * 1024], BF16))
        afH2 = pc.enter_context(nc.sbuf_tensor("afH2", [128, 8 * 1024], BF16))
        wts = pc.enter_context(nc.sbuf_tensor("wts", [128, 2 * 32 * 128], BF16))
        wos = pc.enter_context(nc.sbuf_tensor("wos", [128, 2 * 8 * 128], BF16))
        gsb = pc.enter_context(nc.sbuf_tensor("gsb", [128, 1024], F32))
        gstage = pc.enter_context(nc.sbuf_tensor("gstage", [128, 8 * 1024], BF16))
        sgs = pc.enter_context(nc.sbuf_tensor("sgs", [128, 4 * 1024], F32))
        xcs = pc.enter_context(nc.sbuf_tensor("xcs", [128, 2 * 512], F32))
        osb = pc.enter_context(nc.sbuf_tensor("osb", [128, 2 * 512], F32))
        psg = [pc.enter_context(nc.psum_tensor(f"c_g{i}", [128, 512], F32)) for i in range(2)]
        pso = [pc.enter_context(nc.psum_tensor(f"c_s{i}", [128, 512], F32)) for i in range(2)]
        pstp = [pc.enter_context(nc.psum_tensor(f"c_t{i}", [128, 512], F32)) for i in range(2)]

        psg_ev = [0, 0]
        pso_ev = [0, 0]
        gsb_ev = [0, 0]
        pst_ev = [0, 0]
        xcs_ev = [0, 0]
        ost_st = [0, 0]
        wldt = [0, 0]
        wldo = [0, 0]
        GPRE = 8
        te_prevH = 0
        for H in range(2):
            lh = P.dma("sync", xnC[:, :].rearrange("p (j t) -> p j t", t=1024),
                       xnT_d.ap().rearrange("(j p) t -> p j t", p=128)[:, :, 8 + H * 1024:
                                                                      8 + (H + 1) * 1024],
                       waits=[("TE", te_prevH)])
            for ot in range(GPRE):
                slot = ot % 2
                dsl = "D0" if slot == 0 else "D1"
                lw = None
                for rr8 in range(8):
                    lw = P.dma("sync",
                               wts[:, slot * 4096 + rr8 * 512:
                                   slot * 4096 + (rr8 + 1) * 512]
                               .rearrange("p (b c) -> p b c", c=128),
                               pp_out.ap()[rr8, OF:OF + 2097152]
                               .rearrange("(b p c) -> p b c", p=128, c=4096)
                               [:, :, ot * 128:(ot + 1) * 128],
                               waits=([("TE", wldt[slot])] if rr8 == 0 else []),
                               sem=dsl)
                for tc in range(2):
                    ps = psg[tc]
                    tv = None
                    for j in range(32):
                        tv = P.emit(
                            "tensor",
                            (lambda e, j=j, tc=tc, ps=ps, slot=slot:
                             e.matmul(ps[:, :],
                                      wts[:, slot * 4096 + j * 128: slot * 4096 + (j + 1) * 128],
                                      xnC[:, j * 1024 + tc * 512: j * 1024 + (tc + 1) * 512],
                                      start=(j == 0), stop=(j == 31))),
                            waits=([(dsl, lw), ("LD", lh), ("SE", psg_ev[tc])]
                                   if j == 0 else []),
                            inc="TE")
                    sg_ = P.emit(
                        "scalar",
                        (lambda e, ot=ot, tc=tc, ps=ps:
                         e.activation(gsb[:, tc * 512:(tc + 1) * 512], ps[:, :],
                                      AF.Silu, bias=cfb[:, ot:ot + 1], scale=1.0)),
                        waits=[("TE", tv), ("VE", gsb_ev[tc])], inc="SE")
                    psg_ev[tc] = sg_
                    gsb_ev[tc] = P.emit(
                        "vector",
                        (lambda e, ot=ot, tc=tc:
                         e.tensor_copy(gstage[:, ot * 1024 + tc * 512:
                                              ot * 1024 + (tc + 1) * 512],
                                       gsb[:, tc * 512:(tc + 1) * 512])),
                        waits=[("SE", sg_)], inc="VE")
                wldt[slot] = P.n("TE")
            for half, d in [(0, afH), (1, afH2)]:
                for i in range(4):
                    for ci in range(2):
                        P.dma("gpsimd",
                              d[:, (2 * i + ci) * 1024:(2 * i + ci + 1) * 1024],
                              a2_out.ap().rearrange("r (c p t) -> r c p t", p=128, t=TOK)
                              [half * 4 + i, ci, :, H * 1024:(H + 1) * 1024],
                              waits=[("CC", cc2), ("TE", te_prevH)])
            LD_C0 = P.n("GD")
            P.emit("vector",
                   (lambda e: e.tensor_scalar_mul(afH[:, :], afH[:, :], cbsel[:, 0:1])),
                   waits=[("GD", LD_C0), ("TE", te_prevH)], inc="VE")
            afsel = P.emit("vector",
                           (lambda e: e.scalar_tensor_tensor(afH[:, :], afH2[:, :],
                                                             cbsel[:, 1:2], afH[:, :],
                                                             ALU.mult, ALU.add)),
                           inc="VE")
            afl = LD_C0
            for ot in range(32):
                slot = ot % 2
                dsl = "D0" if slot == 0 else "D1"
                wsl = "W0" if slot == 0 else "W1"
                if ot >= GPRE:
                    lw = None
                    for rr8 in range(8):
                        lw = P.dma("sync",
                                   wts[:, slot * 4096 + rr8 * 512:
                                       slot * 4096 + (rr8 + 1) * 512]
                                   .rearrange("p (b c) -> p b c", c=128),
                                   pp_out.ap()[rr8, OF:OF + 2097152]
                                   .rearrange("(b p c) -> p b c", p=128, c=4096)
                                   [:, :, ot * 128:(ot + 1) * 128],
                                   waits=([("TE", wldt[slot])] if rr8 == 0 else []),
                                   sem=dsl)
                lwo = P.dma("sync",
                            wos[:, slot * 1024:(slot + 1) * 1024]
                            .rearrange("p (j c) -> p j c", c=128),
                            pp_out.ap()[:, OW:OW + 524288]
                            .rearrange("r (p c) -> p r c", p=128, c=4096)
                            [:, :, ot * 128:(ot + 1) * 128],
                            waits=[("TE", wldo[slot])], sem=wsl)
                for tc in range(2):
                    if ot >= GPRE:
                        ps = psg[tc]
                        tv = None
                        for j in range(32):
                            tv = P.emit(
                                "tensor",
                                (lambda e, j=j, tc=tc, ps=ps, slot=slot:
                                 e.matmul(ps[:, :],
                                          wts[:, slot * 4096 + j * 128: slot * 4096 + (j + 1) * 128],
                                          xnC[:, j * 1024 + tc * 512: j * 1024 + (tc + 1) * 512],
                                          start=(j == 0), stop=(j == 31))),
                                waits=([(dsl, lw), ("LD", lh), ("SE", psg_ev[tc])]
                                       if j == 0 else []),
                                inc="TE")
                        sg_ = P.emit(
                            "scalar",
                            (lambda e, ot=ot, tc=tc, ps=ps:
                             e.activation(gsb[:, tc * 512:(tc + 1) * 512], ps[:, :],
                                          AF.Silu, bias=cfb[:, ot:ot + 1], scale=1.0)),
                            waits=[("TE", tv), ("VE", gsb_ev[tc])], inc="SE")
                        psg_ev[tc] = sg_
                    pso_ = pso[tc]
                    tvo = None
                    for j in range(8):
                        tvo = P.emit(
                            "tensor",
                            (lambda e, j=j, tc=tc, pso_=pso_, slot=slot:
                             e.matmul(pso_[:, :],
                                      wos[:, slot * 1024 + j * 128: slot * 1024 + (j + 1) * 128],
                                      afH[:, j * 1024 + tc * 512: j * 1024 + (tc + 1) * 512],
                                      start=(j == 0), stop=(j == 7))),
                            waits=([(wsl, lwo), ("GD", afl), ("VE", afsel), ("VE", pso_ev[tc])]
                                   if j == 0 else []),
                            inc="TE")
                    if ot >= GPRE:
                        vsg = P.emit(
                            "vector",
                            (lambda e, ot=ot, tc=tc, pso_=pso_:
                             e.tensor_mul(sgs[:, (ot % 4) * 1024 + tc * 512:
                                              (ot % 4) * 1024 + (tc + 1) * 512],
                                          pso_[:, :], gsb[:, tc * 512:(tc + 1) * 512])),
                            waits=[("TE", tvo), ("SE", sg_)], inc="VE")
                        gsb_ev[tc] = vsg
                    else:
                        vsg = P.emit(
                            "vector",
                            (lambda e, ot=ot, tc=tc, pso_=pso_:
                             e.tensor_mul(sgs[:, (ot % 4) * 1024 + tc * 512:
                                              (ot % 4) * 1024 + (tc + 1) * 512],
                                          pso_[:, :],
                                          gstage[:, ot * 1024 + tc * 512:
                                                 ot * 1024 + (tc + 1) * 512])),
                            waits=[("TE", tvo)], inc="VE")
                    pso_ev[tc] = vsg
                if ot >= GPRE:
                    wldt[slot] = P.n("TE")
                wldo[slot] = P.n("TE")
                if ot % 4 == 3:
                    oc = ot // 4
                    VSG = P.n("VE")
                    for tc in range(2):
                        for tb in range(4):
                            sl = (tc * 4 + tb) % 2
                            pt = pstp[sl]
                            xsem = "D2" if sl == 0 else "D3"
                            ssem = "SA" if sl == 0 else "SB"
                            lx = P.dma("scalar", xcs[:, sl * 512:(sl + 1) * 512],
                                       x_loc[HALO + H * 1024 + tc * 512 + tb * 128:
                                             HALO + H * 1024 + tc * 512 + (tb + 1) * 128,
                                             oc * 512:(oc + 1) * 512],
                                       waits=[("VE", xcs_ev[sl])], sem=xsem)
                            tt = None
                            for i4 in range(4):
                                tt = P.emit(
                                    "tensor",
                                    (lambda e, i4=i4, tc=tc, tb=tb, pt=pt:
                                     e.transpose(pt[:, i4 * 128:(i4 + 1) * 128],
                                                 sgs[:, i4 * 1024 + tc * 512 + tb * 128:
                                                     i4 * 1024 + tc * 512 + (tb + 1) * 128],
                                                 cidf[:, :])),
                                    waits=[("VE", VSG), ("VE", pst_ev[sl])], inc="TE")
                            vv = P.emit(
                                "vector",
                                (lambda e, sl=sl, pt=pt:
                                 e.tensor_add(osb[:, sl * 512:(sl + 1) * 512],
                                              pt[:, :], xcs[:, sl * 512:(sl + 1) * 512])),
                                waits=[("TE", tt), (xsem, lx), (ssem, ost_st[sl])], inc="VE")
                            st = P.dma("gpsimd",
                                       out_loc[H * 1024 + tc * 512 + tb * 128:
                                               H * 1024 + tc * 512 + (tb + 1) * 128,
                                               oc * 512:(oc + 1) * 512],
                                       osb[:, sl * 512:(sl + 1) * 512],
                                       waits=[("VE", vv)], sem=ssem)
                            pst_ev[sl] = vv
                            xcs_ev[sl] = vv
                            ost_st[sl] = st
            te_prevH = P.n("TE")
    P.emit("gpsimd", None, waits=[("ST", P.n("ST")), ("SA", P.n("SA")),
                                  ("SB", P.n("SB"))])
    P.barrier()

    with nc.Block() as block:
        with allow_ici_all_to_all():
            _replay(P, block, sems)
    es.close()
    return nc, list(taps.keys())


_BUILT = {}


def _consts():
    ident = np.eye(128)
    rot = np.zeros((128, 128), np.float32)
    for i in range(64):
        rot[2 * i + 1, 2 * i] = -1.0
        rot[2 * i, 2 * i + 1] = 1.0
    meank = np.zeros((128, 128), np.float32)
    meanq = np.zeros((128, 128), np.float32)
    for kk in range(128):
        for m64 in range(2):
            meank[kk, m64 * 64 + kk % 64] = 0.125
            meanq[kk, m64 * 64 + kk % 64] = 0.03125
    b2 = np.zeros((2, 128), np.float32)
    b2[0, 0:64] = 1.0
    b2[1, 64:128] = 1.0
    m01 = np.zeros((128, 2048), np.float32)
    kk = np.arange(128)[:, None]
    qq = np.arange(512)[None, :]
    for v in range(4):
        m01[:, v * 512:(v + 1) * 512] = (qq >= kk + 128 * v).astype(np.float32)
    return ident, rot, meank, meanq, b2, m01


def kernel(**inputs):
    x = np.ascontiguousarray(np.asarray(inputs["L0_attn_x"], np.float32))
    ident, rot, meank, meanq, b2, m01 = _consts()
    qc1_b = np.asarray(inputs["L0_attn_qc1_b"], np.float32)
    qc2_b = np.asarray(inputs["L0_attn_qc2_b"], np.float32)
    kc1_b = np.asarray(inputs["L0_attn_kc1_b"], np.float32)
    kc2_b = np.asarray(inputs["L0_attn_kc2_b"], np.float32)
    scales = np.asarray(inputs["L0_ffn_scales"], np.float32)[:, :, 0]
    zeros = np.asarray(inputs["L0_ffn_scaled_zeros"], np.float32)[:, :, 0]
    fbias = np.asarray(inputs["L0_ffn_bias"], np.float32)
    ktemp = float(np.asarray(inputs["L0_attn_key_temp"], np.float32).reshape(-1)[0])

    inv = 1.0 / 10000.0 ** (np.arange(0, HD, 2, dtype=np.float32) / HD)

    wqkv_full = np.asarray(inputs["L0_attn_w_qkv"], np.float32)
    wo_full = np.asarray(inputs["L0_attn_w_o"], np.float32)
    qc2_full = np.asarray(inputs["L0_attn_qc2_w"], np.float32)
    qw_full = np.asarray(inputs["L0_ffn_qweight"], np.int32)
    scT_full = scales.reshape(32, 128, 32).transpose(1, 0, 2)
    zrT_full = zeros.reshape(32, 128, 32).transpose(1, 0, 2)
    shared = dict(
        qc1_w=np.ascontiguousarray(np.asarray(inputs["L0_attn_qc1_w"], np.float32)),
        kc1_w=np.ascontiguousarray(np.asarray(inputs["L0_attn_kc1_w"], np.float32)),
        kc2_w=np.ascontiguousarray(np.asarray(inputs["L0_attn_kc2_w"], np.float32)),
        qc1_bT=np.ascontiguousarray(qc1_b.reshape(16, 64).T),
        qc2_bT=np.ascontiguousarray(qc2_b.reshape(8, 128).T),
        kc1_bT=np.ascontiguousarray(kc1_b.reshape(4, 64).T),
        kc2_bT=np.ascontiguousarray(kc2_b.reshape(2, 128).T),

        ffn_biasT=np.ascontiguousarray(fbias.reshape(32, 128).T),
        key_temp_b=np.full((128, 1), ktemp, np.float32),
        ident_bf=ident.astype(BFNP),
        ident_f32=ident.astype(np.float32),
        rot_bf=rot.astype(BFNP),
        meank_l=meank.astype(BFNP),
        meanq_l=meanq.astype(BFNP),
        bcast2=b2,
        mask01=m01.astype(BFNP),
    )

    in_maps = []
    for c in range(8):
        b = c // 4
        p0 = (c % 4) * TOK
        xl = np.zeros((CL, D), np.float32)
        lo = max(0, p0 - HALO)
        xl[HALO - (p0 - lo):] = x[b, lo:p0 + TOK]
        pos = (p0 + np.arange(TOK, dtype=np.float32))[None, :]
        ang = pos * inv[(np.arange(128) % 64) // 2][:, None]
        m = dict(shared)
        m["w_qkv"] = np.ascontiguousarray(wqkv_full[:, 512 * c:512 * (c + 1)])
        m["w_o"] = np.ascontiguousarray(wo_full[:, 128 * c:128 * (c + 1)])
        m["qc2_w"] = np.ascontiguousarray(qc2_full[:, 128 * c:128 * (c + 1), :])
        m["qweight"] = np.ascontiguousarray(qw_full[:, 4 * c:4 * (c + 1), :])
        m["scalesT"] = np.ascontiguousarray(scT_full[:, :, 4 * c:4 * (c + 1)])
        m["zerosT"] = np.ascontiguousarray(zrT_full[:, :, 4 * c:4 * (c + 1)])
        m["x_loc"] = xl
        m["halo_mask"] = np.full((128, 1), 0.0 if p0 == 0 else 1.0, np.float32)
        bs = np.zeros((128, 2), np.float32)
        bs[:, 0 if b == 0 else 1] = 1.0
        m["bsel"] = bs
        m["cosT"] = np.cos(ang).astype(BFNP)
        m["sinT"] = np.sin(ang).astype(BFNP)
        in_maps.append(m)

    key = "prog"
    if key not in _BUILT:
        _BUILT[key] = build_program()
    nc, tapnames = _BUILT[key]
    res = run_bass_kernel_spmd(nc, in_maps, core_ids=list(range(8)))
    out = np.empty((B, S, D), np.float32)
    for c in range(8):
        out[c // 4, (c % 4) * TOK:(c % 4 + 1) * TOK] = res.results[c]["out_loc"]
    if DEBUG_TAPS:
        kernel.tap_results = res.results
    return out


if __name__ == "__main__":
    pass
